# version 04
# baseline (speedup 1.0000x reference)
import contextlib
import numpy as np
import concourse.bass as bass
import concourse.mybir as mybir
from concourse.bass_utils import run_bass_kernel_spmd

F32 = mybir.dt.float32
BF16 = mybir.dt.bfloat16
AF = mybir.ActivationFunctionType
ALU = mybir.AluOpType
AXX = None

D = 1024
T = 4096
TT = 128
NT = T // TT
NCH = D // 128
NEXP1 = 8
NSLOT = 2048
PSL = 32
EPS = 1e-5
NCORES = 8
DBG_L1A = 0
DBG_PHASE = 99

AXX = mybir.AxisListType.X


class Sem:
    __slots__ = ("h", "n")

    def __init__(self, h):
        self.h = h
        self.n = 0

    def bump(self, k):
        self.n += k
        return self.h


class Builder:
    def __init__(self, nc):
        self.nc = nc
        self.blk = None
        self.es = None

    def declare_io(self):
        nc = self.nc
        din = lambda n, s: nc.dram_tensor(n, s, F32, kind="ExternalInput")
        self.x0 = din("x0", [T, D])
        self.out = nc.dram_tensor("out", [T, D], F32, kind="ExternalOutput")
        self.gw = din("gw", [D, NEXP1])
        self.gb = din("gb", [1, NEXP1])
        self.l1ew = din("l1ew", [NEXP1, D, D])
        self.l1eb = din("l1eb", [NEXP1, D])
        self.phiwT = din("phiwT", [D, NSLOT])
        self.phib = din("phib", [1, NSLOT])
        self.l0ew = din("l0ew", [8, D, D])
        self.l0eb = din("l0eb", [8, D])
        self.l0ng = din("l0ng", [1, D])
        self.l0og = din("l0og", [1, D])
        self.wq0 = din("wq0", [D, 512])
        self.wkv0 = din("wkv0", [D, 128])
        self.wout0 = din("wout0", [512, D])
        self.wq1 = din("wq1", [D, D])
        self.wk1 = din("wk1", [D, D])
        self.wv1 = din("wv1", [D, D])
        self.wo1 = din("wo1", [D, D])
        self.wgam = din("wgam", [D, 16])
        self.wf1 = din("wf1", [D, 64])
        self.wf2 = din("wf2", [64, D])
        self.wog1 = din("wog1", [D, 64])
        self.wog2 = din("wog2", [64, D])
        self.l1ng = din("l1ng", [1, D])
        self.cmask64 = din("cmask64", [128, 16 * 64])
        self.cmskl = nc.dram_tensor("cmskl", [128, 128], mybir.dt.uint8,
                                    kind="ExternalInput")
        self.cmsku = nc.dram_tensor("cmsku", [128, 128], mybir.dt.uint8,
                                    kind="ExternalInput")
        self.cmaskp = din("cmaskp", [128, PSL])
        self.ctri = din("ctri", [128, 128])
        self.cident = din("cident", [128, 128])
        self.ctriu = din("ctriu", [128, 128])
        ds = lambda n, s, dt=BF16: nc.dram_tensor(n, s, dt)
        self.e_tm = ds("e_tm", [NT, 128, NSLOT])
        self.e_sm = ds("e_sm", [NT, 128, NSLOT])
        self.xt_in = ds("xt_in", [NSLOT, D], F32)
        self.xt_rs = ds("xt_rs", [NSLOT // 8, D], F32)
        self.yt_in = ds("yt_in", [NSLOT // 8, D])
        self.yt_ag = ds("yt_ag", [NSLOT, D])

    def alloc_persistent(self, es):
        nc = self.nc
        A = lambda n, s, dt: es.enter_context(nc.sbuf_tensor(n, s, dt))
        self.XRES = A("XRES", [128, NT * D], BF16)
        self.STATS = A("STATS", [128, 768], F32)
        self.ones = A("ones", [1, 128], BF16)
        self.onesf = A("onesf", [1, 128], F32)
        self.gain1 = A("gain1", [128, D], BF16)
        self.mask64 = A("mask64", [128, 16 * 64], BF16)
        self.maskp = A("maskp", [128, PSL], BF16)
        self.tri = A("tri", [128, 128], BF16)
        self.ident = A("ident", [128, 128], BF16)
        self.ceps = A("ceps", [128, 4], F32)
        S = lambda n: Sem(es.enter_context(nc.semaphore(n)))
        self.s_li = S("s_li")
        self.s_lg = S("s_lg")
        self.s_wa = S("s_wa")
        self.s_wb = S("s_wb")
        self.s_tr = S("s_tr")
        self.s_sp = S("s_sp")
        self.s_mm = S("s_mm")
        self.s_v = S("s_v")
        self.s_a = S("s_a")
        self.s_cc = S("s_cc")
        self.s_st = S("s_st")
        self.s_g = S("s_g")
        self.s_ea = S("s_ea")
        self.s_eb = S("s_eb")
        self.s_sc = S("s_sc")

    def lg_dma(self, g, out, in_):
        g.wait_ge(self.s_lg.h, self.s_lg.n)
        g.dma_start(out=out, in_=in_).then_inc(self.s_lg.bump(16), 16)

    def barrier_all(self, eng, *, skip=()):
        for s in (self.s_li, self.s_lg, self.s_wa, self.s_wb, self.s_tr,
                  self.s_sp, self.s_mm, self.s_v, self.s_a, self.s_cc,
                  self.s_g, self.s_ea, self.s_eb, self.s_sc):
            if s not in skip:
                eng.wait_ge(s.h, s.n)

    def load_initial(self):
        b = self

        @b.blk.gpsimd
        def _(g):
            src = bass.AP(b.x0, 0, [[D, 128], [TT * D, NT], [1, D]])
            dst = bass.AP(b.XRES, 0, [[NT * D, 128], [D, NT], [1, D]])
            g.dma_start(out=dst, in_=src).then_inc(b.s_li.bump(16), 16)
            g.memset(b.ones[:, :], 1.0)
            g.memset(b.onesf[:, :], 1.0)
            bc = lambda t: bass.AP(t, 0, [[0, 128], [1, D]])
            g.dma_start(out=b.gain1[:, :], in_=bc(b.l1ng)).then_inc(b.s_li.bump(16), 16)
            g.dma_start(out=b.mask64[:, :], in_=b.cmask64.ap()).then_inc(b.s_li.bump(16), 16)
            g.dma_start(out=b.maskp[:, :], in_=b.cmaskp.ap()).then_inc(b.s_li.bump(16), 16)
            g.dma_start(out=b.tri[:, :], in_=b.ctri.ap()).then_inc(b.s_li.bump(16), 16)
            g.dma_start(out=b.ident[:, :], in_=b.cident.ap()).then_inc(b.s_li.bump(16), 16)
            g.memset(b.ceps[:, 0:1], EPS)
            g.memset(b.ceps[:, 1:2], 1e-12)
            g.memset(b.ceps[:, 2:3], -16.0).then_inc(b.s_g.bump(1), 1)

    def store_out(self):
        b = self

        @b.blk.gpsimd
        def _(g):
            b.barrier_all(g)
            src = bass.AP(b.XRES, 0, [[NT * D, 128], [D, NT], [1, D]])
            dst = bass.AP(b.out, 0, [[D, 128], [TT * D, NT], [1, D]])
            g.dma_start(out=dst, in_=src).then_inc(b.s_st.bump(16), 16)
            g.wait_ge(b.s_st.h, b.s_st.n)

    def xres_tile(self, t):
        return self.XRES[:, t * D:(t + 1) * D]

    def emit_ln_stats(self):
        b = self
        r_ = b.STATS[:, 0:NT]
        nmr = b.STATS[:, NT:2 * NT]
        mv = b.STATS[:, 64:64 + 2 * NT]
        bn = b.STATS[:, 128:128 + 12 * NT]
        var_v = bass.AP(b.STATS, 64 + 1, [[768, 128], [2, NT]])
        mu_v = bass.AP(b.STATS, 64, [[768, 128], [2, NT]])

        @b.blk.gpsimd
        def _(g):
            b.barrier_all(g)

        @b.blk.sync
        def _(sy):
            b.barrier_all(sy)

        @b.blk.scalar
        def _(s):
            b.barrier_all(s, skip=(b.s_a,))

        @b.blk.tensor
        def _(t):
            b.barrier_all(t, skip=(b.s_mm,))

        @b.blk.vector
        def _(v):
            b.barrier_all(v, skip=(b.s_v,))
            for t in range(NT):
                xt = b.xres_tile(t)
                v.bn_stats(bn[:, 12 * t:12 * t + 6], xt[:, 0:512])
                v.bn_stats(bn[:, 12 * t + 6:12 * t + 12], xt[:, 512:1024])
            v.drain()
            for t in range(NT):
                v.bn_aggr(mv[:, 2 * t:2 * t + 2], bn[:, 12 * t:12 * t + 12])
            v.drain().then_inc(b.s_v.bump(1), 1)

        @b.blk.scalar
        def _(s):
            s.wait_ge(b.s_v.h, b.s_v.n)
            s.activation(r_[:, :], var_v, AF.Sqrt, bias=b.ceps[:, 0:1])
            s.drain().then_inc(b.s_a.bump(1), 1)

        @b.blk.vector
        def _(v):
            v.wait_ge(b.s_a.h, b.s_a.n)
            v.reciprocal(r_[:, :], r_[:, :])
            v.drain()
            v.scalar_tensor_tensor(nmr[:, :], mu_v, -1.0, r_[:, :],
                                   ALU.mult, ALU.mult)
            v.drain().then_inc(b.s_v.bump(1), 1)
        return r_, nmr

    def emit_xln_fm(self, dst, r_, nmr, gain=None, tiles=None, also_tm=None):
        b = self
        if tiles is None:
            tiles = list(range(NT))
        ntl = len(tiles)
        if also_tm is None:
            tmp = [b.es.enter_context(b.nc.sbuf_tensor(f"xln_tmp{i}_{b.s_v.n}", [128, D], BF16))
                   for i in range(2)]
            get_tm = lambda i: tmp[i % 2][:, :]
        else:
            get_tm = lambda i: also_tm[:, i * D:(i + 1) * D]
        v_marks = []
        tr_needed = []
        v0 = b.s_v.n
        tr0 = b.s_tr.n

        @b.blk.vector
        def _(v):
            for i, t in enumerate(tiles):
                if also_tm is None and i >= 2:
                    v.wait_ge(b.s_tr.h, tr0 + 16 * NCH * (i - 1))
                v.tensor_scalar(get_tm(i), b.xres_tile(t),
                                r_[:, t:t + 1], nmr[:, t:t + 1],
                                ALU.mult, ALU.add)
                if gain is not None:
                    v.drain()
                    v.tensor_mul(get_tm(i), get_tm(i), gain[:, :])
                v.drain().then_inc(b.s_v.bump(1), 1)
                v_marks.append(b.s_v.n)

        @b.blk.sync
        def _(sy):
            for i, t in enumerate(tiles):
                sy.wait_ge(b.s_v.h, v_marks[i])
                for c in range(NCH):
                    sy.dma_start_transpose(
                        dst[:, c * (ntl * 128) + i * 128:
                            c * (ntl * 128) + (i + 1) * 128],
                        get_tm(i)[:, c * 128:(c + 1) * 128],
                    ).then_inc(b.s_tr.bump(16), 16)
                sy.wait_ge(b.s_tr.h, b.s_tr.n)
        return tr0

    def emit_l1ffn(self, tag="l1f_", final=False):
        b = self
        nc = self.nc
        with contextlib.ExitStack() as es:
            old = b.es
            b.es = es
            r_, nmr = b.emit_ln_stats()
            if DBG_PHASE < 2:
                b.es = old
                return
            xfm = es.enter_context(nc.sbuf_tensor(tag + "xfm", [128, NCH * T], BF16))
            tr0 = b.emit_xln_fm(xfm, r_, nmr)
            if DBG_PHASE < 3:
                b.es = old
                return
            fmtile = lambda c, ti: xfm[:, c * T + ti * 128: c * T + (ti + 1) * 128]

            wsb = es.enter_context(nc.sbuf_tensor(tag + "w", [128, NT * NEXP1], F32))
            wtb = es.enter_context(nc.sbuf_tensor(tag + "wtb", [128, NT * NEXP1], BF16))
            wtT = es.enter_context(nc.sbuf_tensor(tag + "wtT", [NEXP1, T], BF16))
            gwsb = es.enter_context(nc.sbuf_tensor(tag + "gw", [128, NCH * NEXP1], BF16))
            gbsb = es.enter_context(nc.sbuf_tensor(tag + "gb", [1, NEXP1], BF16))
            ebsb = es.enter_context(nc.sbuf_tensor(tag + "eb", [NEXP1, D], BF16))
            gtmp = es.enter_context(nc.sbuf_tensor(tag + "gtmp", [128, NT * NEXP1], F32))
            gstat = es.enter_context(nc.sbuf_tensor(tag + "gs", [128, 2 * NT], F32))
            wbuf = [es.enter_context(nc.sbuf_tensor(f"{tag}wb{i}", [128, NCH * D], BF16))
                    for i in range(2)]
            with (nc.psum_tensor(tag + "pg", [128, 256], F32) as pg,
                  nc.psum_tensor(tag + "pt", [128, 128], BF16) as ptr,
                  nc.psum_tensor(tag + "p0", [128, 512], F32) as p0,
                  nc.psum_tensor(tag + "p1", [128, 512], F32) as p1,
                  nc.psum_tensor(tag + "p2", [128, 512], F32) as p2,
                  nc.psum_tensor(tag + "p3", [128, 512], F32) as p3):
                self._l1ffn_body(es, fmtile, tr0, wsb, wtb, wtT, gwsb, gbsb,
                                 ebsb, gtmp, gstat, wbuf, pg, ptr,
                                 [p0, p1, p2, p3], final)
            b.es = old

    def _l1ffn_body(self, es, fmtile, tr0, wsb, wtb, wtT, gwsb, gbsb, ebsb,
                    gtmp, gstat, wbuf, pg, ptr, psum, final):
        b = self
        nc = self.nc

        @b.blk.gpsimd
        def _(g):
            src = bass.AP(b.gw, 0, [[NEXP1, 128], [128 * NEXP1, NCH], [1, NEXP1]])
            b.lg_dma(g, gwsb[:, :], src)
            b.lg_dma(g, gbsb[:, :], b.gb.ap())
            b.lg_dma(g, ebsb[:, :], b.l1eb.ap())
        ld_gates = b.s_lg.n

        wsem = [b.s_wa, b.s_wb]

        def load_expert(g, e, buf):
            src = bass.AP(b.l1ew, e * D * D,
                          [[D, 128], [128 * D, NCH], [1, D]])
            g.dma_start(out=wbuf[buf][:, :], in_=src).then_inc(wsem[buf].bump(16), 16)
            return wsem[buf].n

        ld_marks = {}

        @b.blk.gpsimd
        def _(g):
            ld_marks[0] = load_expert(g, 0, 0)

        if DBG_PHASE == 5.3:
            @b.blk.gpsimd
            def _(g):
                g.memset(wtb[:, :], 0.125)
                g.memset(wsb[:, :], 0.125).then_inc(b.s_v.bump(1), 1)
        v_gates = b.s_v.n
        if DBG_PHASE == 5.3:
            mm_tr_marks = []

            @b.blk.tensor
            def _(t):
                t.wait_ge(b.s_v.h, v_gates)
                for ti in range(NT):
                    if ti >= 1:
                        t.wait_ge(b.s_v.h, v_gates + ti)
                    t.transpose(ptr[0:NEXP1, :],
                                wtb[:, ti * NEXP1:(ti + 1) * NEXP1],
                                b.ident[:, :]).then_inc(b.s_mm.bump(1), 1)
                    mm_tr_marks.append(b.s_mm.n)

            @b.blk.vector
            def _(v):
                for ti in range(NT):
                    v.wait_ge(b.s_mm.h, mm_tr_marks[ti])
                    v.tensor_copy(wtT[:, ti * 128:(ti + 1) * 128],
                                  ptr[0:NEXP1, :]).then_inc(b.s_v.bump(1), 1)
            return
        @b.blk.tensor
        def _(t):
            t.wait_ge(b.s_lg.h, ld_gates)
            for ti in range(NT):
                t.wait_ge(b.s_tr.h, tr0 + 16 * NCH * (ti + 1))
                for c in range(NCH):
                    t.matmul(pg[:, ti * NEXP1:(ti + 1) * NEXP1],
                             fmtile(c, ti), gwsb[:, c * NEXP1:(c + 1) * NEXP1],
                             start=(c == 0), stop=False)
                mmi = t.matmul(pg[:, ti * NEXP1:(ti + 1) * NEXP1],
                               b.ones[0:1, 0:128], gbsb[0:1, :],
                               start=False, stop=True)
                if ti == NT - 1:
                    mmi.then_inc(b.s_mm.bump(1), 1)
        mm_gates = b.s_mm.n

        if DBG_PHASE < 4:
            return
        @b.blk.scalar
        def _(s):
            s.wait_ge(b.s_mm.h, mm_gates)
            s.activation(gtmp[:, :], pg[:, :], AF.Exp)
            s.drain().then_inc(b.s_a.bump(1), 1)
        a_exp = b.s_a.n

        gsum = gstat[:, 0:NT]
        gwsum = gstat[:, NT:2 * NT]

        @b.blk.vector
        def _(v):
            v.wait_ge(b.s_a.h, a_exp)
            gv = bass.AP(gtmp, 0, [[gtmp[:, :].ap[0][0], 128], [NEXP1, NT], [1, NEXP1]])
            wv = bass.AP(wsb, 0, [[wsb[:, :].ap[0][0], 128], [NEXP1, NT], [1, NEXP1]])
            sumb = bass.AP(gstat, 0, [[gstat[:, :].ap[0][0], 128], [1, NT], [0, NEXP1]])
            wsumb = bass.AP(gstat, NT, [[gstat[:, :].ap[0][0], 128], [1, NT], [0, NEXP1]])
            v.reduce_sum(gsum[:, :], gv, axis=AXX)
            v.drain()
            v.tensor_scalar_mul(gsum[:, :], gsum[:, :], 0.1)
            v.drain()
            v.tensor_tensor(wv, gv, sumb, ALU.is_ge)
            v.drain()
            v.tensor_mul(wv, wv, gv)
            v.drain()
            v.reduce_sum(gwsum[:, :], wv, axis=AXX)
            v.drain()
            v.reciprocal(gwsum[:, :], gwsum[:, :])
            v.drain()
            v.tensor_scalar_mul(gwsum[:, :], gwsum[:, :], 0.5)
            v.drain()
            v.tensor_tensor(wv, wv, wsumb, ALU.mult)
            v.drain()
            v.tensor_copy(wtb[:, :], wsb[:, :])
            v.drain().then_inc(b.s_v.bump(1), 1)
        v_gates = b.s_v.n

        if DBG_PHASE < 5:
            return
        if DBG_PHASE == 7:
            @b.blk.sync
            def _(sy):
                sy.wait_ge(b.s_v.h, v_gates)
                sy.dma_start(out=bass.AP(b.out, 0, [[D, 128], [1, NT * NEXP1]]),
                             in_=wsb[:, :]).then_inc(b.s_tr.bump(16), 16)
            return
        mm_tr_marks = []
        ntr_loop = 1 if DBG_PHASE == 5.41 else NT

        @b.blk.tensor
        def _(t):
            t.wait_ge(b.s_v.h, v_gates)
            for ti in range(ntr_loop):
                if ti >= 1:
                    t.wait_ge(b.s_v.h, v_gates + ti)
                t.transpose(ptr[0:NEXP1, :], wtb[:, ti * NEXP1:(ti + 1) * NEXP1],
                            b.ident[:, :]).then_inc(b.s_mm.bump(1), 1)
                mm_tr_marks.append(b.s_mm.n)

        @b.blk.vector
        def _(v):
            for ti in range(ntr_loop):
                v.wait_ge(b.s_mm.h, mm_tr_marks[ti])
                v.tensor_copy(wtT[:, ti * 128:(ti + 1) * 128],
                              ptr[0:NEXP1, :]).then_inc(b.s_v.bump(1), 1)
        v_wtT = b.s_v.n

        if DBG_PHASE < 5.8:
            return
        mm_bias_marks = []

        @b.blk.tensor
        def _(t):
            for ti in range(NT):
                t.wait_ge(b.s_v.h, v_wtT - NT + ti + 1)
                if ti >= 2:
                    t.wait_ge(b.s_v.h, v_wtT + 2 * (ti - 1))
                for h in range(2):
                    t.matmul(psum[(2 * ti + h) % 4][:, :],
                             wtT[:, ti * 128:(ti + 1) * 128],
                             ebsb[:, h * 512:(h + 1) * 512],
                             start=True, stop=True).then_inc(b.s_mm.bump(1), 1)
                    mm_bias_marks.append(b.s_mm.n)

        @b.blk.vector
        def _(v):
            for ti in range(NT):
                for h in range(2):
                    v.wait_ge(b.s_mm.h, mm_bias_marks[2 * ti + h])
                    xs = b.XRES[:, ti * D + h * 512: ti * D + (h + 1) * 512]
                    v.tensor_add(xs, xs,
                                 psum[(2 * ti + h) % 4][:, :]).then_inc(b.s_v.bump(1), 1)

        if DBG_PHASE < 6:
            return
        mm_done_expert = {}
        v_done_expert = {}
        for e in range(NEXP1):
            mmb = b.s_mm.n
            vb = b.s_v.n

            @b.blk.tensor
            def _(t, e=e, mmb=mmb, vb=vb):
                t.wait_ge(wsem[e % 2].h, ld_marks[e])
                t.wait_ge(b.s_v.h, vb)
                k = 0
                for ti in range(NT):
                    for h in range(2):
                        if k >= 4:
                            t.wait_ge(b.s_v.h, vb + k - 3)
                        pb = psum[k % 4]
                        for c in range(NCH):
                            mmi = t.matmul(pb[:, :], fmtile(c, ti),
                                           wbuf[e % 2][:, c * D + h * 512:
                                                       c * D + (h + 1) * 512],
                                           start=(c == 0), stop=(c == NCH - 1))
                        mmi.then_inc(b.s_mm.bump(1), 1)
                        k += 1
            mm_done_expert[e] = b.s_mm.n

            @b.blk.gpsimd
            def _(g, e=e):
                if e + 1 < NEXP1:
                    if e >= 1:
                        g.wait_ge(b.s_mm.h, mm_done_expert[e - 1])
                    ld_marks[e + 1] = load_expert(g, e + 1, (e + 1) % 2)

            @b.blk.vector
            def _(v, e=e, mmb=mmb):
                k = 0
                for ti in range(NT):
                    for h in range(2):
                        v.wait_ge(b.s_mm.h, mmb + k + 1)
                        xs = b.XRES[:, ti * D + h * 512: ti * D + (h + 1) * 512]
                        v.scalar_tensor_tensor(
                            xs, psum[k % 4][:, :],
                            wsb[:, ti * NEXP1 + e: ti * NEXP1 + e + 1],
                            xs, ALU.mult, ALU.add).then_inc(b.s_v.bump(1), 1)
                        k += 1
            v_done_expert[e] = b.s_v.n

    def emit_l0ffn(self, tag):
        b = self
        nc = self.nc
        with contextlib.ExitStack() as es:
            old = b.es
            b.es = es
            self._l0ffn(es, tag)
            b.es = old

    def _l0ffn(self, es, tag):
        b = self
        nc = self.nc
        A = lambda n, s, dt: es.enter_context(nc.sbuf_tensor(tag + n, s, dt))
        r_, nmr = b.emit_ln_stats()
        rD = A("rD", [128, NT * 64], BF16)
        rCp = A("rCp", [32, T], BF16)
        with contextlib.ExitStack() as es_ab:
            xtm = es_ab.enter_context(
                nc.sbuf_tensor(tag + "xtm", [128, NT * D], BF16))
            self._l0ffn_passA(es_ab, tag, r_, nmr, xtm, rD, rCp)
            self._l0ffn_passB(es_ab, tag, xtm, rD)
        ysm = A("ysm", [128, 16 * D], BF16)
        with contextlib.ExitStack() as es_c:
            self._l0ffn_passC(es_c, tag, ysm)
        self._l0ffn_passD(es, tag, rCp, ysm)

    def _l0ffn_passA(self, es0, tag, r_, nmr, xtm, rD, rCp):
        b = self
        nc = self.nc
        with contextlib.ExitStack() as es:
            A = lambda n, s, dt: es.enter_context(nc.sbuf_tensor(tag + n, s, dt))
            phiw = A("phiw", [128, NCH * NSLOT], BF16)
            phib = A("phib", [1, NSLOT], BF16)
            fmr = A("fmr", [128, 2 * D], BF16)
            etm = [A(f"etm{i}", [128, NSLOT], BF16) for i in range(2)]
            esm = [A(f"esm{i}", [128, NSLOT], BF16) for i in range(2)]

            @b.blk.gpsimd
            def _(g):
                src = bass.AP(b.phiwT, 0,
                              [[NSLOT, 128], [128 * NSLOT, NCH], [1, NSLOT]])
                b.lg_dma(g, phiw[:, :], src)
                b.lg_dma(g, phib[:, :], b.phib.ap())
            ld_phi = b.s_lg.n

            with (nc.psum_tensor(tag + "pl0", [128, 512], F32) as pl0,
                  nc.psum_tensor(tag + "pl1", [128, 512], F32) as pl1,
                  nc.psum_tensor(tag + "pl2", [128, 512], F32) as pl2,
                  nc.psum_tensor(tag + "pl3", [128, 512], F32) as pl3,
                  nc.psum_tensor(tag + "psd", [128, 64], F32) as psd,
                  nc.psum_tensor(tag + "psc", [32, 128], F32) as psc):
                pls = [pl0, pl1, pl2, pl3]
                vA, trF, mmA, aA, scE, mmS, vS = [], [], [], [], [], [], []
                for ti in range(NT):
                    @b.blk.vector
                    def _(v, ti=ti):
                        if ti == 0:
                            v.wait_ge(b.s_lg.h, ld_phi)
                        v.tensor_scalar(xtm[:, ti * D:(ti + 1) * D],
                                        b.xres_tile(ti), r_[:, ti:ti + 1],
                                        nmr[:, ti:ti + 1], ALU.mult, ALU.add)
                        v.drain().then_inc(b.s_v.bump(1), 1)
                    vA.append(b.s_v.n)

                    @b.blk.sync
                    def _(sy, ti=ti):
                        sy.wait_ge(b.s_v.h, vA[ti])
                        if ti >= 2:
                            sy.wait_ge(b.s_mm.h, mmA[ti - 2])
                        for c in range(NCH):
                            sy.dma_start_transpose(
                                fmr[:, (ti % 2) * D + c * 128:
                                    (ti % 2) * D + (c + 1) * 128],
                                xtm[:, ti * D + c * 128: ti * D + (c + 1) * 128],
                            ).then_inc(b.s_tr.bump(16), 16)
                        sy.wait_ge(b.s_tr.h, b.s_tr.n)
                    trF.append(b.s_tr.n)

                    @b.blk.tensor
                    def _(t, ti=ti):
                        if ti == 0:
                            t.wait_ge(b.s_lg.h, ld_phi)
                        t.wait_ge(b.s_tr.h, trF[ti])
                        if ti >= 1:
                            t.wait_ge(b.s_a.h, aA[ti - 1])
                        for st in range(4):
                            for c in range(NCH):
                                t.matmul(pls[st][:, :],
                                         fmr[:, (ti % 2) * D + c * 128:
                                             (ti % 2) * D + (c + 1) * 128],
                                         phiw[:, c * NSLOT + st * 512:
                                              c * NSLOT + (st + 1) * 512],
                                         start=(c == 0), stop=False)
                            mmi = t.matmul(pls[st][:, :], b.ones[0:1, 0:128],
                                           phib[0:1, st * 512:(st + 1) * 512],
                                           start=False, stop=True)
                        mmi.then_inc(b.s_mm.bump(1), 1)
                    mmA.append(b.s_mm.n)

                    @b.blk.scalar
                    def _(s, ti=ti):
                        s.wait_ge(b.s_mm.h, mmA[ti])
                        if ti >= 2:
                            s.wait_ge(b.s_mm.h, mmS[ti - 2])
                            s.wait_ge(b.s_sc.h, scE[ti - 2])
                        for st in range(4):
                            s.activation(etm[ti % 2][:, st * 512:(st + 1) * 512],
                                         pls[st][:, :], AF.Exp)
                        s.drain().then_inc(b.s_a.bump(1), 1)
                        s.dma_start(
                            out=bass.AP(b.e_tm, ti * 128 * NSLOT,
                                        [[NSLOT, 128], [1, NSLOT]]),
                            in_=etm[ti % 2][:, :]).then_inc(b.s_sc.bump(16), 16)
                        for j in range(16):
                            s.dma_start_transpose(
                                esm[ti % 2][:, j * 128:(j + 1) * 128],
                                etm[ti % 2][:, j * 128:(j + 1) * 128],
                            ).then_inc(b.s_sc.bump(16), 16)
                        s.wait_ge(b.s_sc.h, b.s_sc.n)
                        s.dma_start(
                            out=bass.AP(b.e_sm, ti * 128 * NSLOT,
                                        [[NSLOT, 128], [1, NSLOT]]),
                            in_=esm[ti % 2][:, :]).then_inc(b.s_sc.bump(16), 16)
                    aA.append(b.s_a.n)
                    scE.append(b.s_sc.n)

                    @b.blk.tensor
                    def _(t, ti=ti):
                        t.wait_ge(b.s_sc.h, scE[ti] - 16)
                        if ti >= 1:
                            t.wait_ge(b.s_v.h, vS[ti - 1])
                        for j in range(16):
                            ch = esm[ti % 2][:, j * 128:(j + 1) * 128]
                            t.matmul(psd[:, :], ch,
                                     b.mask64[:, j * 64:(j + 1) * 64],
                                     start=(j == 0), stop=(j == 15))
                        for j in range(16):
                            ch = esm[ti % 2][:, j * 128:(j + 1) * 128]
                            mmi = t.matmul(psc[:, :], b.maskp[:, :], ch,
                                           start=(j == 0), stop=(j == 15))
                        mmi.then_inc(b.s_mm.bump(1), 1)
                    mmS.append(b.s_mm.n)

                    @b.blk.vector
                    def _(v, ti=ti):
                        v.wait_ge(b.s_mm.h, mmS[ti])
                        with nc.allow_low_precision(reason="bf16 softmax divisors"):
                            v.reciprocal(rD[:, ti * 64:(ti + 1) * 64], psd[:, :])
                            v.reciprocal(rCp[:, ti * 128:(ti + 1) * 128],
                                         psc[0:32, :])
                        v.drain().then_inc(b.s_v.bump(1), 1)
                    vS.append(b.s_v.n)
                self._l0_scE_all = b.s_sc.n
                self._l0_vS_all = b.s_v.n

    def _l0ffn_passB(self, es0, tag, xtm, rD):
        b = self
        nc = self.nc
        with contextlib.ExitStack() as es:
            A = lambda n, s, dt: es.enter_context(nc.sbuf_tensor(tag + n, s, dt))
            er = [A(f"er{i}", [128, 512], BF16) for i in range(2)]
            dr = [A(f"dr{i}", [128, 512], BF16) for i in range(2)]
            xst = [A(f"xst{i}", [128, D], F32) for i in range(2)]
            with contextlib.ExitStack() as pes:
                pb = [pes.enter_context(
                    nc.psum_tensor(f"{tag}pb{i}", [128, 512], F32))
                    for i in range(8)]
                gE, vDv, mmB, vEv, spX = [], [], [], [], []
                k = 0
                for g in range(4):
                    for ti in range(NT):
                        slot = k % 2
                        sem = [b.s_ea, b.s_eb][slot]

                        @b.blk.gpsimd
                        def _(gp, g=g, ti=ti, slot=slot, sem=sem, k=k):
                            if k == 0:
                                gp.wait_ge(b.s_sc.h, self._l0_scE_all)
                                gp.wait_ge(b.s_v.h, self._l0_vS_all)
                            if k >= 2:
                                gp.wait_ge(b.s_v.h, vDv[k - 2])
                            src = bass.AP(b.e_tm, (ti * 128 * NSLOT) + g * 512,
                                          [[NSLOT, 128], [1, 512]])
                            gp.dma_start(out=er[slot][:, :], in_=src
                                         ).then_inc(sem.bump(16), 16)
                        gE.append(sem.n)

                        @b.blk.vector
                        def _(v, g=g, ti=ti, slot=slot, sem=sem, k=k):
                            v.wait_ge(sem.h, gE[k])
                            if k >= 2:
                                v.wait_ge(b.s_mm.h, mmB[k - 2])
                            dvs = bass.AP(rD, ti * 64 + 16 * g,
                                          [[rD[:, :].ap[0][0], 128], [1, 16], [0, PSL]])
                            v.tensor_tensor(dr[slot][:, :], er[slot][:, :],
                                            dvs, ALU.mult)
                            v.drain().then_inc(b.s_v.bump(1), 1)
                        vDv.append(b.s_v.n)

                        @b.blk.tensor
                        def _(t, g=g, ti=ti, slot=slot, k=k):
                            t.wait_ge(b.s_v.h, vDv[k])
                            if ti == 0 and g > 0:
                                t.wait_ge(b.s_v.h, vEv[g - 1])
                            for ch in range(4):
                                for h in range(2):
                                    mmi = t.matmul(
                                        pb[ch * 2 + h][:, :],
                                        dr[slot][:, ch * 128:(ch + 1) * 128],
                                        xtm[:, ti * D + h * 512:
                                            ti * D + (h + 1) * 512],
                                        start=(ti == 0), stop=(ti == NT - 1))
                            mmi.then_inc(b.s_mm.bump(1), 1)
                        mmB.append(b.s_mm.n)
                        k += 1

                    for ch in range(4):
                        @b.blk.vector
                        def _(v, g=g, ch=ch):
                            if ch == 0:
                                v.wait_ge(b.s_mm.h, mmB[-1])
                            if len(spX) >= 2:
                                v.wait_ge(b.s_sp.h, spX[-2])
                            v.tensor_copy(xst[ch % 2][:, 0:512], pb[ch * 2][:, :])
                            v.tensor_copy(xst[ch % 2][:, 512:1024],
                                          pb[ch * 2 + 1][:, :])
                            v.drain().then_inc(b.s_v.bump(1), 1)
                        vch = b.s_v.n

                        @b.blk.gpsimd
                        def _(gp, g=g, ch=ch, vch=vch):
                            gp.wait_ge(b.s_v.h, vch)
                            dst = bass.AP(b.xt_in, (g * 512 + ch * 128) * D,
                                          [[D, 128], [1, D]])
                            gp.dma_start(out=dst, in_=xst[ch % 2][:, :]
                                         ).then_inc(b.s_sp.bump(16), 16)
                        spX.append(b.s_sp.n)
                    vEv.append(b.s_v.n)

    def _l0ffn_passC(self, es, tag, ysm):
        b = self
        nc = self.nc
        A = lambda n, s, dt: es.enter_context(nc.sbuf_tensor(tag + n, s, dt))
        xtc = A("xtc", [128, 2 * D], F32)
        xtt = A("xtt", [128, 2 * D], BF16)
        xtf = A("xtf", [128, 2 * D], BF16)
        ytst = [A(f"ytst{i}", [32, D], BF16) for i in range(2)]
        l0eb = A("l0eb", [1, 8 * D], BF16)
        st2 = b.STATS[:, 512:544]

        @b.blk.gpsimd
        def _(g):
            g.wait_ge(b.s_sp.h, b.s_sp.n)
            g.collective_compute(
                "ReduceScatter", mybir.AluOpType.add,
                replica_groups=[list(range(NCORES))],
                ins=[b.xt_in.ap().opt()], outs=[b.xt_rs.ap().opt()],
            ).then_inc(b.s_cc.bump(1), 1)
            g.wait_ge(b.s_cc.h, b.s_cc.n)
            src = bass.AP(b.xt_rs, 0, [[D, 128], [128 * D, 2], [1, D]])
            b.lg_dma(g, xtc[:, :], src)
            b.lg_dma(g, l0eb[:, :], bass.AP(b.l0eb, 0, [[8 * D, 1], [1, 8 * D]]))
        ld_xtc = b.s_lg.n

        wbuf = [A(f"wb{i}", [128, NCH * D], BF16) for i in range(2)]
        wsem = [b.s_wa, b.s_wb]
        ldW = {}

        @b.blk.gpsimd
        def _(g):
            src = bass.AP(b.l0ew, 0, [[D, 128], [128 * D, NCH], [1, D]])
            g.dma_start(out=wbuf[0][:, :], in_=src).then_inc(b.s_wa.bump(16), 16)
            ldW[0] = b.s_wa.n

        @b.blk.vector
        def _(v):
            v.wait_ge(b.s_lg.h, ld_xtc)
            v.tensor_scalar(xtc[:, :], xtc[:, :], -33000.0, 65000.0,
                            ALU.max, ALU.min)
            v.drain()
            bnB = b.STATS[:, 544:592]
            for i in range(2):
                for hh in range(2):
                    v.bn_stats(bnB[:, i * 12 + hh * 6:i * 12 + hh * 6 + 6],
                               xtc[:, i * D + hh * 512:i * D + (hh + 1) * 512])
            v.drain()
            for i in range(2):
                v.bn_aggr(st2[:, 2 * i:2 * i + 2], bnB[:, i * 12:i * 12 + 12])
            v.drain().then_inc(b.s_v.bump(1), 1)

        @b.blk.scalar
        def _(s):
            s.wait_ge(b.s_v.h, b.s_v.n)
            varv = bass.AP(b.STATS, 512 + 1, [[768, 128], [2, 2]])
            s.activation(st2[:, 4:6], varv, AF.Sqrt, bias=b.ceps[:, 0:1])
            s.drain().then_inc(b.s_a.bump(1), 1)

        @b.blk.vector
        def _(v):
            v.wait_ge(b.s_a.h, b.s_a.n)
            v.reciprocal(st2[:, 4:6], st2[:, 4:6])
            v.drain()
            muv = bass.AP(b.STATS, 512, [[768, 128], [2, 2]])
            v.scalar_tensor_tensor(st2[:, 6:8], muv, -1.0, st2[:, 4:6],
                                   ALU.mult, ALU.mult)
            v.drain()
            for i in range(2):
                v.tensor_scalar(xtc[:, i * D:(i + 1) * D],
                                xtc[:, i * D:(i + 1) * D],
                                st2[:, 4 + i:5 + i], st2[:, 6 + i:7 + i],
                                ALU.mult, ALU.add)
            v.drain().then_inc(b.s_v.bump(1), 1)

        @b.blk.scalar
        def _(s):
            s.wait_ge(b.s_v.h, b.s_v.n)
            s.activation(xtt[:, :], xtc[:, :], AF.Tanh)
            s.drain().then_inc(b.s_a.bump(1), 1)
        a_tanh = b.s_a.n

        @b.blk.sync
        def _(sy):
            sy.wait_ge(b.s_a.h, a_tanh)
            for i in range(2):
                for j in range(NCH):
                    sy.dma_start_transpose(
                        xtf[:, j * 256 + i * 128: j * 256 + (i + 1) * 128],
                        xtt[:, i * D + j * 128: i * D + (j + 1) * 128],
                    ).then_inc(b.s_tr.bump(16), 16)
            sy.wait_ge(b.s_tr.h, b.s_tr.n)
        tr_xt = b.s_tr.n

        with (nc.psum_tensor(tag + "pe0", [32, D], F32) as pe0,
              nc.psum_tensor(tag + "pe1", [32, D], F32) as pe1):
            pes_ = [pe0, pe1]
            mmE, vY, spY = [], [], []
            for e in range(8):
                @b.blk.tensor
                def _(t, e=e):
                    if e == 0:
                        t.wait_ge(b.s_tr.h, tr_xt)
                        t.wait_ge(b.s_lg.h, ld_xtc)
                    t.wait_ge(wsem[e % 2].h, ldW[e])
                    if e >= 2:
                        t.wait_ge(b.s_v.h, vY[e - 2])
                    i, so = e // 4, (e % 4) * 32
                    for h in range(2):
                        for j in range(NCH):
                            t.matmul(pes_[e % 2][:, h * 512:(h + 1) * 512],
                                     xtf[:, j * 256 + i * 128 + so:
                                         j * 256 + i * 128 + so + 32],
                                     wbuf[e % 2][:, j * D + h * 512:
                                                 j * D + (h + 1) * 512],
                                     start=(j == 0), stop=False)
                        mmi = t.matmul(pes_[e % 2][:, h * 512:(h + 1) * 512],
                                       b.ones[0:1, 0:32],
                                       l0eb[0:1, e * D + h * 512:
                                            e * D + (h + 1) * 512],
                                       start=False, stop=True)
                    mmi.then_inc(b.s_mm.bump(1), 1)
                mmE.append(b.s_mm.n)

                @b.blk.gpsimd
                def _(g, e=e):
                    if e + 1 < 8:
                        if e >= 1:
                            g.wait_ge(b.s_mm.h, mmE[e - 1])
                        src = bass.AP(b.l0ew, (e + 1) * D * D,
                                      [[D, 128], [128 * D, NCH], [1, D]])
                        g.dma_start(out=wbuf[(e + 1) % 2][:, :], in_=src
                                    ).then_inc(wsem[(e + 1) % 2].bump(16), 16)
                        ldW[e + 1] = wsem[(e + 1) % 2].n

                @b.blk.vector
                def _(v, e=e):
                    v.wait_ge(b.s_mm.h, mmE[e])
                    if e >= 2:
                        v.wait_ge(b.s_sp.h, spY[e - 2])
                    v.tensor_copy(ytst[e % 2][:, :], pes_[e % 2][0:32, :])
                    v.drain().then_inc(b.s_v.bump(1), 1)
                vY.append(b.s_v.n)

                @b.blk.gpsimd
                def _(g, e=e):
                    g.wait_ge(b.s_v.h, vY[e])
                    dst = bass.AP(b.yt_in, e * 32 * D, [[D, 32], [1, D]])
                    g.dma_start(out=dst, in_=ytst[e % 2][:, :]
                                ).then_inc(b.s_sp.bump(16), 16)
                spY.append(b.s_sp.n)

        @b.blk.gpsimd
        def _(g):
            g.wait_ge(b.s_sp.h, b.s_sp.n)
            g.collective_compute(
                "AllGather", mybir.AluOpType.bypass,
                replica_groups=[list(range(NCORES))],
                ins=[b.yt_in.ap().opt()], outs=[b.yt_ag.ap().opt()],
            ).then_inc(b.s_cc.bump(1), 1)
            g.wait_ge(b.s_cc.h, b.s_cc.n)
            src = bass.AP(b.yt_ag, 0, [[D, 128], [128 * D, 16], [1, D]])
            b.lg_dma(g, ysm[:, :], src)

    def _l0ffn_passD(self, es, tag, rCp, ysm):
        b = self
        nc = self.nc
        A = lambda n, s, dt: es.enter_context(nc.sbuf_tensor(tag + n, s, dt))
        rCe = A("rCe", [128, T], BF16)
        emr = [A(f"emr{i}", [128, NSLOT], BF16) for i in range(2)]
        cr = [A(f"cr{i}", [128, NSLOT], BF16) for i in range(2)]
        ld_ysm = b.s_lg.n

        @b.blk.gpsimd
        def _(g):
            g.wait_ge(b.s_tr.h, b.s_tr.n)
            g.wait_ge(b.s_v.h, b.s_v.n)
            for i in range(4):
                b.lg_dma(g, rCe[32 * i:32 * (i + 1), :], rCp[0:32, :])
        ld_rce = b.s_lg.n

        with contextlib.ExitStack() as pes:
            pd = [pes.enter_context(
                nc.psum_tensor(f"{tag}pd{i}", [128, D], F32)) for i in range(2)]
            gE, vC, mmD, vR = [], [], [], []
            for ti in range(NT):
                slot = ti % 2
                sem = [b.s_ea, b.s_eb][slot]

                @b.blk.gpsimd
                def _(g, ti=ti, slot=slot, sem=sem):
                    if ti < 2:
                        g.wait_ge(b.s_lg.h, ld_rce)
                    else:
                        g.wait_ge(b.s_v.h, vC[ti - 2])
                    src = bass.AP(b.e_sm, ti * 128 * NSLOT, [[NSLOT, 128], [1, NSLOT]])
                    g.dma_start(out=emr[slot][:, :], in_=src
                                ).then_inc(sem.bump(16), 16)
                gE.append(sem.n)

                @b.blk.vector
                def _(v, ti=ti, slot=slot, sem=sem):
                    v.wait_ge(sem.h, gE[ti])
                    if ti >= 2:
                        v.wait_ge(b.s_mm.h, mmD[ti - 2])
                    dvs = bass.AP(rCe, ti * 128,
                                  [[rCe[:, :].ap[0][0], 128], [0, 16], [1, 128]])
                    v.tensor_tensor(cr[slot][:, :], emr[slot][:, :], dvs, ALU.mult)
                    v.drain().then_inc(b.s_v.bump(1), 1)
                vC.append(b.s_v.n)

                @b.blk.tensor
                def _(t, ti=ti, slot=slot):
                    t.wait_ge(b.s_v.h, vC[ti])
                    if ti == 0:
                        t.wait_ge(b.s_lg.h, ld_ysm)
                    if ti >= 2:
                        t.wait_ge(b.s_v.h, vR[ti - 2])
                    for h in range(2):
                        for j in range(16):
                            mmi = t.matmul(pd[slot][:, h * 512:(h + 1) * 512],
                                           cr[slot][:, j * 128:(j + 1) * 128],
                                           ysm[:, j * D + h * 512:
                                               j * D + (h + 1) * 512],
                                           start=(j == 0), stop=(j == 15))
                    mmi.then_inc(b.s_mm.bump(1), 1)
                mmD.append(b.s_mm.n)

                @b.blk.vector
                def _(v, ti=ti, slot=slot):
                    v.wait_ge(b.s_mm.h, mmD[ti])
                    for h in range(2):
                        xs = b.XRES[:, ti * D + h * 512: ti * D + (h + 1) * 512]
                        v.scalar_tensor_tensor(xs, pd[slot][:, h * 512:(h + 1) * 512],
                                               0.5, xs, ALU.mult, ALU.add)
                    v.drain().then_inc(b.s_v.bump(1), 1)
                vR.append(b.s_v.n)

    def emit_l0attn(self):
        b = self
        nc = self.nc
        with contextlib.ExitStack() as es:
            old = b.es
            b.es = es
            self._l0attn(es)
            b.es = old

    def _l0attn(self, es):
        b = self
        nc = self.nc
        tag = "l0a_"
        A = lambda n, s, dt: es.enter_context(nc.sbuf_tensor(tag + n, s, dt))
        r_, nmr = b.emit_ln_stats()
        gain0 = A("gain0", [128, D], BF16)
        gain0o = A("gain0o", [128, D], BF16)

        @b.blk.gpsimd
        def _(g):
            bc = lambda t_: bass.AP(t_, 0, [[0, 128], [1, D]])
            b.lg_dma(g, gain0[:, :], bc(b.l0ng))
            b.lg_dma(g, gain0o[:, :], bc(b.l0og))

        k2 = A("k2", [128, T], BF16)
        vt1 = A("vt1", [128, NT * 65], BF16)
        rq = A("rq", [128, NT * 8], F32)
        rk = A("rk", [128, NT], F32)
        sqq = A("sqq", [128, NT * 8], F32)
        ksq = A("ksq", [128, 64], F32)
        woutp = A("woutp", [64, 8 * D], BF16)
        q_fm = A("qfm", [128, 4 * T], BF16)
        k_fm = A("kfm", [128, T], BF16)
        osb = A("osb", [64, 8 * 512], BF16)
        with contextlib.ExitStack() as es1:
            A1 = lambda n, s, dt: es1.enter_context(nc.sbuf_tensor(tag + n, s, dt))
            tmp = A1("tmp", [128, 2 * D], BF16)
            fmr = A1("fmr", [128, 2 * D], BF16)
            q_tm = A1("qtm", [128, NT * 512], BF16)
            wq = A1("wq", [128, NCH * 512], BF16)
            wkv = A1("wkv", [128, NCH * 128], BF16)
            sqs = [A1(f"sqs{i}", [128, 512], BF16) for i in range(2)]

            @b.blk.gpsimd
            def _(g):
                b.lg_dma(g, wq[:, :], bass.AP(
                    b.wq0, 0, [[512, 128], [128 * 512, NCH], [1, 512]]))
                b.lg_dma(g, wkv[:, :], bass.AP(
                    b.wkv0, 0, [[128, 128], [128 * 128, NCH], [1, 128]]))
                b.lg_dma(g, woutp[:, :], bass.AP(
                    b.wout0, 0, [[D, 64], [64 * D, 8], [1, D]]))
                g.wait_ge(b.s_v.h, b.s_v.n)
                g.memset(bass.AP(vt1, 64, [[vt1[:, :].ap[0][0], 128],
                                           [65, NT], [1, 1]]),
                         1.0).then_inc(b.s_g.bump(1), 1)
            ld_w = b.s_lg.n

            with (nc.psum_tensor(tag + "pq0", [128, 512], F32) as pq0,
                  nc.psum_tensor(tag + "pq1", [128, 512], F32) as pq1,
                  nc.psum_tensor(tag + "pk0", [128, 128], F32) as pk0,
                  nc.psum_tensor(tag + "pk1", [128, 128], F32) as pk1):
                pqs, pks = [pq0, pq1], [pk0, pk1]
                vLn, trR, mmq, vq, aq, vq2 = [], [], [], [], [], []
                for ti in range(NT):
                    @b.blk.vector
                    def _(v, ti=ti):
                        if ti >= 2:
                            v.wait_ge(b.s_tr.h, trR[ti - 2])
                        v.tensor_scalar(tmp[:, (ti % 2) * D:(ti % 2 + 1) * D],
                                        b.xres_tile(ti), r_[:, ti:ti + 1],
                                        nmr[:, ti:ti + 1], ALU.mult, ALU.add)
                        v.drain()
                        v.tensor_mul(tmp[:, (ti % 2) * D:(ti % 2 + 1) * D],
                                     tmp[:, (ti % 2) * D:(ti % 2 + 1) * D],
                                     gain0[:, :])
                        v.drain().then_inc(b.s_v.bump(1), 1)
                    vLn.append(b.s_v.n)

                    @b.blk.sync
                    def _(sy, ti=ti):
                        sy.wait_ge(b.s_v.h, vLn[ti])
                        if ti >= 2:
                            sy.wait_ge(b.s_mm.h, mmq[ti - 2])
                        for c in range(NCH):
                            sy.dma_start_transpose(
                                fmr[:, (ti % 2) * D + c * 128:
                                    (ti % 2) * D + (c + 1) * 128],
                                tmp[:, (ti % 2) * D + c * 128:
                                    (ti % 2) * D + (c + 1) * 128],
                            ).then_inc(b.s_tr.bump(16), 16)
                        sy.wait_ge(b.s_tr.h, b.s_tr.n)
                    trR.append(b.s_tr.n)

                    @b.blk.tensor
                    def _(t, ti=ti):
                        if ti == 0:
                            t.wait_ge(b.s_lg.h, ld_w)
                            t.wait_ge(b.s_g.h, b.s_g.n)
                        t.wait_ge(b.s_tr.h, trR[ti])
                        if ti >= 2:
                            t.wait_ge(b.s_a.h, aq[ti - 2])
                            t.wait_ge(b.s_v.h, vq[ti - 2])
                        for c in range(NCH):
                            t.matmul(pqs[ti % 2][:, :],
                                     fmr[:, (ti % 2) * D + c * 128:
                                         (ti % 2) * D + (c + 1) * 128],
                                     wq[:, c * 512:(c + 1) * 512],
                                     start=(c == 0), stop=(c == NCH - 1))
                        for c in range(NCH):
                            mmi = t.matmul(pks[ti % 2][:, :],
                                           fmr[:, (ti % 2) * D + c * 128:
                                               (ti % 2) * D + (c + 1) * 128],
                                           wkv[:, c * 128:(c + 1) * 128],
                                           start=(c == 0), stop=(c == NCH - 1))
                        mmi.then_inc(b.s_mm.bump(1), 1)
                    mmq.append(b.s_mm.n)

                    @b.blk.scalar
                    def _(s, ti=ti):
                        s.wait_ge(b.s_mm.h, mmq[ti])
                        if ti >= 2:
                            s.wait_ge(b.s_v.h, vq2[ti - 2])
                        s.activation(sqs[ti % 2][:, :], pqs[ti % 2][:, :],
                                     AF.Square)
                        s.drain().then_inc(b.s_a.bump(1), 1)
                    aq.append(b.s_a.n)

                    @b.blk.vector
                    def _(v, ti=ti):
                        v.wait_ge(b.s_mm.h, mmq[ti])
                        v.tensor_copy(q_tm[:, ti * 512:(ti + 1) * 512],
                                      pqs[ti % 2][:, :])
                        v.drain()
                        v.tensor_copy(k2[:, ti * 128: ti * 128 + 64],
                                      pks[ti % 2][:, 0:64])
                        v.drain()
                        v.tensor_copy(vt1[:, ti * 65: ti * 65 + 64],
                                      pks[ti % 2][:, 64:128])
                        v.drain().then_inc(b.s_v.bump(1), 1)
                    vq.append(b.s_v.n)

                    @b.blk.vector
                    def _(v, ti=ti):
                        v.wait_ge(b.s_a.h, aq[ti])
                        qv = bass.AP(sqs[ti % 2], 0,
                                     [[sqs[ti % 2][:, :].ap[0][0], 128], [64, 8], [1, 64]])
                        v.tensor_reduce(sqq[:, ti * 8:(ti + 1) * 8], qv,
                                        axis=AXX, op=ALU.add)
                        v.drain().then_inc(b.s_v.bump(1), 1)
                    vq2.append(b.s_v.n)

                @b.blk.vector
                def _(v):
                    v.wait_ge(b.s_v.h, vq[-1])
                    for ti in range(NT):
                        kk = k2[:, ti * 128: ti * 128 + 64]
                        v.tensor_mul(ksq[:, :], kk, kk)
                        v.drain()
                        v.tensor_reduce(rk[:, ti:ti + 1], ksq[:, :],
                                        axis=AXX, op=ALU.add)
                        v.drain()
                    v.drain().then_inc(b.s_v.bump(1), 1)

                @b.blk.scalar
                def _(s):
                    s.wait_ge(b.s_v.h, b.s_v.n)
                    s.activation(sqq[:, :], sqq[:, :], AF.Sqrt, bias=b.ceps[:, 1:2])
                    s.drain()
                    s.activation(rk[:, :], rk[:, :], AF.Sqrt, bias=b.ceps[:, 1:2])
                    s.drain().then_inc(b.s_a.bump(1), 1)

                @b.blk.vector
                def _(v):
                    v.wait_ge(b.s_a.h, b.s_a.n)
                    v.reciprocal(sqq[:, :], sqq[:, :])
                    v.drain()
                    v.reciprocal(rk[:, :], rk[:, :])
                    v.drain()
                    v.tensor_scalar_mul(sqq[:, :], sqq[:, :], 4.0)
                    v.drain()
                    v.tensor_scalar_mul(rk[:, :], rk[:, :], 4.0)
                    v.drain()
                    qb = bass.AP(sqq, 0, [[sqq[:, :].ap[0][0], 128], [1, NT * 8], [0, 64]])
                    qv2 = bass.AP(q_tm, 0, [[q_tm[:, :].ap[0][0], 128], [64, NT * 8], [1, 64]])
                    v.tensor_tensor(qv2, qv2, qb, ALU.mult)
                    v.drain()
                    kb = bass.AP(rk, 0, [[rk[:, :].ap[0][0], 128], [1, NT], [0, 64]])
                    k_s = bass.AP(k2, 0, [[k2[:, :].ap[0][0], 128], [128, NT], [1, 64]])
                    k_d = bass.AP(k2, 64, [[k2[:, :].ap[0][0], 128], [128, NT], [1, 64]])
                    v.tensor_tensor(k_s, k_s, kb, ALU.mult)
                    v.drain()
                    v.tensor_copy(k_d, k_s)
                    v.drain().then_inc(b.s_v.bump(1), 1)
                v_scale_done = b.s_v.n

            @b.blk.sync
            def _(sy):
                sy.wait_ge(b.s_v.h, v_scale_done)
                for ti in range(NT):
                    for hp in range(4):
                        sy.dma_start_transpose(
                            q_fm[:, hp * T + ti * 128:hp * T + (ti + 1) * 128],
                            q_tm[:, ti * 512 + hp * 128: ti * 512 + (hp + 1) * 128],
                        ).then_inc(b.s_tr.bump(16), 16)
                    sy.dma_start_transpose(
                        k_fm[:, ti * 128:(ti + 1) * 128],
                        k2[:, ti * 128:(ti + 1) * 128],
                    ).then_inc(b.s_tr.bump(16), 16)
                    if ti % 4 == 3:
                        sy.wait_ge(b.s_tr.h, b.s_tr.n)
                sy.wait_ge(b.s_tr.h, b.s_tr.n)
            tr_qk = b.s_tr.n

        et = A("et", [128, 32 * 512], BF16)
        gout = A("gout", [128, D], BF16)
        rsum = A("rsum", [1, 512], F32)
        bnC = b.STATS[:, 600:624]
        mvC = b.STATS[:, 624:626]
        rC3 = b.STATS[:, 626:627]
        nmC = b.STATS[:, 627:628]
        varC = bass.AP(b.STATS, 625, [[768, 128], [1, 1]])
        muC = bass.AP(b.STATS, 624, [[768, 128], [1, 1]])
        mmO_prev = [0]
        mmP_last = [0]
        aE_last = [0, 0]
        vN_last = [0, 0]
        vP_last = [0, 0]
        with (nc.psum_tensor(tag + "ps0", [128, 512], F32) as ps0,
              nc.psum_tensor(tag + "ps1", [128, 512], F32) as ps1,
              nc.psum_tensor(tag + "po0", [65, 512], F32) as po0,
              nc.psum_tensor(tag + "po1", [65, 512], F32) as po1,
              nc.psum_tensor(tag + "pbc", [64, 512], F32) as pbc,
              nc.psum_tensor(tag + "pp0", [128, 512], F32) as pp0,
              nc.psum_tensor(tag + "pp1", [128, 512], F32) as pp1):
            pss, pos = [ps0, ps1], [po0, po1]
            for qt in range(8):
                mmS_, aE, mmO, vN, mmB, aO, vO = [], [], [], [], [], [], []
                if qt == 0:
                    aO_last = [0, 0]
                for h in range(8):
                    hb = (h % 2) * 64
                    hp = h // 2

                    aE_base = b.s_a.n

                    @b.blk.tensor
                    def _(t, qt=qt, h=h, hb=hb, hp=hp, aE_base=aE_base):
                        if qt == 0 and h == 0:
                            t.wait_ge(b.s_tr.h, tr_qk)
                        for kc in range(32):
                            if kc >= 2:
                                t.wait_ge(b.s_a.h, aE_base + kc - 1)
                            elif h == 0 and qt == 0:
                                pass
                            else:
                                t.wait_ge(b.s_a.h, aE_base - 2 + kc + 1)
                            t.matmul(pss[kc % 2][:, :],
                                     k_fm[hb:hb + 64, kc * 128:(kc + 1) * 128],
                                     q_fm[hb:hb + 64,
                                          hp * T + qt * 512:hp * T + (qt + 1) * 512],
                                     start=True, stop=True,
                                     ).then_inc(b.s_mm.bump(1), 1)
                    mmS_.append(b.s_mm.n)

                    @b.blk.scalar
                    def _(s, qt=qt, h=h):
                        base = mmS_[h] - 32
                        if h >= 1 or qt > 0:
                            s.wait_ge(b.s_mm.h, mmO_prev[0])
                        for kc in range(32):
                            s.wait_ge(b.s_mm.h, base + kc + 1)
                            s.activation(et[:, kc * 512:(kc + 1) * 512],
                                         pss[kc % 2][:, :], AF.Exp,
                                         bias=b.ceps[:, 2:3],
                                         ).then_inc(b.s_a.bump(1), 1)
                        s.drain().then_inc(b.s_a.bump(1), 1)
                    aE.append(b.s_a.n)

                    @b.blk.tensor
                    def _(t, qt=qt, h=h):
                        t.wait_ge(b.s_a.h, aE[h])
                        if h >= 2:
                            t.wait_ge(b.s_v.h, vN[h - 2])
                            t.wait_ge(b.s_a.h, aO[h - 2])
                        elif qt > 0:
                            t.wait_ge(b.s_v.h, vN_last[h])
                            t.wait_ge(b.s_a.h, aO_last[h])
                        for kc in range(32):
                            mmi = t.matmul(pos[h % 2][:, :],
                                           vt1[:, kc * 65:(kc + 1) * 65],
                                           et[:, kc * 512:(kc + 1) * 512],
                                           start=(kc == 0), stop=(kc == 31))
                        mmi.then_inc(b.s_mm.bump(1), 1)
                        mmO_prev[0] = b.s_mm.n
                    mmO.append(b.s_mm.n)

                    @b.blk.vector
                    def _(v, qt=qt, h=h):
                        v.wait_ge(b.s_mm.h, mmO[h])
                        if h >= 1:
                            v.wait_ge(b.s_mm.h, mmB[h - 1])
                        with nc.allow_low_precision(reason="softmax recip"):
                            v.reciprocal(rsum[0:1, :], pos[h % 2][64:65, :])
                        v.drain().then_inc(b.s_v.bump(1), 1)
                    vN.append(b.s_v.n)

                    @b.blk.tensor
                    def _(t, qt=qt, h=h):
                        t.wait_ge(b.s_v.h, vN[h])
                        if h >= 1:
                            t.wait_ge(b.s_v.h, vO[h - 1])
                        t.matmul(pbc[:, :], b.onesf[0:1, 0:64], rsum[0:1, :],
                                 start=True, stop=True).then_inc(b.s_mm.bump(1), 1)
                    mmB.append(b.s_mm.n)

                    @b.blk.scalar
                    def _(s, qt=qt, h=h):
                        s.wait_ge(b.s_mm.h, mmO[h])
                        if qt > 0 and h == 0:
                            s.wait_ge(b.s_mm.h, mmP_last[0])
                        s.activation(osb[0:64, h * 512:(h + 1) * 512],
                                     pos[h % 2][0:64, :], AF.Copy)
                        s.drain().then_inc(b.s_a.bump(1), 1)
                    aO.append(b.s_a.n)

                    @b.blk.vector
                    def _(v, qt=qt, h=h):
                        v.wait_ge(b.s_mm.h, mmB[h])
                        v.wait_ge(b.s_a.h, aO[h])
                        v.tensor_tensor(osb[0:64, h * 512:(h + 1) * 512],
                                        osb[0:64, h * 512:(h + 1) * 512],
                                        pbc[0:64, :], ALU.mult)
                        v.drain().then_inc(b.s_v.bump(1), 1)
                    vO.append(b.s_v.n)
                aE_last = aE[-2:]
                vN_last = vN[-2:]
                aO_last = aO[-2:]

                mmP, vP = [], []
                for sub in range(4):
                    ti = qt * 4 + sub

                    @b.blk.tensor
                    def _(t, qt=qt, sub=sub):
                        if sub == 0:
                            t.wait_ge(b.s_v.h, vO[-1])
                        if sub >= 1:
                            t.wait_ge(b.s_v.h, vP[sub - 1])
                        elif qt > 0:
                            t.wait_ge(b.s_v.h, vP_last[-1])
                        for half, pps in ((0, pp0), (1, pp1)):
                            for h in range(8):
                                mmi = t.matmul(
                                    pps[:, :],
                                    osb[0:64, h * 512 + sub * 128:
                                        h * 512 + (sub + 1) * 128],
                                    woutp[0:64, h * D + half * 512:
                                          h * D + (half + 1) * 512],
                                    start=(h == 0), stop=(h == 7))
                            mmi.then_inc(b.s_mm.bump(1), 1)
                    mmP.append(b.s_mm.n)
                    mmP_last[0] = b.s_mm.n

                    @b.blk.vector
                    def _(v, qt=qt, sub=sub, ti=ti):
                        v.wait_ge(b.s_mm.h, mmP[sub])
                        v.bn_stats(bnC[:, 0:6], pp0[:, :])
                        v.bn_stats(bnC[:, 6:12], pp1[:, :])
                        v.drain()
                        v.bn_aggr(mvC[:, :], bnC[:, 0:12])
                        v.drain().then_inc(b.s_v.bump(1), 1)

                    @b.blk.scalar
                    def _(s, qt=qt, sub=sub):
                        s.wait_ge(b.s_v.h, b.s_v.n)
                        s.activation(rC3[:, :], varC, AF.Sqrt, bias=b.ceps[:, 0:1])
                        s.drain().then_inc(b.s_a.bump(1), 1)

                    @b.blk.vector
                    def _(v, qt=qt, sub=sub, ti=ti):
                        v.wait_ge(b.s_a.h, b.s_a.n)
                        v.reciprocal(rC3[:, :], rC3[:, :])
                        v.drain()
                        v.scalar_tensor_tensor(nmC[:, :], muC, -1.0, rC3[:, :],
                                               ALU.mult, ALU.mult)
                        v.drain()
                        for half, pps in ((0, pp0), (1, pp1)):
                            v.tensor_scalar(gout[:, half * 512:(half + 1) * 512],
                                            pps[:, :], rC3[:, 0:1], nmC[:, 0:1],
                                            ALU.mult, ALU.add)
                        v.drain()
                        v.tensor_mul(gout[:, :], gout[:, :], gain0o[:, :])
                        v.drain()
                        v.tensor_add(b.xres_tile(ti), b.xres_tile(ti), gout[:, :])
                        v.drain().then_inc(b.s_v.bump(1), 1)
                    vP.append(b.s_v.n)
                vP_last = vP[-2:]


    def emit_l1attn(self):
        b = self
        nc = self.nc
        with contextlib.ExitStack() as es:
            old = b.es
            b.es = es
            self._l1attn(es)
            b.es = old

    def _l1attn(self, es):
        b = self
        nc = self.nc
        tag = "l1a_"
        A = lambda n, s, dt: es.enter_context(nc.sbuf_tensor(tag + n, s, dt))
        r_, nmr = b.emit_ln_stats()

        wq = A("wq", [128, NCH * D], BF16)
        wk = A("wk", [128, NCH * D], BF16)
        wv = A("wv", [128, NCH * D], BF16)
        wo = A("wo", [128, NCH * D], BF16)
        wgm = A("wgm", [128, NCH * 16], BF16)
        wf1 = A("wf1", [128, NCH * 64], BF16)
        wf2 = A("wf2", [128, D], BF16)
        wog1 = A("wog1", [128, NCH * 64], BF16)
        wog2 = A("wog2", [128, D], BF16)
        ctu = A("ctu", [128, 128], F32)

        @b.blk.gpsimd
        def _(g):
            chl = lambda t_, w_: bass.AP(t_, 0, [[w_, 128], [128 * w_, NCH], [1, w_]])
            b.lg_dma(g, wq[:, :], chl(b.wq1, D))
            b.lg_dma(g, wk[:, :], chl(b.wk1, D))
            b.lg_dma(g, wv[:, :], chl(b.wv1, D))
            b.lg_dma(g, wo[:, :], chl(b.wo1, D))
            b.lg_dma(g, wgm[:, :], chl(b.wgam, 16))
            b.lg_dma(g, wf1[:, :], chl(b.wf1, 64))
            b.lg_dma(g, wog1[:, :], chl(b.wog1, 64))
            b.lg_dma(g, wf2[0:64, :], b.wf2.ap())
            b.lg_dma(g, wf2[64:128, :], b.wf2.ap())
            b.lg_dma(g, wog2[0:64, :], b.wog2.ap())
            b.lg_dma(g, wog2[64:128, :], b.wog2.ap())
            b.lg_dma(g, ctu[:, :], b.ctriu.ap())
        ld_w = b.s_lg.n

        S_sb = A("S", [128, 8 * 64], BF16)
        S0m = A("S0m", [128, 8 * 64], BF16)
        powTT = A("powTT", [128, 14 * 128], BF16)
        powT = [powTT[:, 0:896], powTT[:, 896:1792]]
        Lr = [A(f"Lr{i}", [128, 128], BF16) for i in range(2)]
        PTs = None
        usb = [A(f"usb{i}", [128, 64], BF16) for i in range(2)]
        PTs = Lr
        zeros = A("zeros", [128, 128], BF16)
        mskL = es.enter_context(nc.sbuf_tensor(tag + "mskL", [128, 128],
                                               mybir.dt.uint8))
        mskU = es.enter_context(nc.sbuf_tensor(tag + "mskU", [128, 128],
                                               mybir.dt.uint8))

        @b.blk.gpsimd
        def _(g):
            g.memset(zeros[:, :], 0.0)
            g.memset(S_sb[:, :], 0.0).then_inc(b.s_g.bump(1), 1)
            b.lg_dma(g, mskL[:, :], b.cmskl.ap())
            b.lg_dma(g, mskU[:, :], b.cmsku.ap())

        RD = 2
        qf = [A(f"qf{i}", [128, 8 * 128], BF16) for i in range(RD)]
        kcf = [A(f"kcf{i}", [128, 8 * 128], BF16) for i in range(RD)]
        bhf = [A(f"bhf{i}", [128, 8 * 128], BF16) for i in range(RD)]
        kct = [A(f"kct{i}", [128, D], BF16) for i in range(RD)]
        vtm = [A(f"vtm{i}", [128, D], BF16) for i in range(RD)]
        ogt = [A(f"ogt{i}", [128, D], BF16) for i in range(RD)]
        otm = [A(f"otm{i}", [128, D], BF16) for i in range(RD)]
        pml = [A(f"pml{i}", [128, 24], F32) for i in range(RD)]
        xln2 = A("xln2", [128, D], BF16)
        fmr2 = A("fmr2", [128, 2 * D], BF16)
        emf = fmr2
        yfm = xln2
        qs_tm2 = [A(f"qs{i}", [128, D], BF16) for i in range(2)]
        kr_tm2 = [A(f"kr{i}", [128, D], BF16) for i in range(2)]
        sp_ = A("sp", [128, D], F32)
        cs_ = A("cs", [128, D], F32)
        mid_ = A("mid", [1, D], F32)
        gam2 = [A(f"gam{i}", [128, 16], F32) for i in range(2)]
        t12 = A("t12", [128, 128], BF16)
        t12f = A("t12f", [128, 128], BF16)
        o12 = A("o12", [128, 128], BF16)
        o12f = A("o12f", [128, 128], BF16)

        rk2 = A("rk2", [128, 16], F32)
        st4 = b.STATS[:, 640:680]

        with contextlib.ExitStack() as pes:
            P = lambda n, s, dt=F32: pes.enter_context(
                nc.psum_tensor(tag + n, s, dt))
            pPJ = P("pPJ", [128, 1024])
            pCS = P("pCS", [128, 512])
            pA = [P("pA0", [128, 128]), P("pA1", [128, 128])]
            pBs = P("pB0", [128, 128], BF16)
            pB = [pBs, pBs]
            pU = [P("pU0", [128, 64]), P("pU1", [128, 64])]

            def fmc(sub, c):
                return fmr2[:, sub * D + c * 128: sub * D + (c + 1) * 128]

            def pre(pair):
                for sub in range(2):
                    ti = 2 * pair + sub
                    rd = ti % RD

                    @b.blk.vector
                    def _(v, ti=ti, sub=sub):
                        v.wait_ge(b.s_mm.h, b.s_mm.n)
                        if sub == 1:
                            v.wait_ge(b.s_tr.h, b.s_tr.n)
                        v.tensor_scalar(xln2[:, :],
                                        b.xres_tile(ti), r_[:, ti:ti + 1],
                                        nmr[:, ti:ti + 1], ALU.mult, ALU.add)
                        v.drain().then_inc(b.s_v.bump(1), 1)
                    v_ln = b.s_v.n

                    @b.blk.sync
                    def _(sy, sub=sub, v_ln=v_ln):
                        sy.wait_ge(b.s_v.h, v_ln)
                        for c in range(NCH):
                            sy.dma_start_transpose(
                                fmc(sub, c),
                                xln2[:, c * 128:(c + 1) * 128],
                            ).then_inc(b.s_tr.bump(16), 16)
                        sy.wait_ge(b.s_tr.h, b.s_tr.n)
                    tr_f = b.s_tr.n

                    @b.blk.tensor
                    def _(t, ti=ti, sub=sub, tr_f=tr_f):
                        if ti == 0:
                            t.wait_ge(b.s_lg.h, ld_w)
                            t.wait_ge(b.s_v.h, b.s_v.n)
                        t.wait_ge(b.s_tr.h, tr_f)
                        t.wait_ge(b.s_a.h, b.s_a.n)
                        t.wait_ge(b.s_v.h, b.s_v.n)
                        for half in range(2):
                            for c in range(NCH):
                                mmi = t.matmul(pPJ[:, half * 512:(half + 1) * 512],
                                               fmc(sub, c),
                                               wq[:, c * D + half * 512:
                                                  c * D + (half + 1) * 512],
                                               start=(c == 0), stop=(c == NCH - 1))
                        mmi.then_inc(b.s_mm.bump(1), 1)
                    mm_q = b.s_mm.n

                    @b.blk.scalar
                    def _(s, sub=sub, mm_q=mm_q):
                        s.wait_ge(b.s_mm.h, mm_q)
                        s.activation(qs_tm2[sub][:, :], pPJ[:, :],
                                     AF.Silu).then_inc(b.s_a.bump(1), 1)
                    a_q = b.s_a.n

                    @b.blk.tensor
                    def _(t, sub=sub, a_q=a_q):
                        t.wait_ge(b.s_a.h, a_q)
                        for half in range(2):
                            for c in range(NCH):
                                t.matmul(pPJ[:, half * 512:(half + 1) * 512],
                                         fmc(sub, c),
                                         wk[:, c * D + half * 512:
                                            c * D + (half + 1) * 512],
                                         start=(c == 0), stop=(c == NCH - 1))
                        for c in range(NCH):
                            t.matmul(pU[0][:, 0:64], fmc(sub, c),
                                     wf1[:, c * 64:(c + 1) * 64],
                                     start=(c == 0), stop=(c == NCH - 1))
                        for c in range(NCH):
                            t.matmul(pU[1][:, 0:64], fmc(sub, c),
                                     wog1[:, c * 64:(c + 1) * 64],
                                     start=(c == 0), stop=(c == NCH - 1))
                        for c in range(NCH):
                            mmi = t.matmul(pA[0][:, 0:16], fmc(sub, c),
                                           wgm[:, c * 16:(c + 1) * 16],
                                           start=(c == 0), stop=(c == NCH - 1))
                        mmi.then_inc(b.s_mm.bump(1), 1)
                    mm_p1 = b.s_mm.n

                    @b.blk.scalar
                    def _(s, sub=sub, mm_p1=mm_p1):
                        s.wait_ge(b.s_mm.h, mm_p1)
                        s.activation(kr_tm2[sub][:, :], pPJ[:, :], AF.Silu)
                        s.drain()
                        s.activation(gam2[sub][:, :], pA[0][:, 0:16], AF.Sigmoid)
                        s.drain().then_inc(b.s_a.bump(1), 1)
                    a_p1 = b.s_a.n

                    @b.blk.vector
                    def _(v, ti=ti, sub=sub, rd=rd, a_p1=a_p1, mm_p1=mm_p1):
                        v.wait_ge(b.s_a.h, a_p1)
                        v.tensor_copy(t12[:, sub * 64:(sub + 1) * 64],
                                      pU[0][:, 0:64])
                        v.drain()
                        v.tensor_copy(o12[:, sub * 64:(sub + 1) * 64],
                                      pU[1][:, 0:64])
                        v.drain()
                        v.tensor_scalar_mul(gam2[sub][:, :], gam2[sub][:, :], -1.0)
                        v.drain()
                        ksq2 = otm[rd]
                        v.tensor_mul(ksq2[:, :], kr_tm2[sub][:, :], kr_tm2[sub][:, :])
                        v.drain()
                        kv3 = bass.AP(ksq2, 0, [[ksq2[:, :].ap[0][0], 128],
                                                [64, 16], [1, 64]])
                        v.tensor_reduce(rk2[:, :], kv3, axis=AXX, op=ALU.add)
                        v.drain().then_inc(b.s_v.bump(1), 1)

                    @b.blk.scalar
                    def _(s):
                        s.wait_ge(b.s_v.h, b.s_v.n)
                        s.activation(rk2[:, :], rk2[:, :], AF.Sqrt,
                                     bias=b.ceps[:, 1:2])
                        s.drain().then_inc(b.s_a.bump(1), 1)

                    @b.blk.vector
                    def _(v, rd=rd, sub=sub):
                        v.wait_ge(b.s_a.h, b.s_a.n)
                        v.reciprocal(rk2[:, :], rk2[:, :])
                        v.drain()
                        rb = bass.AP(rk2, 0, [[rk2[:, :].ap[0][0], 128],
                                              [1, 16], [0, 64]])
                        kv4 = bass.AP(kr_tm2[sub], 0,
                                      [[kr_tm2[sub][:, :].ap[0][0], 128],
                                       [64, 16], [1, 64]])
                        v.tensor_tensor(kv4, kv4, rb, ALU.mult)
                        v.drain()
                        v.tensor_copy(kct[rd][:, :], kr_tm2[sub][:, :])
                        v.drain().then_inc(b.s_v.bump(1), 1)
                    v_krdy = b.s_v.n

                    @b.blk.tensor
                    def _(t, sub=sub, v_krdy=v_krdy):
                        t.wait_ge(b.s_v.h, v_krdy)
                        for half in range(2):
                            for c in range(NCH):
                                mmi = t.matmul(pPJ[:, half * 512:(half + 1) * 512],
                                               fmc(sub, c),
                                               wv[:, c * D + half * 512:
                                                  c * D + (half + 1) * 512],
                                               start=(c == 0), stop=(c == NCH - 1))
                        mmi.then_inc(b.s_mm.bump(1), 1)
                    mm_v = b.s_mm.n

                    @b.blk.vector
                    def _(v, rd=rd, mm_v=mm_v):
                        v.wait_ge(b.s_mm.h, mm_v)
                        v.tensor_copy(vtm[rd][:, :], pPJ[:, :])
                        v.drain().then_inc(b.s_v.bump(1), 1)
                    v_v = b.s_v.n

                @b.blk.sync
                def _(sy):
                    sy.wait_ge(b.s_v.h, b.s_v.n)
                    sy.dma_start_transpose(t12f[:, :], t12[:, :]
                                           ).then_inc(b.s_tr.bump(16), 16)
                    sy.dma_start_transpose(o12f[:, :], o12[:, :]
                                           ).then_inc(b.s_tr.bump(16), 16)
                    sy.wait_ge(b.s_tr.h, b.s_tr.n)
                tr_t12 = b.s_tr.n

                for sub in range(2):
                    ti = 2 * pair + sub
                    rd = ti % RD
                    bs = sub * 64

                    @b.blk.tensor
                    def _(t, sub=sub, bs=bs, tr_t12=tr_t12):
                        t.wait_ge(b.s_tr.h, tr_t12)
                        t.wait_ge(b.s_v.h, b.s_v.n)
                        t.wait_ge(b.s_a.h, b.s_a.n)
                        for half in range(2):
                            mmi = t.matmul(pPJ[:, half * 512:(half + 1) * 512],
                                           t12f[bs:bs + 64, :],
                                           wf2[bs:bs + 64, half * 512:
                                               (half + 1) * 512],
                                           start=True, stop=True)
                        mmi.then_inc(b.s_mm.bump(1), 1)
                    mm_z = b.s_mm.n

                    @b.blk.scalar
                    def _(s, mm_z=mm_z):
                        s.wait_ge(b.s_mm.h, mm_z)
                        s.activation(sp_[:, :], pPJ[:, :], AF.Exp, scale=-1.0)
                        s.drain()
                        s.activation(sp_[:, :], sp_[:, :], AF.Ln, bias=1.0)
                        s.drain().then_inc(b.s_a.bump(1), 1)
                    a_sp = b.s_a.n

                    v_cs = None
                    for half in range(2):
                        @b.blk.tensor
                        def _(t, half=half, a_sp=a_sp):
                            t.wait_ge(b.s_a.h, a_sp)
                            if half == 1:
                                t.wait_ge(b.s_v.h, b.s_v.n)
                            t.matmul(pCS[:, :], ctu[:, :],
                                     sp_[:, half * 512:(half + 1) * 512],
                                     start=True, stop=True
                                     ).then_inc(b.s_mm.bump(1), 1)
                        mm_csh = b.s_mm.n

                        @b.blk.vector
                        def _(v, half=half, mm_csh=mm_csh):
                            v.wait_ge(b.s_mm.h, mm_csh)
                            v.tensor_copy(cs_[:, half * 512:(half + 1) * 512],
                                          pCS[:, :])
                            v.drain().then_inc(b.s_v.bump(1), 1)
                    v_cs = b.s_v.n

                    @b.blk.scalar
                    def _(s, rd=rd, v_cs=v_cs):
                        s.wait_ge(b.s_v.h, v_cs)
                        s.activation(ogt[rd][:, :], cs_[:, :], AF.Exp, scale=-1.0)
                        s.drain().then_inc(b.s_a.bump(1), 1)
                    a_emm = b.s_a.n

                    @b.blk.gpsimd
                    def _(g, v_cs=v_cs):
                        g.wait_ge(b.s_v.h, v_cs)
                        g.dma_start(out=mid_[0:1, :], in_=cs_[63:64, :]
                                    ).then_inc(b.s_ea.bump(16), 16)
                    g_mid = b.s_ea.n

                    mmh = []
                    for half in range(2):
                        @b.blk.tensor
                        def _(t, half=half, g_mid=g_mid):
                            t.wait_ge(b.s_ea.h, g_mid)
                            if half == 1:
                                t.wait_ge(b.s_v.h, b.s_v.n)
                            t.matmul(pCS[:, :], b.onesf[0:1, :],
                                     mid_[0:1, half * 512:(half + 1) * 512],
                                     start=True, stop=True
                                     ).then_inc(b.s_mm.bump(1), 1)
                        mmh.append(b.s_mm.n)

                        @b.blk.vector
                        def _(v, half=half, a_emm=a_emm, mmh=tuple(mmh)):
                            v.wait_ge(b.s_mm.h, mmh[-1])
                            if half == 0:
                                v.wait_ge(b.s_a.h, a_emm)
                            v.tensor_sub(cs_[:, half * 512:(half + 1) * 512],
                                         cs_[:, half * 512:(half + 1) * 512],
                                         pCS[:, :])
                            v.drain().then_inc(b.s_v.bump(1), 1)
                    mm_mid = b.s_mm.n

                    v_sc = b.s_v.n

                    @b.blk.scalar
                    def _(s, v_sc=v_sc):
                        s.wait_ge(b.s_v.h, v_sc)
                        s.activation(pPJ[:, :], cs_[:, :], AF.Exp,
                                     scale=-1.0).then_inc(b.s_a.bump(1), 1)
                    a_lam = b.s_a.n

                    @b.blk.vector
                    def _(v, rd=rd, sub=sub, a_lam=a_lam):
                        v.wait_ge(b.s_a.h, a_lam)
                        v.tensor_mul(qs_tm2[sub][:, :], qs_tm2[sub][:, :],
                                     pPJ[:, :])
                        v.drain().then_inc(b.s_v.bump(1), 1)
                    v_qt = b.s_v.n

                    @b.blk.scalar
                    def _(s, v_qt=v_qt):
                        s.wait_ge(b.s_v.h, v_qt)
                        s.activation(pPJ[:, :], cs_[:, :],
                                     AF.Exp).then_inc(b.s_a.bump(1), 1)
                    a_en = b.s_a.n

                    @b.blk.vector
                    def _(v, rd=rd, a_en=a_en):
                        v.wait_ge(b.s_a.h, a_en)
                        v.tensor_mul(kct[rd][:, :], kct[rd][:, :], pPJ[:, :])
                        v.drain()
                        v.tensor_add(sp_[:, :], sp_[:, :], cs_[:, :])
                        v.drain().then_inc(b.s_v.bump(1), 1)
                    v_kap = b.s_v.n

                    @b.blk.scalar
                    def _(s, rd=rd, v_kap=v_kap):
                        s.wait_ge(b.s_v.h, v_kap)
                        s.activation(ogt[rd][96:128, :], cs_[96:128, :],
                                     AF.Exp, scale=-1.0)
                        s.drain()
                        s.activation(pPJ[:, :], sp_[:, :], AF.Exp,
                                     scale=-1.0).then_inc(b.s_a.bump(1), 1)
                    a_e1 = b.s_a.n

                    @b.blk.vector
                    def _(v, rd=rd, sub=sub, a_e1=a_e1):
                        v.wait_ge(b.s_a.h, a_e1)
                        v.tensor_mul(kr_tm2[sub][:, :], kr_tm2[sub][:, :],
                                     pPJ[:, :])
                        v.drain()
                        gb = bass.AP(gam2[sub], 0, [[gam2[sub][:, :].ap[0][0], 128],
                                                    [1, 16], [0, 64]])
                        kv5 = bass.AP(kr_tm2[sub], 0,
                                      [[kr_tm2[sub][:, :].ap[0][0], 128],
                                       [64, 16], [1, 64]])
                        v.tensor_tensor(kv5, kv5, gb, ALU.mult)
                        v.drain().then_inc(b.s_v.bump(1), 1)
                    v_drv = b.s_v.n

                    @b.blk.sync
                    def _(sy, rd=rd, sub=sub, v_drv=v_drv):
                        sy.wait_ge(b.s_v.h, v_drv)
                        for c in range(NCH):
                            sy.dma_start_transpose(
                                qf[rd][:, c * 128:(c + 1) * 128],
                                qs_tm2[sub][:, c * 128:(c + 1) * 128],
                            ).then_inc(b.s_tr.bump(16), 16)
                        for c in range(NCH):
                            sy.dma_start_transpose(
                                kcf[rd][:, c * 128:(c + 1) * 128],
                                kct[rd][:, c * 128:(c + 1) * 128],
                            ).then_inc(b.s_tr.bump(16), 16)
                        for c in range(NCH):
                            sy.dma_start_transpose(
                                bhf[rd][:, c * 128:(c + 1) * 128],
                                kr_tm2[sub][:, c * 128:(c + 1) * 128],
                            ).then_inc(b.s_tr.bump(16), 16)
                        for c in range(NCH):
                            sy.dma_start_transpose(
                                emf[:, c * 128:(c + 1) * 128],
                                ogt[rd][:, c * 128:(c + 1) * 128],
                            ).then_inc(b.s_tr.bump(16), 16)
                        sy.wait_ge(b.s_tr.h, b.s_tr.n)
                    tr_drv = b.s_tr.n

                    @b.blk.vector
                    def _(v, rd=rd, tr_drv=tr_drv):
                        v.wait_ge(b.s_tr.h, tr_drv)
                        p63 = bass.AP(emf, 63, [[emf[:, :].ap[0][0], 128], [128, 8]])
                        p127 = bass.AP(emf, 127, [[emf[:, :].ap[0][0], 128], [128, 8]])
                        v.tensor_copy(pml[rd][:, 0:8], p63)
                        v.drain()
                        v.tensor_copy(pml[rd][:, 8:16], p127)
                        v.drain().then_inc(b.s_v.bump(1), 1)

                    @b.blk.tensor
                    def _(t, sub=sub, bs=bs):
                        t.wait_ge(b.s_v.h, b.s_v.n)
                        for half in range(2):
                            mmi = t.matmul(pPJ[:, half * 512:(half + 1) * 512],
                                           o12f[bs:bs + 64, :],
                                           wog2[bs:bs + 64, half * 512:
                                                (half + 1) * 512],
                                           start=True, stop=True)
                        mmi.then_inc(b.s_mm.bump(1), 1)
                    mm_og = b.s_mm.n

                    @b.blk.scalar
                    def _(s, rd=rd, mm_og=mm_og):
                        s.wait_ge(b.s_mm.h, mm_og)
                        s.activation(ogt[rd][:, :], pPJ[:, :], AF.Sigmoid)
                        s.drain().then_inc(b.s_a.bump(1), 1)

            def unit_pair(ti, h0):
                rd = ti % RD
                hs = (h0, h0 + 1)
                mk = {h: {} for h in hs}

                def sl_bh(h):
                    return (h % 2) * 64

                def sl_S(h, t_):
                    bh = sl_bh(h)
                    return t_[bh:bh + 64, (h // 2) * 64:(h // 2) * 64 + 64]

                def sl_fm(h, t_):
                    bh = sl_bh(h)
                    return t_[bh:bh + 64, (h // 2) * 128:(h // 2) * 128 + 128]

                for h in hs:
                    i = h - h0

                    @b.blk.vector
                    def _(v, h=h, i=i):
                        v.wait_ge(b.s_v.h, b.s_v.n)
                        pm = pml[rd][sl_bh(h):sl_bh(h) + 64, h // 2:h // 2 + 1]
                        v.tensor_scalar(sl_S(h, S0m), sl_S(h, S_sb), pm, None,
                                        ALU.mult)
                        v.drain().then_inc(b.s_v.bump(1), 1)
                    mk[h]["s0m"] = b.s_v.n

                for h in hs:
                    i = h - h0

                    @b.blk.tensor
                    def _(t, h=h, i=i):
                        t.wait_ge(b.s_v.h, mk[h]["s0m"])
                        t.matmul(pU[i][:, :], sl_fm(h, bhf[rd]), sl_S(h, S0m),
                                 start=True, stop=True).then_inc(b.s_mm.bump(1), 1)
                        bf_ = sl_fm(h, bhf[rd])
                        kf_ = sl_fm(h, kcf[rd])
                        t.matmul(pA[i][:, 0:64], bf_, kf_[:, 0:64],
                                 start=True, stop=True).then_inc(b.s_mm.bump(1), 1)
                        t.matmul(pA[i][64:128, 64:128], bf_[:, 64:128],
                                 kf_[:, 64:128], start=True, stop=True,
                                 tile_position=(sl_bh(h), 64)
                                 ).then_inc(b.s_mm.bump(1), 1)
                    mk[h]["mmu1"] = b.s_mm.n - 2
                    mk[h]["mmL"] = b.s_mm.n

                for h in hs:
                    i = h - h0

                    @b.blk.vector
                    def _(v, h=h, i=i):
                        v.wait_ge(b.s_mm.h, mk[h]["mmu1"])
                        v.tensor_add(usb[i][:, :], vtm[rd][:, h * 64:(h + 1) * 64],
                                     pU[i][:, :])
                        v.drain()
                        v.wait_ge(b.s_mm.h, mk[h]["mmL"])
                        v.select(Lr[i][:, 0:64], mskL[:, 0:64], pA[i][:, 0:64],
                                 zeros[:, 0:64])
                        v.select(Lr[i][64:128, 64:128], mskL[64:128, 64:128],
                                 pA[i][64:128, 64:128], zeros[64:128, 64:128])
                        v.memset(Lr[i][0:64, 64:128], 0.0)
                        v.drain().then_inc(b.s_v.bump(1), 1)
                    mk[h]["vL"] = b.s_v.n

                for h in hs:
                    i = h - h0
                    aprev = b.s_a.n

                    @b.blk.tensor
                    def _(t, h=h, i=i, aprev=aprev):
                        t.wait_ge(b.s_v.h, mk[h]["vL"])
                        t.wait_ge(b.s_a.h, aprev)
                        t.transpose(pB[i][:, :], Lr[i][:, :], b.ident[:, :]
                                    ).then_inc(b.s_mm.bump(1), 1)
                    mk[h]["mmT0"] = b.s_mm.n

                    @b.blk.scalar
                    def _(s, h=h, i=i):
                        s.wait_ge(b.s_mm.h, mk[h]["mmT0"])
                        s.activation(powT[i][:, 0:128], pB[i][:, :], AF.Copy)
                        s.drain().then_inc(b.s_a.bump(1), 1)
                    mk[h]["aT0"] = b.s_a.n

                for lev in range(1, 7):
                    for h in hs:
                        i = h - h0

                        @b.blk.tensor
                        def _(t, h=h, i=i, lev=lev):
                            if lev == 1:
                                t.wait_ge(b.s_a.h, mk[h]["aT0"])
                            else:
                                t.wait_ge(b.s_a.h, mk[h][f"aT{lev-1}"])
                                t.wait_ge(b.s_v.h, mk[h][f"vSq{lev-1}"])
                            t.matmul(pA[i][:, :],
                                     powT[i][:, (lev - 1) * 128: lev * 128],
                                     Lr[i][:, :], start=True, stop=True
                                     ).then_inc(b.s_mm.bump(1), 1)
                        mk[h][f"mmSq{lev}"] = b.s_mm.n
                    for h in hs:
                        i = h - h0

                        @b.blk.vector
                        def _(v, h=h, i=i, lev=lev):
                            v.wait_ge(b.s_mm.h, mk[h][f"mmSq{lev}"])
                            v.tensor_copy(Lr[i][:, :], pA[i][:, :])
                            v.drain().then_inc(b.s_v.bump(1), 1)
                        mk[h][f"vSq{lev}"] = b.s_v.n
                    for h in hs:
                        i = h - h0
                        aprev2 = b.s_a.n

                        @b.blk.tensor
                        def _(t, h=h, i=i, lev=lev, aprev2=aprev2):
                            t.wait_ge(b.s_v.h, mk[h][f"vSq{lev}"])
                            t.wait_ge(b.s_a.h, aprev2)
                            t.transpose(pB[i][:, :], Lr[i][:, :], b.ident[:, :]
                                        ).then_inc(b.s_mm.bump(1), 1)
                        mk[h][f"mmTr{lev}"] = b.s_mm.n

                        @b.blk.scalar
                        def _(s, h=h, i=i, lev=lev):
                            s.wait_ge(b.s_mm.h, mk[h][f"mmTr{lev}"])
                            s.activation(powT[i][:, lev * 128:(lev + 1) * 128],
                                         pB[i][:, :], AF.Copy)
                            s.drain().then_inc(b.s_a.bump(1), 1)
                        mk[h][f"aT{lev}"] = b.s_a.n

                for j in range(6, -1, -1):
                    for h in hs:
                        i = h - h0

                        @b.blk.tensor
                        def _(t, h=h, i=i, j=j):
                            if j == 6:
                                t.wait_ge(b.s_a.h, mk[h]["aT6"])
                                t.wait_ge(b.s_v.h, mk[h]["vL"])
                            else:
                                t.wait_ge(b.s_v.h, mk[h][f"vAp{j+1}"])
                            t.matmul(pU[i][:, :], powT[i][:, j * 128:(j + 1) * 128],
                                     usb[i][:, :], start=True, stop=True
                                     ).then_inc(b.s_mm.bump(1), 1)
                        mk[h][f"mmAp{j}"] = b.s_mm.n
                    for h in hs:
                        i = h - h0

                        @b.blk.vector
                        def _(v, h=h, i=i, j=j):
                            v.wait_ge(b.s_mm.h, mk[h][f"mmAp{j}"])
                            v.tensor_add(usb[i][:, :], usb[i][:, :], pU[i][:, :])
                            v.drain().then_inc(b.s_v.bump(1), 1)
                        mk[h][f"vAp{j}"] = b.s_v.n

                for h in hs:
                    i = h - h0

                    @b.blk.tensor
                    def _(t, h=h, i=i):
                        t.wait_ge(b.s_v.h, mk[h]["vAp0"])
                        kf_ = sl_fm(h, kcf[rd])
                        qf_ = sl_fm(h, qf[rd])
                        t.matmul(pA[i][0:64, :], kf_[:, 0:64], qf_,
                                 start=True, stop=True,
                                 ).then_inc(b.s_mm.bump(1), 1)
                        t.matmul(pA[i][64:128, 64:128], kf_[:, 64:128],
                                 qf_[:, 64:128], start=True, stop=True,
                                 tile_position=(sl_bh(h), 64)
                                 ).then_inc(b.s_mm.bump(1), 1)
                    mk[h]["mmPT"] = b.s_mm.n
                for h in hs:
                    i = h - h0

                    @b.blk.vector
                    def _(v, h=h, i=i):
                        v.wait_ge(b.s_mm.h, mk[h]["mmPT"])
                        v.select(PTs[i][0:64, :], mskU[0:64, :], pA[i][0:64, :],
                                 zeros[0:64, :])
                        v.select(PTs[i][64:128, 64:128], mskU[64:128, 64:128],
                                 pA[i][64:128, 64:128], zeros[64:128, 64:128])
                        v.memset(PTs[i][64:128, 0:64], 0.0)
                        v.drain().then_inc(b.s_v.bump(1), 1)
                    mk[h]["vPT"] = b.s_v.n
                for h in hs:
                    i = h - h0

                    @b.blk.tensor
                    def _(t, h=h, i=i):
                        t.wait_ge(b.s_v.h, mk[h]["vPT"])
                        t.matmul(pA[i][:, 0:64], PTs[i][:, :], usb[i][:, :],
                                 start=True, stop=True).then_inc(b.s_mm.bump(1), 1)
                        t.matmul(pU[i][:, :], sl_fm(h, qf[rd]), sl_S(h, S0m),
                                 start=True, stop=True).then_inc(b.s_mm.bump(1), 1)
                    mk[h]["mmO"] = b.s_mm.n
                for h in hs:
                    i = h - h0

                    @b.blk.vector
                    def _(v, h=h, i=i):
                        v.wait_ge(b.s_mm.h, mk[h]["mmO"])
                        osl = otm[rd][:, h * 64:(h + 1) * 64]
                        v.tensor_copy(osl, pA[i][:, 0:64])
                        v.drain()
                        v.tensor_add(osl, osl, pU[i][:, :])
                        v.drain().then_inc(b.s_v.bump(1), 1)
                    mk[h]["vO"] = b.s_v.n
                for h in hs:
                    i = h - h0
                    bh = sl_bh(h)

                    @b.blk.tensor
                    def _(t, h=h, i=i, bh=bh):
                        t.wait_ge(b.s_v.h, mk[h]["vO"])
                        t.matmul(pU[i][bh:bh + 64, 0:64],
                                 kct[rd][:, h * 64:(h + 1) * 64], usb[i][:, :],
                                 start=True, stop=True,
                                 tile_position=(0, bh) if bh else None
                                 ).then_inc(b.s_mm.bump(1), 1)
                    mk[h]["mmG"] = b.s_mm.n
                for h in hs:
                    i = h - h0
                    bh = sl_bh(h)

                    @b.blk.vector
                    def _(v, h=h, i=i, bh=bh):
                        v.wait_ge(b.s_mm.h, mk[h]["mmG"])
                        v.tensor_add(sl_S(h, S0m), sl_S(h, S0m),
                                     pU[i][bh:bh + 64, 0:64])
                        v.drain()
                        lc = pml[rd][bh:bh + 64, 8 + h // 2:9 + h // 2]
                        v.tensor_scalar(sl_S(h, S_sb), sl_S(h, S0m), lc, None,
                                        ALU.mult)
                        v.drain().then_inc(b.s_v.bump(1), 1)

            def post(ti):
                rd = ti % RD
                bnB = b.STATS[:, 640:664]
                mvB = b.STATS[:, 664:666]
                rB = b.STATS[:, 666:667]
                nmB = b.STATS[:, 667:668]
                varB = bass.AP(b.STATS, 665, [[768, 128], [1, 1]])
                muB = bass.AP(b.STATS, 664, [[768, 128], [1, 1]])

                @b.blk.vector
                def _(v):
                    v.wait_ge(b.s_v.h, b.s_v.n)
                    v.wait_ge(b.s_a.h, b.s_a.n)
                    v.tensor_mul(otm[rd][:, :], otm[rd][:, :], ogt[rd][:, :])
                    v.drain()
                    for hh in range(2):
                        v.bn_stats(bnB[:, hh * 6:hh * 6 + 6],
                                   otm[rd][:, hh * 512:(hh + 1) * 512])
                    v.drain()
                    v.bn_aggr(mvB[:, :], bnB[:, 0:12])
                    v.drain().then_inc(b.s_v.bump(1), 1)

                @b.blk.scalar
                def _(s):
                    s.wait_ge(b.s_v.h, b.s_v.n)
                    s.activation(rB[:, :], varB, AF.Sqrt, bias=b.ceps[:, 0:1])
                    s.drain().then_inc(b.s_a.bump(1), 1)

                @b.blk.vector
                def _(v):
                    v.wait_ge(b.s_a.h, b.s_a.n)
                    v.reciprocal(rB[:, :], rB[:, :])
                    v.drain()
                    v.scalar_tensor_tensor(nmB[:, :], muB, -1.0, rB[:, :],
                                           ALU.mult, ALU.mult)
                    v.drain()
                    v.tensor_scalar(otm[rd][:, :], otm[rd][:, :], rB[:, 0:1],
                                    nmB[:, 0:1], ALU.mult, ALU.add)
                    v.drain()
                    v.tensor_mul(otm[rd][:, :], otm[rd][:, :], b.gain1[:, :])
                    v.drain().then_inc(b.s_v.bump(1), 1)
                v_y = b.s_v.n

                @b.blk.sync
                def _(sy, v_y=v_y):
                    sy.wait_ge(b.s_v.h, v_y)
                    for c in range(NCH):
                        sy.dma_start_transpose(
                            yfm[:, c * 128:(c + 1) * 128],
                            otm[rd][:, c * 128:(c + 1) * 128],
                        ).then_inc(b.s_tr.bump(16), 16)
                    sy.wait_ge(b.s_tr.h, b.s_tr.n)
                tr_y = b.s_tr.n

                @b.blk.tensor
                def _(t, tr_y=tr_y):
                    t.wait_ge(b.s_tr.h, tr_y)
                    t.wait_ge(b.s_v.h, b.s_v.n)
                    for half in range(2):
                        for c in range(NCH):
                            mmi = t.matmul(pPJ[:, half * 512:(half + 1) * 512],
                                           yfm[:, c * 128:(c + 1) * 128],
                                           wo[:, c * D + half * 512:
                                              c * D + (half + 1) * 512],
                                           start=(c == 0), stop=(c == NCH - 1))
                        mmi.then_inc(b.s_mm.bump(1), 1)
                mm_wo = b.s_mm.n

                @b.blk.vector
                def _(v, ti=ti, mm_wo=mm_wo):
                    v.wait_ge(b.s_mm.h, mm_wo)
                    for half in range(2):
                        xs = b.XRES[:, ti * D + half * 512: ti * D + (half + 1) * 512]
                        v.tensor_add(xs, xs, pPJ[:, half * 512:(half + 1) * 512])
                        v.drain()
                    v.drain().then_inc(b.s_v.bump(1), 1)

            for pair in range(NT // 2):
                pre(pair)
                for sub in range(2):
                    ti = 2 * pair + sub
                    for h0 in range(0, 16, 2):
                        unit_pair(ti, h0)
                    if DBG_L1A:
                        @b.blk.gpsimd
                        def _(g, ti=ti):
                            b.barrier_all(g)
                            g.dma_start(
                                out=bass.AP(b.e_tm, ti * 128 * NSLOT,
                                            [[NSLOT, 128], [1, D]]),
                                in_=otm[ti % RD][:, :]
                            ).then_inc(b.s_st.bump(16), 16)
                            g.dma_start(
                                out=bass.AP(b.e_sm, ti * 128 * NSLOT,
                                            [[NSLOT, 128], [1, D]]),
                                in_=ogt[ti % RD][:, :]
                            ).then_inc(b.s_st.bump(16), 16)
                            g.wait_ge(b.s_st.h, b.s_st.n)

                        @b.blk.vector
                        def _(v):
                            v.wait_ge(b.s_st.h, b.s_st.n)
                    post(ti)


SUBNAMES = ["l0ffn_a", "l0attn", "l0ffn_b", "l1ffn_a", "l1attn", "l1ffn_b"]


def build_nc(first=0, last=5):
    nc = bass.Bass("TRN2", target_bir_lowering=False,
                   detect_race_conditions=False)
    b = Builder(nc)
    b.declare_io()
    with contextlib.ExitStack() as es:
        b.alloc_persistent(es)
        with nc.Block() as blk:
            b.blk = blk
            b.load_initial()
            for k in range(first, last + 1):
                name = SUBNAMES[k]
                if name in ("l0ffn_a", "l0ffn_b"):
                    b.emit_l0ffn(tag=f"l0f{k}_")
                elif name == "l0attn":
                    b.emit_l0attn()
                elif name in ("l1ffn_a", "l1ffn_b"):
                    b.emit_l1ffn(tag=f"l1f{k}_", final=(k == 5))
                elif name == "l1attn":
                    b.emit_l1attn()
            b.store_out()
    return nc


def make_consts():
    mask64 = np.zeros((128, 16 * 64), np.float32)
    for c in range(16):
        for p in range(128):
            mask64[p, c * 64 + (4 * c + p // 32)] = 1.0
    maskp = np.zeros((128, PSL), np.float32)
    for p in range(128):
        maskp[p, p % PSL] = 1.0
    tri = np.tril(np.ones((128, 128), np.float32))
    ident = np.eye(128, dtype=np.float32)
    triu = np.triu(np.ones((128, 128), np.float32))
    return mask64, maskp, tri, ident, triu


def stage_inputs(inputs, core):
    i = inputs
    mask64, maskp, tri, ident, triu = make_consts()
    C = np.ascontiguousarray
    return {
        "x0": C(i["L0_attn_x"][core]),
        "gw": C(i["L1_ffn_gate_w"]),
        "gb": C(i["L1_ffn_gate_b"].reshape(1, NEXP1)),
        "l1ew": C(i["L1_ffn_expert_w"]),
        "l1eb": C(i["L1_ffn_expert_b"]),
        "phiwT": C(i["L0_ffn_phi_w"].T),
        "phib": C(i["L0_ffn_phi_b"].reshape(1, NSLOT)),
        "l0ew": C(i["L0_ffn_expert_w"][8 * core:8 * core + 8]),
        "l0eb": C(i["L0_ffn_expert_b"][8 * core:8 * core + 8]),
        "l0ng": C(i["L0_attn_norm_g"].reshape(1, D)),
        "l0og": C(i["L0_attn_out_norm_g"].reshape(1, D)),
        "wq0": C(i["L0_attn_Wq"]),
        "wkv0": C(i["L0_attn_Wkv"]),
        "wout0": C(i["L0_attn_Wout"]),
        "wq1": C(i["L1_attn_Wq"]),
        "wk1": C(i["L1_attn_Wk"]),
        "wv1": C(i["L1_attn_Wv"]),
        "wo1": C(i["L1_attn_Wo"]),
        "wgam": C(i["L1_attn_Wgamma"]),
        "wf1": C(i["L1_attn_Wf1"]),
        "wf2": C(i["L1_attn_Wf2"]),
        "wog1": C(i["L1_attn_Wog1"]),
        "wog2": C(i["L1_attn_Wog2"]),
        "l1ng": C(i["L1_attn_norm_w"].reshape(1, D)),
        "cmask64": mask64,
        "cmaskp": maskp,
        "ctri": tri,
        "cident": ident,
        "ctriu": triu,
        "cmskl": np.tril(np.ones((128, 128), np.uint8), -1),
        "cmsku": np.triu(np.ones((128, 128), np.uint8)),
    }


def kernel(**inputs):
    nc = build_nc(0, 5)
    in_maps = [stage_inputs(inputs, c) for c in range(NCORES)]
    res = run_bass_kernel_spmd(nc, in_maps, core_ids=list(range(NCORES)))
    return np.stack([res.results[c]["out"] for c in range(NCORES)], axis=0)


# version 06
# speedup vs baseline: 1.3574x; 1.3574x over previous
import contextlib
import numpy as np
import concourse.bass as bass
import concourse.mybir as mybir
from concourse.bass_utils import run_bass_kernel_spmd

F32 = mybir.dt.float32
BF16 = mybir.dt.bfloat16
AF = mybir.ActivationFunctionType
ALU = mybir.AluOpType
AXX = None

D = 1024
T = 4096
TT = 128
NT = T // TT
NCH = D // 128
NEXP1 = 8
NSLOT = 2048
PSL = 32
EPS = 1e-5
NCORES = 8
DBG_L1A = 0
DBG_UG = 99
DBG_PHASE = 99

AXX = mybir.AxisListType.X


class Sem:
    __slots__ = ("h", "n")

    def __init__(self, h):
        self.h = h
        self.n = 0

    def bump(self, k):
        self.n += k
        return self.h


class Builder:
    def __init__(self, nc):
        self.nc = nc
        self.blk = None
        self.es = None

    def declare_io(self):
        nc = self.nc
        din = lambda n, s: nc.dram_tensor(n, s, F32, kind="ExternalInput")
        self.x0 = din("x0", [T, D])
        self.out = nc.dram_tensor("out", [T, D], F32, kind="ExternalOutput")
        self.gw = din("gw", [D, NEXP1])
        self.gb = din("gb", [1, NEXP1])
        self.l1ew = din("l1ew", [NEXP1, D, D])
        self.l1eb = din("l1eb", [NEXP1, D])
        self.phiwT = din("phiwT", [D, NSLOT])
        self.phib = din("phib", [1, NSLOT])
        self.l0ew = din("l0ew", [8, D, D])
        self.l0eb = din("l0eb", [8, D])
        self.l0ng = din("l0ng", [1, D])
        self.l0og = din("l0og", [1, D])
        self.wq0 = din("wq0", [D, 512])
        self.wkv0 = din("wkv0", [D, 128])
        self.wout0 = din("wout0", [512, D])
        self.wq1 = din("wq1", [D, D])
        self.wk1 = din("wk1", [D, D])
        self.wv1 = din("wv1", [D, D])
        self.wo1 = din("wo1", [D, D])
        self.wgam = din("wgam", [D, 16])
        self.wf1 = din("wf1", [D, 64])
        self.wf2 = din("wf2", [64, D])
        self.wog1 = din("wog1", [D, 64])
        self.wog2 = din("wog2", [64, D])
        self.l1ng = din("l1ng", [1, D])
        self.cmask64 = din("cmask64", [128, 16 * 64])
        self.cmskl = nc.dram_tensor("cmskl", [128, 128], mybir.dt.uint8,
                                    kind="ExternalInput")
        self.cmsku = nc.dram_tensor("cmsku", [128, 128], mybir.dt.uint8,
                                    kind="ExternalInput")
        self.cmsku1 = nc.dram_tensor("cmsku1", [128, 128], mybir.dt.uint8,
                                     kind="ExternalInput")
        self.cmaskp = din("cmaskp", [128, PSL])
        self.ctri = din("ctri", [128, 128])
        self.cident = din("cident", [128, 128])
        self.ctriu = din("ctriu", [128, 128])
        ds = lambda n, s, dt=BF16: nc.dram_tensor(n, s, dt)
        self.e_tm = ds("e_tm", [NT, 128, NSLOT])
        self.e_sm = ds("e_sm", [NT, 128, NSLOT])
        self.xt_in = ds("xt_in", [NSLOT, D], F32)
        self.xt_rs = ds("xt_rs", [NSLOT // 8, D], F32)
        self.yt_in = ds("yt_in", [NSLOT // 8, D])
        self.yt_ag = ds("yt_ag", [NSLOT, D])

    def alloc_persistent(self, es):
        nc = self.nc
        A = lambda n, s, dt: es.enter_context(nc.sbuf_tensor(n, s, dt))
        self.XRES = A("XRES", [128, NT * D], BF16)
        self.STATS = A("STATS", [128, 768], F32)
        self.ones = A("ones", [1, 128], BF16)
        self.onesf = A("onesf", [1, 128], F32)
        self.gain1 = A("gain1", [128, D], BF16)
        self.ident = A("ident", [128, 128], BF16)
        self.ceps = A("ceps", [128, 4], F32)
        S = lambda n: Sem(es.enter_context(nc.semaphore(n)))
        self.s_li = S("s_li")
        self.s_lg = S("s_lg")
        self.s_wa = S("s_wa")
        self.s_wb = S("s_wb")
        self.s_tr = S("s_tr")
        self.s_sp = S("s_sp")
        self.s_mm = S("s_mm")
        self.s_v = S("s_v")
        self.s_a = S("s_a")
        self.s_cc = S("s_cc")
        self.s_st = S("s_st")
        self.s_g = S("s_g")
        self.s_ea = S("s_ea")
        self.s_eb = S("s_eb")
        self.s_sc = S("s_sc")

    def lg_dma(self, g, out, in_):
        g.wait_ge(self.s_lg.h, self.s_lg.n)
        g.dma_start(out=out, in_=in_).then_inc(self.s_lg.bump(16), 16)

    def barrier_all(self, eng, *, skip=()):
        for s in (self.s_li, self.s_lg, self.s_wa, self.s_wb, self.s_tr,
                  self.s_sp, self.s_mm, self.s_v, self.s_a, self.s_cc,
                  self.s_g, self.s_ea, self.s_eb, self.s_sc):
            if s not in skip:
                eng.wait_ge(s.h, s.n)

    def load_initial(self):
        b = self

        @b.blk.gpsimd
        def _(g):
            src = bass.AP(b.x0, 0, [[D, 128], [TT * D, NT], [1, D]])
            dst = bass.AP(b.XRES, 0, [[NT * D, 128], [D, NT], [1, D]])
            g.dma_start(out=dst, in_=src).then_inc(b.s_li.bump(16), 16)
            g.memset(b.ones[:, :], 1.0)
            g.memset(b.onesf[:, :], 1.0)
            bc = lambda t: bass.AP(t, 0, [[0, 128], [1, D]])
            g.dma_start(out=b.gain1[:, :], in_=bc(b.l1ng)).then_inc(b.s_li.bump(16), 16)
            g.dma_start(out=b.ident[:, :], in_=b.cident.ap()).then_inc(b.s_li.bump(16), 16)
            g.memset(b.ceps[:, 0:1], EPS)
            g.memset(b.ceps[:, 1:2], 1e-12)
            g.memset(b.ceps[:, 2:3], -16.0).then_inc(b.s_g.bump(1), 1)

    def store_out(self):
        b = self

        @b.blk.gpsimd
        def _(g):
            b.barrier_all(g)
            src = bass.AP(b.XRES, 0, [[NT * D, 128], [D, NT], [1, D]])
            dst = bass.AP(b.out, 0, [[D, 128], [TT * D, NT], [1, D]])
            g.dma_start(out=dst, in_=src).then_inc(b.s_st.bump(16), 16)
            g.wait_ge(b.s_st.h, b.s_st.n)

    def xres_tile(self, t):
        return self.XRES[:, t * D:(t + 1) * D]

    def emit_ln_stats(self):
        b = self
        r_ = b.STATS[:, 0:NT]
        nmr = b.STATS[:, NT:2 * NT]
        mv = b.STATS[:, 64:64 + 2 * NT]
        bn = b.STATS[:, 128:128 + 12 * NT]
        var_v = bass.AP(b.STATS, 64 + 1, [[768, 128], [2, NT]])
        mu_v = bass.AP(b.STATS, 64, [[768, 128], [2, NT]])

        @b.blk.gpsimd
        def _(g):
            b.barrier_all(g)

        @b.blk.sync
        def _(sy):
            b.barrier_all(sy)

        @b.blk.scalar
        def _(s):
            b.barrier_all(s, skip=(b.s_a,))

        @b.blk.tensor
        def _(t):
            b.barrier_all(t, skip=(b.s_mm,))

        @b.blk.vector
        def _(v):
            b.barrier_all(v, skip=(b.s_v,))
            for t in range(NT):
                xt = b.xres_tile(t)
                v.bn_stats(bn[:, 12 * t:12 * t + 6], xt[:, 0:512])
                v.bn_stats(bn[:, 12 * t + 6:12 * t + 12], xt[:, 512:1024])
            v.drain()
            for t in range(NT):
                v.bn_aggr(mv[:, 2 * t:2 * t + 2], bn[:, 12 * t:12 * t + 12])
            v.drain().then_inc(b.s_v.bump(1), 1)

        @b.blk.scalar
        def _(s):
            s.wait_ge(b.s_v.h, b.s_v.n)
            s.activation(r_[:, :], var_v, AF.Sqrt, bias=b.ceps[:, 0:1])
            s.drain().then_inc(b.s_a.bump(1), 1)

        @b.blk.vector
        def _(v):
            v.wait_ge(b.s_a.h, b.s_a.n)
            v.reciprocal(r_[:, :], r_[:, :])
            v.drain()
            v.scalar_tensor_tensor(nmr[:, :], mu_v, -1.0, r_[:, :],
                                   ALU.mult, ALU.mult)
            v.drain().then_inc(b.s_v.bump(1), 1)
        return r_, nmr

    def emit_xln_fm(self, dst, r_, nmr, gain=None, tiles=None, also_tm=None):
        b = self
        if tiles is None:
            tiles = list(range(NT))
        ntl = len(tiles)
        if also_tm is None:
            tmp = [b.es.enter_context(b.nc.sbuf_tensor(f"xln_tmp{i}_{b.s_v.n}", [128, D], BF16))
                   for i in range(2)]
            get_tm = lambda i: tmp[i % 2][:, :]
        else:
            get_tm = lambda i: also_tm[:, i * D:(i + 1) * D]
        v_marks = []
        tr_needed = []
        v0 = b.s_v.n
        tr0 = b.s_tr.n

        @b.blk.vector
        def _(v):
            for i, t in enumerate(tiles):
                if also_tm is None and i >= 2:
                    v.wait_ge(b.s_tr.h, tr0 + 16 * NCH * (i - 1))
                v.tensor_scalar(get_tm(i), b.xres_tile(t),
                                r_[:, t:t + 1], nmr[:, t:t + 1],
                                ALU.mult, ALU.add)
                if gain is not None:
                    v.drain()
                    v.tensor_mul(get_tm(i), get_tm(i), gain[:, :])
                v.drain().then_inc(b.s_v.bump(1), 1)
                v_marks.append(b.s_v.n)

        @b.blk.sync
        def _(sy):
            for i, t in enumerate(tiles):
                sy.wait_ge(b.s_v.h, v_marks[i])
                for c in range(NCH):
                    sy.dma_start_transpose(
                        dst[:, c * (ntl * 128) + i * 128:
                            c * (ntl * 128) + (i + 1) * 128],
                        get_tm(i)[:, c * 128:(c + 1) * 128],
                    ).then_inc(b.s_tr.bump(16), 16)
                sy.wait_ge(b.s_tr.h, b.s_tr.n)
        return tr0

    def emit_l1ffn(self, tag="l1f_", final=False):
        b = self
        nc = self.nc
        with contextlib.ExitStack() as es:
            old = b.es
            b.es = es
            r_, nmr = b.emit_ln_stats()
            if DBG_PHASE < 2:
                b.es = old
                return
            xfm = es.enter_context(nc.sbuf_tensor(tag + "xfm", [128, NCH * T], BF16))
            tr0 = b.emit_xln_fm(xfm, r_, nmr)
            if DBG_PHASE < 3:
                b.es = old
                return
            fmtile = lambda c, ti: xfm[:, c * T + ti * 128: c * T + (ti + 1) * 128]

            wsb = es.enter_context(nc.sbuf_tensor(tag + "w", [128, NT * NEXP1], F32))
            wtb = es.enter_context(nc.sbuf_tensor(tag + "wtb", [128, NT * NEXP1], BF16))
            wtT = es.enter_context(nc.sbuf_tensor(tag + "wtT", [NEXP1, T], BF16))
            gwsb = es.enter_context(nc.sbuf_tensor(tag + "gw", [128, NCH * NEXP1], BF16))
            gbsb = es.enter_context(nc.sbuf_tensor(tag + "gb", [1, NEXP1], BF16))
            ebsb = es.enter_context(nc.sbuf_tensor(tag + "eb", [NEXP1, D], BF16))
            gtmp = es.enter_context(nc.sbuf_tensor(tag + "gtmp", [128, NT * NEXP1], F32))
            gstat = es.enter_context(nc.sbuf_tensor(tag + "gs", [128, 2 * NT], F32))
            wbuf = [es.enter_context(nc.sbuf_tensor(f"{tag}wb{i}", [128, NCH * D], BF16))
                    for i in range(2)]
            with (nc.psum_tensor(tag + "pg", [128, 256], F32) as pg,
                  nc.psum_tensor(tag + "pt", [128, 128], BF16) as ptr,
                  nc.psum_tensor(tag + "p0", [128, 512], F32) as p0,
                  nc.psum_tensor(tag + "p1", [128, 512], F32) as p1,
                  nc.psum_tensor(tag + "p2", [128, 512], F32) as p2,
                  nc.psum_tensor(tag + "p3", [128, 512], F32) as p3):
                self._l1ffn_body(es, fmtile, tr0, wsb, wtb, wtT, gwsb, gbsb,
                                 ebsb, gtmp, gstat, wbuf, pg, ptr,
                                 [p0, p1, p2, p3], final)
            b.es = old

    def _l1ffn_body(self, es, fmtile, tr0, wsb, wtb, wtT, gwsb, gbsb, ebsb,
                    gtmp, gstat, wbuf, pg, ptr, psum, final):
        b = self
        nc = self.nc

        @b.blk.gpsimd
        def _(g):
            src = bass.AP(b.gw, 0, [[NEXP1, 128], [128 * NEXP1, NCH], [1, NEXP1]])
            b.lg_dma(g, gwsb[:, :], src)
            b.lg_dma(g, gbsb[:, :], b.gb.ap())
            b.lg_dma(g, ebsb[:, :], b.l1eb.ap())
        ld_gates = b.s_lg.n

        wsem = [b.s_wa, b.s_wb]

        def load_expert(g, e, buf):
            src = bass.AP(b.l1ew, e * D * D,
                          [[D, 128], [128 * D, NCH], [1, D]])
            g.dma_start(out=wbuf[buf][:, :], in_=src).then_inc(wsem[buf].bump(16), 16)
            return wsem[buf].n

        ld_marks = {}

        @b.blk.gpsimd
        def _(g):
            ld_marks[0] = load_expert(g, 0, 0)

        if DBG_PHASE == 5.3:
            @b.blk.gpsimd
            def _(g):
                g.memset(wtb[:, :], 0.125)
                g.memset(wsb[:, :], 0.125).then_inc(b.s_v.bump(1), 1)
        v_gates = b.s_v.n
        if DBG_PHASE == 5.3:
            mm_tr_marks = []

            @b.blk.tensor
            def _(t):
                t.wait_ge(b.s_v.h, v_gates)
                for ti in range(NT):
                    if ti >= 1:
                        t.wait_ge(b.s_v.h, v_gates + ti)
                    t.transpose(ptr[0:NEXP1, :],
                                wtb[:, ti * NEXP1:(ti + 1) * NEXP1],
                                b.ident[:, :]).then_inc(b.s_mm.bump(1), 1)
                    mm_tr_marks.append(b.s_mm.n)

            @b.blk.vector
            def _(v):
                for ti in range(NT):
                    v.wait_ge(b.s_mm.h, mm_tr_marks[ti])
                    v.tensor_copy(wtT[:, ti * 128:(ti + 1) * 128],
                                  ptr[0:NEXP1, :]).then_inc(b.s_v.bump(1), 1)
            return
        @b.blk.tensor
        def _(t):
            t.wait_ge(b.s_lg.h, ld_gates)
            for ti in range(NT):
                t.wait_ge(b.s_tr.h, tr0 + 16 * NCH * (ti + 1))
                for c in range(NCH):
                    t.matmul(pg[:, ti * NEXP1:(ti + 1) * NEXP1],
                             fmtile(c, ti), gwsb[:, c * NEXP1:(c + 1) * NEXP1],
                             start=(c == 0), stop=False)
                mmi = t.matmul(pg[:, ti * NEXP1:(ti + 1) * NEXP1],
                               b.ones[0:1, 0:128], gbsb[0:1, :],
                               start=False, stop=True)
                if ti == NT - 1:
                    mmi.then_inc(b.s_mm.bump(1), 1)
        mm_gates = b.s_mm.n

        if DBG_PHASE < 4:
            return
        @b.blk.scalar
        def _(s):
            s.wait_ge(b.s_mm.h, mm_gates)
            s.activation(gtmp[:, :], pg[:, :], AF.Exp)
            s.drain().then_inc(b.s_a.bump(1), 1)
        a_exp = b.s_a.n

        gsum = gstat[:, 0:NT]
        gwsum = gstat[:, NT:2 * NT]

        @b.blk.vector
        def _(v):
            v.wait_ge(b.s_a.h, a_exp)
            gv = bass.AP(gtmp, 0, [[gtmp[:, :].ap[0][0], 128], [NEXP1, NT], [1, NEXP1]])
            wv = bass.AP(wsb, 0, [[wsb[:, :].ap[0][0], 128], [NEXP1, NT], [1, NEXP1]])
            sumb = bass.AP(gstat, 0, [[gstat[:, :].ap[0][0], 128], [1, NT], [0, NEXP1]])
            wsumb = bass.AP(gstat, NT, [[gstat[:, :].ap[0][0], 128], [1, NT], [0, NEXP1]])
            v.reduce_sum(gsum[:, :], gv, axis=AXX)
            v.drain()
            v.tensor_scalar_mul(gsum[:, :], gsum[:, :], 0.1)
            v.drain()
            v.tensor_tensor(wv, gv, sumb, ALU.is_ge)
            v.drain()
            v.tensor_mul(wv, wv, gv)
            v.drain()
            v.reduce_sum(gwsum[:, :], wv, axis=AXX)
            v.drain()
            v.reciprocal(gwsum[:, :], gwsum[:, :])
            v.drain()
            v.tensor_scalar_mul(gwsum[:, :], gwsum[:, :], 0.5)
            v.drain()
            v.tensor_tensor(wv, wv, wsumb, ALU.mult)
            v.drain()
            v.tensor_copy(wtb[:, :], wsb[:, :])
            v.drain().then_inc(b.s_v.bump(1), 1)
        v_gates = b.s_v.n

        if DBG_PHASE < 5:
            return
        if DBG_PHASE == 7:
            @b.blk.sync
            def _(sy):
                sy.wait_ge(b.s_v.h, v_gates)
                sy.dma_start(out=bass.AP(b.out, 0, [[D, 128], [1, NT * NEXP1]]),
                             in_=wsb[:, :]).then_inc(b.s_tr.bump(16), 16)
            return
        mm_tr_marks = []
        ntr_loop = 1 if DBG_PHASE == 5.41 else NT

        @b.blk.tensor
        def _(t):
            t.wait_ge(b.s_v.h, v_gates)
            for ti in range(ntr_loop):
                if ti >= 1:
                    t.wait_ge(b.s_v.h, v_gates + ti)
                t.transpose(ptr[0:NEXP1, :], wtb[:, ti * NEXP1:(ti + 1) * NEXP1],
                            b.ident[:, :]).then_inc(b.s_mm.bump(1), 1)
                mm_tr_marks.append(b.s_mm.n)

        @b.blk.vector
        def _(v):
            for ti in range(ntr_loop):
                v.wait_ge(b.s_mm.h, mm_tr_marks[ti])
                v.tensor_copy(wtT[:, ti * 128:(ti + 1) * 128],
                              ptr[0:NEXP1, :]).then_inc(b.s_v.bump(1), 1)
        v_wtT = b.s_v.n

        if DBG_PHASE < 5.8:
            return
        mm_bias_marks = []

        @b.blk.tensor
        def _(t):
            for ti in range(NT):
                t.wait_ge(b.s_v.h, v_wtT - NT + ti + 1)
                if ti >= 2:
                    t.wait_ge(b.s_v.h, v_wtT + 2 * (ti - 1))
                for h in range(2):
                    t.matmul(psum[(2 * ti + h) % 4][:, :],
                             wtT[:, ti * 128:(ti + 1) * 128],
                             ebsb[:, h * 512:(h + 1) * 512],
                             start=True, stop=True).then_inc(b.s_mm.bump(1), 1)
                    mm_bias_marks.append(b.s_mm.n)

        @b.blk.vector
        def _(v):
            for ti in range(NT):
                for h in range(2):
                    v.wait_ge(b.s_mm.h, mm_bias_marks[2 * ti + h])
                    xs = b.XRES[:, ti * D + h * 512: ti * D + (h + 1) * 512]
                    v.tensor_add(xs, xs,
                                 psum[(2 * ti + h) % 4][:, :]).then_inc(b.s_v.bump(1), 1)

        if DBG_PHASE < 6:
            return
        mm_done_expert = {}
        v_done_expert = {}
        for e in range(NEXP1):
            mmb = b.s_mm.n
            vb = b.s_v.n

            @b.blk.tensor
            def _(t, e=e, mmb=mmb, vb=vb):
                t.wait_ge(wsem[e % 2].h, ld_marks[e])
                t.wait_ge(b.s_v.h, vb)
                k = 0
                for ti in range(NT):
                    for h in range(2):
                        if k >= 4:
                            t.wait_ge(b.s_v.h, vb + k - 3)
                        pb = psum[k % 4]
                        for c in range(NCH):
                            mmi = t.matmul(pb[:, :], fmtile(c, ti),
                                           wbuf[e % 2][:, c * D + h * 512:
                                                       c * D + (h + 1) * 512],
                                           start=(c == 0), stop=(c == NCH - 1))
                        mmi.then_inc(b.s_mm.bump(1), 1)
                        k += 1
            mm_done_expert[e] = b.s_mm.n

            @b.blk.gpsimd
            def _(g, e=e):
                if e + 1 < NEXP1:
                    if e >= 1:
                        g.wait_ge(b.s_mm.h, mm_done_expert[e - 1])
                    ld_marks[e + 1] = load_expert(g, e + 1, (e + 1) % 2)

            @b.blk.vector
            def _(v, e=e, mmb=mmb):
                k = 0
                for ti in range(NT):
                    for h in range(2):
                        v.wait_ge(b.s_mm.h, mmb + k + 1)
                        xs = b.XRES[:, ti * D + h * 512: ti * D + (h + 1) * 512]
                        v.scalar_tensor_tensor(
                            xs, psum[k % 4][:, :],
                            wsb[:, ti * NEXP1 + e: ti * NEXP1 + e + 1],
                            xs, ALU.mult, ALU.add).then_inc(b.s_v.bump(1), 1)
                        k += 1
            v_done_expert[e] = b.s_v.n

    def emit_l0ffn(self, tag):
        b = self
        nc = self.nc
        with contextlib.ExitStack() as es:
            old = b.es
            b.es = es
            self._l0ffn(es, tag)
            b.es = old

    def _l0ffn(self, es, tag):
        b = self
        nc = self.nc
        A = lambda n, s, dt: es.enter_context(nc.sbuf_tensor(tag + n, s, dt))
        r_, nmr = b.emit_ln_stats()
        rD = A("rD", [128, NT * 64], BF16)
        rCp = A("rCp", [32, T], BF16)
        with contextlib.ExitStack() as es_ab:
            xtm = es_ab.enter_context(
                nc.sbuf_tensor(tag + "xtm", [128, NT * D], BF16))
            self._l0ffn_passA(es_ab, tag, r_, nmr, xtm, rD, rCp)
            self._l0ffn_passB(es_ab, tag, xtm, rD)
        ysm = A("ysm", [128, 16 * D], BF16)
        with contextlib.ExitStack() as es_c:
            self._l0ffn_passC(es_c, tag, ysm)
        self._l0ffn_passD(es, tag, rCp, ysm)

    def _l0ffn_passA(self, es0, tag, r_, nmr, xtm, rD, rCp):
        b = self
        nc = self.nc
        with contextlib.ExitStack() as es:
            A = lambda n, s, dt: es.enter_context(nc.sbuf_tensor(tag + n, s, dt))
            phiw = A("phiw", [128, NCH * NSLOT], BF16)
            phib = A("phib", [1, NSLOT], BF16)
            mask64 = A("mask64", [128, 16 * 64], BF16)
            maskp = A("maskp", [128, PSL], BF16)
            fmr = A("fmr", [128, 2 * D], BF16)
            etm = [A(f"etm{i}", [128, NSLOT], BF16) for i in range(2)]
            esm = [A(f"esm{i}", [128, NSLOT], BF16) for i in range(2)]

            @b.blk.gpsimd
            def _(g):
                src = bass.AP(b.phiwT, 0,
                              [[NSLOT, 128], [128 * NSLOT, NCH], [1, NSLOT]])
                b.lg_dma(g, phiw[:, :], src)
                b.lg_dma(g, phib[:, :], b.phib.ap())
                b.lg_dma(g, mask64[:, :], b.cmask64.ap())
                b.lg_dma(g, maskp[:, :], b.cmaskp.ap())
            ld_phi = b.s_lg.n

            with (nc.psum_tensor(tag + "pl0", [128, 512], F32) as pl0,
                  nc.psum_tensor(tag + "pl1", [128, 512], F32) as pl1,
                  nc.psum_tensor(tag + "pl2", [128, 512], F32) as pl2,
                  nc.psum_tensor(tag + "pl3", [128, 512], F32) as pl3,
                  nc.psum_tensor(tag + "psd", [128, 64], F32) as psd,
                  nc.psum_tensor(tag + "psc", [32, 128], F32) as psc):
                pls = [pl0, pl1, pl2, pl3]
                vA, trF, mmA, aA, scE, mmS, vS = [], [], [], [], [], [], []
                for ti in range(NT):
                    @b.blk.vector
                    def _(v, ti=ti):
                        if ti == 0:
                            v.wait_ge(b.s_lg.h, ld_phi)
                        v.tensor_scalar(xtm[:, ti * D:(ti + 1) * D],
                                        b.xres_tile(ti), r_[:, ti:ti + 1],
                                        nmr[:, ti:ti + 1], ALU.mult, ALU.add)
                        v.drain().then_inc(b.s_v.bump(1), 1)
                    vA.append(b.s_v.n)

                    @b.blk.sync
                    def _(sy, ti=ti):
                        sy.wait_ge(b.s_v.h, vA[ti])
                        if ti >= 2:
                            sy.wait_ge(b.s_mm.h, mmA[ti - 2])
                        for c in range(NCH):
                            sy.dma_start_transpose(
                                fmr[:, (ti % 2) * D + c * 128:
                                    (ti % 2) * D + (c + 1) * 128],
                                xtm[:, ti * D + c * 128: ti * D + (c + 1) * 128],
                            ).then_inc(b.s_tr.bump(16), 16)
                        sy.wait_ge(b.s_tr.h, b.s_tr.n)
                    trF.append(b.s_tr.n)

                    @b.blk.tensor
                    def _(t, ti=ti):
                        if ti == 0:
                            t.wait_ge(b.s_lg.h, ld_phi)
                        t.wait_ge(b.s_tr.h, trF[ti])
                        if ti >= 1:
                            t.wait_ge(b.s_a.h, aA[ti - 1])
                        for st in range(4):
                            for c in range(NCH):
                                t.matmul(pls[st][:, :],
                                         fmr[:, (ti % 2) * D + c * 128:
                                             (ti % 2) * D + (c + 1) * 128],
                                         phiw[:, c * NSLOT + st * 512:
                                              c * NSLOT + (st + 1) * 512],
                                         start=(c == 0), stop=False)
                            mmi = t.matmul(pls[st][:, :], b.ones[0:1, 0:128],
                                           phib[0:1, st * 512:(st + 1) * 512],
                                           start=False, stop=True)
                        mmi.then_inc(b.s_mm.bump(1), 1)
                    mmA.append(b.s_mm.n)

                    @b.blk.scalar
                    def _(s, ti=ti):
                        s.wait_ge(b.s_mm.h, mmA[ti])
                        if ti >= 2:
                            s.wait_ge(b.s_mm.h, mmS[ti - 2])
                            s.wait_ge(b.s_sc.h, scE[ti - 2])
                        for st in range(4):
                            s.activation(etm[ti % 2][:, st * 512:(st + 1) * 512],
                                         pls[st][:, :], AF.Exp)
                        s.drain().then_inc(b.s_a.bump(1), 1)
                        s.dma_start(
                            out=bass.AP(b.e_tm, ti * 128 * NSLOT,
                                        [[NSLOT, 128], [1, NSLOT]]),
                            in_=etm[ti % 2][:, :]).then_inc(b.s_sc.bump(16), 16)
                        for j in range(16):
                            s.dma_start_transpose(
                                esm[ti % 2][:, j * 128:(j + 1) * 128],
                                etm[ti % 2][:, j * 128:(j + 1) * 128],
                            ).then_inc(b.s_sc.bump(16), 16)
                        s.wait_ge(b.s_sc.h, b.s_sc.n)
                        s.dma_start(
                            out=bass.AP(b.e_sm, ti * 128 * NSLOT,
                                        [[NSLOT, 128], [1, NSLOT]]),
                            in_=esm[ti % 2][:, :]).then_inc(b.s_sc.bump(16), 16)
                    aA.append(b.s_a.n)
                    scE.append(b.s_sc.n)

                    @b.blk.tensor
                    def _(t, ti=ti):
                        t.wait_ge(b.s_sc.h, scE[ti] - 16)
                        if ti >= 1:
                            t.wait_ge(b.s_v.h, vS[ti - 1])
                        for j in range(16):
                            ch = esm[ti % 2][:, j * 128:(j + 1) * 128]
                            t.matmul(psd[:, :], ch,
                                     mask64[:, j * 64:(j + 1) * 64],
                                     start=(j == 0), stop=(j == 15))
                        for j in range(16):
                            ch = esm[ti % 2][:, j * 128:(j + 1) * 128]
                            mmi = t.matmul(psc[:, :], maskp[:, :], ch,
                                           start=(j == 0), stop=(j == 15))
                        mmi.then_inc(b.s_mm.bump(1), 1)
                    mmS.append(b.s_mm.n)

                    @b.blk.vector
                    def _(v, ti=ti):
                        v.wait_ge(b.s_mm.h, mmS[ti])
                        with nc.allow_low_precision(reason="bf16 softmax divisors"):
                            v.reciprocal(rD[:, ti * 64:(ti + 1) * 64], psd[:, :])
                            v.reciprocal(rCp[:, ti * 128:(ti + 1) * 128],
                                         psc[0:32, :])
                        v.drain().then_inc(b.s_v.bump(1), 1)
                    vS.append(b.s_v.n)
                self._l0_scE_all = b.s_sc.n
                self._l0_vS_all = b.s_v.n

    def _l0ffn_passB(self, es0, tag, xtm, rD):
        b = self
        nc = self.nc
        with contextlib.ExitStack() as es:
            A = lambda n, s, dt: es.enter_context(nc.sbuf_tensor(tag + n, s, dt))
            er = [A(f"er{i}", [128, 512], BF16) for i in range(2)]
            dr = [A(f"dr{i}", [128, 512], BF16) for i in range(2)]
            xst = [A(f"xst{i}", [128, D], F32) for i in range(2)]
            with contextlib.ExitStack() as pes:
                pb = [pes.enter_context(
                    nc.psum_tensor(f"{tag}pb{i}", [128, 512], F32))
                    for i in range(8)]
                gE, vDv, mmB, vEv, spX = [], [], [], [], []
                k = 0
                for g in range(4):
                    for ti in range(NT):
                        slot = k % 2
                        sem = [b.s_ea, b.s_eb][slot]

                        @b.blk.gpsimd
                        def _(gp, g=g, ti=ti, slot=slot, sem=sem, k=k):
                            if k == 0:
                                gp.wait_ge(b.s_sc.h, self._l0_scE_all)
                                gp.wait_ge(b.s_v.h, self._l0_vS_all)
                            if k >= 2:
                                gp.wait_ge(b.s_v.h, vDv[k - 2])
                            src = bass.AP(b.e_tm, (ti * 128 * NSLOT) + g * 512,
                                          [[NSLOT, 128], [1, 512]])
                            gp.dma_start(out=er[slot][:, :], in_=src
                                         ).then_inc(sem.bump(16), 16)
                        gE.append(sem.n)

                        @b.blk.vector
                        def _(v, g=g, ti=ti, slot=slot, sem=sem, k=k):
                            v.wait_ge(sem.h, gE[k])
                            if k >= 2:
                                v.wait_ge(b.s_mm.h, mmB[k - 2])
                            dvs = bass.AP(rD, ti * 64 + 16 * g,
                                          [[rD[:, :].ap[0][0], 128], [1, 16], [0, PSL]])
                            v.tensor_tensor(dr[slot][:, :], er[slot][:, :],
                                            dvs, ALU.mult)
                            v.drain().then_inc(b.s_v.bump(1), 1)
                        vDv.append(b.s_v.n)

                        @b.blk.tensor
                        def _(t, g=g, ti=ti, slot=slot, k=k):
                            t.wait_ge(b.s_v.h, vDv[k])
                            if ti == 0 and g > 0:
                                t.wait_ge(b.s_v.h, vEv[g - 1])
                            for ch in range(4):
                                for h in range(2):
                                    mmi = t.matmul(
                                        pb[ch * 2 + h][:, :],
                                        dr[slot][:, ch * 128:(ch + 1) * 128],
                                        xtm[:, ti * D + h * 512:
                                            ti * D + (h + 1) * 512],
                                        start=(ti == 0), stop=(ti == NT - 1))
                            mmi.then_inc(b.s_mm.bump(1), 1)
                        mmB.append(b.s_mm.n)
                        k += 1

                    for ch in range(4):
                        @b.blk.vector
                        def _(v, g=g, ch=ch):
                            if ch == 0:
                                v.wait_ge(b.s_mm.h, mmB[-1])
                            if len(spX) >= 2:
                                v.wait_ge(b.s_sp.h, spX[-2])
                            v.tensor_copy(xst[ch % 2][:, 0:512], pb[ch * 2][:, :])
                            v.tensor_copy(xst[ch % 2][:, 512:1024],
                                          pb[ch * 2 + 1][:, :])
                            v.drain().then_inc(b.s_v.bump(1), 1)
                        vch = b.s_v.n

                        @b.blk.gpsimd
                        def _(gp, g=g, ch=ch, vch=vch):
                            gp.wait_ge(b.s_v.h, vch)
                            dst = bass.AP(b.xt_in, (g * 512 + ch * 128) * D,
                                          [[D, 128], [1, D]])
                            gp.dma_start(out=dst, in_=xst[ch % 2][:, :]
                                         ).then_inc(b.s_sp.bump(16), 16)
                        spX.append(b.s_sp.n)
                    vEv.append(b.s_v.n)

    def _l0ffn_passC(self, es, tag, ysm):
        b = self
        nc = self.nc
        A = lambda n, s, dt: es.enter_context(nc.sbuf_tensor(tag + n, s, dt))
        xtc = A("xtc", [128, 2 * D], F32)
        xtt = A("xtt", [128, 2 * D], BF16)
        xtf = A("xtf", [128, 2 * D], BF16)
        ytst = [A(f"ytst{i}", [32, D], BF16) for i in range(2)]
        l0eb = A("l0eb", [1, 8 * D], BF16)
        st2 = b.STATS[:, 512:544]

        @b.blk.gpsimd
        def _(g):
            g.wait_ge(b.s_sp.h, b.s_sp.n)
            g.collective_compute(
                "ReduceScatter", mybir.AluOpType.add,
                replica_groups=[list(range(NCORES))],
                ins=[b.xt_in.ap().opt()], outs=[b.xt_rs.ap().opt()],
            ).then_inc(b.s_cc.bump(1), 1)
            g.wait_ge(b.s_cc.h, b.s_cc.n)
            src = bass.AP(b.xt_rs, 0, [[D, 128], [128 * D, 2], [1, D]])
            b.lg_dma(g, xtc[:, :], src)
            b.lg_dma(g, l0eb[:, :], bass.AP(b.l0eb, 0, [[8 * D, 1], [1, 8 * D]]))
        ld_xtc = b.s_lg.n

        wbuf = [A(f"wb{i}", [128, NCH * D], BF16) for i in range(2)]
        wsem = [b.s_wa, b.s_wb]
        ldW = {}

        @b.blk.gpsimd
        def _(g):
            src = bass.AP(b.l0ew, 0, [[D, 128], [128 * D, NCH], [1, D]])
            g.dma_start(out=wbuf[0][:, :], in_=src).then_inc(b.s_wa.bump(16), 16)
            ldW[0] = b.s_wa.n

        @b.blk.vector
        def _(v):
            v.wait_ge(b.s_lg.h, ld_xtc)
            v.tensor_scalar(xtc[:, :], xtc[:, :], -33000.0, 65000.0,
                            ALU.max, ALU.min)
            v.drain()
            bnB = b.STATS[:, 544:592]
            for i in range(2):
                for hh in range(2):
                    v.bn_stats(bnB[:, i * 12 + hh * 6:i * 12 + hh * 6 + 6],
                               xtc[:, i * D + hh * 512:i * D + (hh + 1) * 512])
            v.drain()
            for i in range(2):
                v.bn_aggr(st2[:, 2 * i:2 * i + 2], bnB[:, i * 12:i * 12 + 12])
            v.drain().then_inc(b.s_v.bump(1), 1)

        @b.blk.scalar
        def _(s):
            s.wait_ge(b.s_v.h, b.s_v.n)
            varv = bass.AP(b.STATS, 512 + 1, [[768, 128], [2, 2]])
            s.activation(st2[:, 4:6], varv, AF.Sqrt, bias=b.ceps[:, 0:1])
            s.drain().then_inc(b.s_a.bump(1), 1)

        @b.blk.vector
        def _(v):
            v.wait_ge(b.s_a.h, b.s_a.n)
            v.reciprocal(st2[:, 4:6], st2[:, 4:6])
            v.drain()
            muv = bass.AP(b.STATS, 512, [[768, 128], [2, 2]])
            v.scalar_tensor_tensor(st2[:, 6:8], muv, -1.0, st2[:, 4:6],
                                   ALU.mult, ALU.mult)
            v.drain()
            for i in range(2):
                v.tensor_scalar(xtc[:, i * D:(i + 1) * D],
                                xtc[:, i * D:(i + 1) * D],
                                st2[:, 4 + i:5 + i], st2[:, 6 + i:7 + i],
                                ALU.mult, ALU.add)
            v.drain().then_inc(b.s_v.bump(1), 1)

        @b.blk.scalar
        def _(s):
            s.wait_ge(b.s_v.h, b.s_v.n)
            s.activation(xtt[:, :], xtc[:, :], AF.Tanh)
            s.drain().then_inc(b.s_a.bump(1), 1)
        a_tanh = b.s_a.n

        @b.blk.sync
        def _(sy):
            sy.wait_ge(b.s_a.h, a_tanh)
            for i in range(2):
                for j in range(NCH):
                    sy.dma_start_transpose(
                        xtf[:, j * 256 + i * 128: j * 256 + (i + 1) * 128],
                        xtt[:, i * D + j * 128: i * D + (j + 1) * 128],
                    ).then_inc(b.s_tr.bump(16), 16)
            sy.wait_ge(b.s_tr.h, b.s_tr.n)
        tr_xt = b.s_tr.n

        with (nc.psum_tensor(tag + "pe0", [32, D], F32) as pe0,
              nc.psum_tensor(tag + "pe1", [32, D], F32) as pe1):
            pes_ = [pe0, pe1]
            mmE, vY, spY = [], [], []
            for e in range(8):
                @b.blk.tensor
                def _(t, e=e):
                    if e == 0:
                        t.wait_ge(b.s_tr.h, tr_xt)
                        t.wait_ge(b.s_lg.h, ld_xtc)
                    t.wait_ge(wsem[e % 2].h, ldW[e])
                    if e >= 2:
                        t.wait_ge(b.s_v.h, vY[e - 2])
                    i, so = e // 4, (e % 4) * 32
                    for h in range(2):
                        for j in range(NCH):
                            t.matmul(pes_[e % 2][:, h * 512:(h + 1) * 512],
                                     xtf[:, j * 256 + i * 128 + so:
                                         j * 256 + i * 128 + so + 32],
                                     wbuf[e % 2][:, j * D + h * 512:
                                                 j * D + (h + 1) * 512],
                                     start=(j == 0), stop=False)
                        mmi = t.matmul(pes_[e % 2][:, h * 512:(h + 1) * 512],
                                       b.ones[0:1, 0:32],
                                       l0eb[0:1, e * D + h * 512:
                                            e * D + (h + 1) * 512],
                                       start=False, stop=True)
                    mmi.then_inc(b.s_mm.bump(1), 1)
                mmE.append(b.s_mm.n)

                @b.blk.gpsimd
                def _(g, e=e):
                    if e + 1 < 8:
                        if e >= 1:
                            g.wait_ge(b.s_mm.h, mmE[e - 1])
                        src = bass.AP(b.l0ew, (e + 1) * D * D,
                                      [[D, 128], [128 * D, NCH], [1, D]])
                        g.dma_start(out=wbuf[(e + 1) % 2][:, :], in_=src
                                    ).then_inc(wsem[(e + 1) % 2].bump(16), 16)
                        ldW[e + 1] = wsem[(e + 1) % 2].n

                @b.blk.vector
                def _(v, e=e):
                    v.wait_ge(b.s_mm.h, mmE[e])
                    if e >= 2:
                        v.wait_ge(b.s_sp.h, spY[e - 2])
                    v.tensor_copy(ytst[e % 2][:, :], pes_[e % 2][0:32, :])
                    v.drain().then_inc(b.s_v.bump(1), 1)
                vY.append(b.s_v.n)

                @b.blk.gpsimd
                def _(g, e=e):
                    g.wait_ge(b.s_v.h, vY[e])
                    dst = bass.AP(b.yt_in, e * 32 * D, [[D, 32], [1, D]])
                    g.dma_start(out=dst, in_=ytst[e % 2][:, :]
                                ).then_inc(b.s_sp.bump(16), 16)
                spY.append(b.s_sp.n)

        @b.blk.gpsimd
        def _(g):
            g.wait_ge(b.s_sp.h, b.s_sp.n)
            g.collective_compute(
                "AllGather", mybir.AluOpType.bypass,
                replica_groups=[list(range(NCORES))],
                ins=[b.yt_in.ap().opt()], outs=[b.yt_ag.ap().opt()],
            ).then_inc(b.s_cc.bump(1), 1)
            g.wait_ge(b.s_cc.h, b.s_cc.n)
            src = bass.AP(b.yt_ag, 0, [[D, 128], [128 * D, 16], [1, D]])
            b.lg_dma(g, ysm[:, :], src)

    def _l0ffn_passD(self, es, tag, rCp, ysm):
        b = self
        nc = self.nc
        A = lambda n, s, dt: es.enter_context(nc.sbuf_tensor(tag + n, s, dt))
        rCe = A("rCe", [128, T], BF16)
        emr = [A(f"emr{i}", [128, NSLOT], BF16) for i in range(2)]
        cr = [A(f"cr{i}", [128, NSLOT], BF16) for i in range(2)]
        ld_ysm = b.s_lg.n

        @b.blk.gpsimd
        def _(g):
            g.wait_ge(b.s_tr.h, b.s_tr.n)
            g.wait_ge(b.s_v.h, b.s_v.n)
            for i in range(4):
                b.lg_dma(g, rCe[32 * i:32 * (i + 1), :], rCp[0:32, :])
        ld_rce = b.s_lg.n

        with contextlib.ExitStack() as pes:
            pd = [pes.enter_context(
                nc.psum_tensor(f"{tag}pd{i}", [128, D], F32)) for i in range(2)]
            gE, vC, mmD, vR = [], [], [], []
            for ti in range(NT):
                slot = ti % 2
                sem = [b.s_ea, b.s_eb][slot]

                @b.blk.gpsimd
                def _(g, ti=ti, slot=slot, sem=sem):
                    if ti < 2:
                        g.wait_ge(b.s_lg.h, ld_rce)
                    else:
                        g.wait_ge(b.s_v.h, vC[ti - 2])
                    src = bass.AP(b.e_sm, ti * 128 * NSLOT, [[NSLOT, 128], [1, NSLOT]])
                    g.dma_start(out=emr[slot][:, :], in_=src
                                ).then_inc(sem.bump(16), 16)
                gE.append(sem.n)

                @b.blk.vector
                def _(v, ti=ti, slot=slot, sem=sem):
                    v.wait_ge(sem.h, gE[ti])
                    if ti >= 2:
                        v.wait_ge(b.s_mm.h, mmD[ti - 2])
                    dvs = bass.AP(rCe, ti * 128,
                                  [[rCe[:, :].ap[0][0], 128], [0, 16], [1, 128]])
                    v.tensor_tensor(cr[slot][:, :], emr[slot][:, :], dvs, ALU.mult)
                    v.drain().then_inc(b.s_v.bump(1), 1)
                vC.append(b.s_v.n)

                @b.blk.tensor
                def _(t, ti=ti, slot=slot):
                    t.wait_ge(b.s_v.h, vC[ti])
                    if ti == 0:
                        t.wait_ge(b.s_lg.h, ld_ysm)
                    if ti >= 2:
                        t.wait_ge(b.s_v.h, vR[ti - 2])
                    for h in range(2):
                        for j in range(16):
                            mmi = t.matmul(pd[slot][:, h * 512:(h + 1) * 512],
                                           cr[slot][:, j * 128:(j + 1) * 128],
                                           ysm[:, j * D + h * 512:
                                               j * D + (h + 1) * 512],
                                           start=(j == 0), stop=(j == 15))
                    mmi.then_inc(b.s_mm.bump(1), 1)
                mmD.append(b.s_mm.n)

                @b.blk.vector
                def _(v, ti=ti, slot=slot):
                    v.wait_ge(b.s_mm.h, mmD[ti])
                    for h in range(2):
                        xs = b.XRES[:, ti * D + h * 512: ti * D + (h + 1) * 512]
                        v.scalar_tensor_tensor(xs, pd[slot][:, h * 512:(h + 1) * 512],
                                               0.5, xs, ALU.mult, ALU.add)
                    v.drain().then_inc(b.s_v.bump(1), 1)
                vR.append(b.s_v.n)

    def emit_l0attn(self):
        b = self
        nc = self.nc
        with contextlib.ExitStack() as es:
            old = b.es
            b.es = es
            self._l0attn(es)
            b.es = old

    def _l0attn(self, es):
        b = self
        nc = self.nc
        tag = "l0a_"
        A = lambda n, s, dt: es.enter_context(nc.sbuf_tensor(tag + n, s, dt))
        r_, nmr = b.emit_ln_stats()
        gain0 = A("gain0", [128, D], BF16)
        gain0o = A("gain0o", [128, D], BF16)

        @b.blk.gpsimd
        def _(g):
            bc = lambda t_: bass.AP(t_, 0, [[0, 128], [1, D]])
            b.lg_dma(g, gain0[:, :], bc(b.l0ng))
            b.lg_dma(g, gain0o[:, :], bc(b.l0og))

        k2 = A("k2", [128, T], BF16)
        vt1 = A("vt1", [128, NT * 65], BF16)
        rq = A("rq", [128, NT * 8], F32)
        rk = A("rk", [128, NT], F32)
        sqq = A("sqq", [128, NT * 8], F32)
        ksq = A("ksq", [128, 64], F32)
        woutp = A("woutp", [64, 8 * D], BF16)
        q_fm = A("qfm", [128, 4 * T], BF16)
        k_fm = A("kfm", [128, T], BF16)
        osb = A("osb", [64, 8 * 512], BF16)
        with contextlib.ExitStack() as es1:
            A1 = lambda n, s, dt: es1.enter_context(nc.sbuf_tensor(tag + n, s, dt))
            tmp = A1("tmp", [128, 2 * D], BF16)
            fmr = A1("fmr", [128, 2 * D], BF16)
            q_tm = A1("qtm", [128, NT * 512], BF16)
            wq = A1("wq", [128, NCH * 512], BF16)
            wkv = A1("wkv", [128, NCH * 128], BF16)
            sqs = [A1(f"sqs{i}", [128, 512], BF16) for i in range(2)]

            @b.blk.gpsimd
            def _(g):
                b.lg_dma(g, wq[:, :], bass.AP(
                    b.wq0, 0, [[512, 128], [128 * 512, NCH], [1, 512]]))
                b.lg_dma(g, wkv[:, :], bass.AP(
                    b.wkv0, 0, [[128, 128], [128 * 128, NCH], [1, 128]]))
                b.lg_dma(g, woutp[:, :], bass.AP(
                    b.wout0, 0, [[D, 64], [64 * D, 8], [1, D]]))
                g.wait_ge(b.s_v.h, b.s_v.n)
                g.memset(bass.AP(vt1, 64, [[vt1[:, :].ap[0][0], 128],
                                           [65, NT], [1, 1]]),
                         1.0).then_inc(b.s_g.bump(1), 1)
            ld_w = b.s_lg.n

            with (nc.psum_tensor(tag + "pq0", [128, 512], F32) as pq0,
                  nc.psum_tensor(tag + "pq1", [128, 512], F32) as pq1,
                  nc.psum_tensor(tag + "pk0", [128, 128], F32) as pk0,
                  nc.psum_tensor(tag + "pk1", [128, 128], F32) as pk1):
                pqs, pks = [pq0, pq1], [pk0, pk1]
                vLn, trR, mmq, vq, aq, vq2 = [], [], [], [], [], []
                for ti in range(NT):
                    @b.blk.vector
                    def _(v, ti=ti):
                        if ti >= 2:
                            v.wait_ge(b.s_tr.h, trR[ti - 2])
                        v.tensor_scalar(tmp[:, (ti % 2) * D:(ti % 2 + 1) * D],
                                        b.xres_tile(ti), r_[:, ti:ti + 1],
                                        nmr[:, ti:ti + 1], ALU.mult, ALU.add)
                        v.drain()
                        v.tensor_mul(tmp[:, (ti % 2) * D:(ti % 2 + 1) * D],
                                     tmp[:, (ti % 2) * D:(ti % 2 + 1) * D],
                                     gain0[:, :])
                        v.drain().then_inc(b.s_v.bump(1), 1)
                    vLn.append(b.s_v.n)

                    @b.blk.sync
                    def _(sy, ti=ti):
                        sy.wait_ge(b.s_v.h, vLn[ti])
                        if ti >= 2:
                            sy.wait_ge(b.s_mm.h, mmq[ti - 2])
                        for c in range(NCH):
                            sy.dma_start_transpose(
                                fmr[:, (ti % 2) * D + c * 128:
                                    (ti % 2) * D + (c + 1) * 128],
                                tmp[:, (ti % 2) * D + c * 128:
                                    (ti % 2) * D + (c + 1) * 128],
                            ).then_inc(b.s_tr.bump(16), 16)
                        sy.wait_ge(b.s_tr.h, b.s_tr.n)
                    trR.append(b.s_tr.n)

                    @b.blk.tensor
                    def _(t, ti=ti):
                        if ti == 0:
                            t.wait_ge(b.s_lg.h, ld_w)
                            t.wait_ge(b.s_g.h, b.s_g.n)
                        t.wait_ge(b.s_tr.h, trR[ti])
                        if ti >= 2:
                            t.wait_ge(b.s_a.h, aq[ti - 2])
                            t.wait_ge(b.s_v.h, vq[ti - 2])
                        for c in range(NCH):
                            t.matmul(pqs[ti % 2][:, :],
                                     fmr[:, (ti % 2) * D + c * 128:
                                         (ti % 2) * D + (c + 1) * 128],
                                     wq[:, c * 512:(c + 1) * 512],
                                     start=(c == 0), stop=(c == NCH - 1))
                        for c in range(NCH):
                            mmi = t.matmul(pks[ti % 2][:, :],
                                           fmr[:, (ti % 2) * D + c * 128:
                                               (ti % 2) * D + (c + 1) * 128],
                                           wkv[:, c * 128:(c + 1) * 128],
                                           start=(c == 0), stop=(c == NCH - 1))
                        mmi.then_inc(b.s_mm.bump(1), 1)
                    mmq.append(b.s_mm.n)

                    @b.blk.scalar
                    def _(s, ti=ti):
                        s.wait_ge(b.s_mm.h, mmq[ti])
                        if ti >= 2:
                            s.wait_ge(b.s_v.h, vq2[ti - 2])
                        s.activation(sqs[ti % 2][:, :], pqs[ti % 2][:, :],
                                     AF.Square)
                        s.drain().then_inc(b.s_a.bump(1), 1)
                    aq.append(b.s_a.n)

                    @b.blk.vector
                    def _(v, ti=ti):
                        v.wait_ge(b.s_mm.h, mmq[ti])
                        v.tensor_copy(q_tm[:, ti * 512:(ti + 1) * 512],
                                      pqs[ti % 2][:, :])
                        v.drain()
                        v.tensor_copy(k2[:, ti * 128: ti * 128 + 64],
                                      pks[ti % 2][:, 0:64])
                        v.drain()
                        v.tensor_copy(vt1[:, ti * 65: ti * 65 + 64],
                                      pks[ti % 2][:, 64:128])
                        v.drain().then_inc(b.s_v.bump(1), 1)
                    vq.append(b.s_v.n)

                    @b.blk.vector
                    def _(v, ti=ti):
                        v.wait_ge(b.s_a.h, aq[ti])
                        qv = bass.AP(sqs[ti % 2], 0,
                                     [[sqs[ti % 2][:, :].ap[0][0], 128], [64, 8], [1, 64]])
                        v.tensor_reduce(sqq[:, ti * 8:(ti + 1) * 8], qv,
                                        axis=AXX, op=ALU.add)
                        v.drain().then_inc(b.s_v.bump(1), 1)
                    vq2.append(b.s_v.n)

                @b.blk.vector
                def _(v):
                    v.wait_ge(b.s_v.h, vq[-1])
                    for ti in range(NT):
                        kk = k2[:, ti * 128: ti * 128 + 64]
                        v.tensor_mul(ksq[:, :], kk, kk)
                        v.drain()
                        v.tensor_reduce(rk[:, ti:ti + 1], ksq[:, :],
                                        axis=AXX, op=ALU.add)
                        v.drain()
                    v.drain().then_inc(b.s_v.bump(1), 1)

                @b.blk.scalar
                def _(s):
                    s.wait_ge(b.s_v.h, b.s_v.n)
                    s.activation(sqq[:, :], sqq[:, :], AF.Sqrt, bias=b.ceps[:, 1:2])
                    s.drain()
                    s.activation(rk[:, :], rk[:, :], AF.Sqrt, bias=b.ceps[:, 1:2])
                    s.drain().then_inc(b.s_a.bump(1), 1)

                @b.blk.vector
                def _(v):
                    v.wait_ge(b.s_a.h, b.s_a.n)
                    v.reciprocal(sqq[:, :], sqq[:, :])
                    v.drain()
                    v.reciprocal(rk[:, :], rk[:, :])
                    v.drain()
                    v.tensor_scalar_mul(sqq[:, :], sqq[:, :], 4.0)
                    v.drain()
                    v.tensor_scalar_mul(rk[:, :], rk[:, :], 4.0)
                    v.drain()
                    qb = bass.AP(sqq, 0, [[sqq[:, :].ap[0][0], 128], [1, NT * 8], [0, 64]])
                    qv2 = bass.AP(q_tm, 0, [[q_tm[:, :].ap[0][0], 128], [64, NT * 8], [1, 64]])
                    v.tensor_tensor(qv2, qv2, qb, ALU.mult)
                    v.drain()
                    kb = bass.AP(rk, 0, [[rk[:, :].ap[0][0], 128], [1, NT], [0, 64]])
                    k_s = bass.AP(k2, 0, [[k2[:, :].ap[0][0], 128], [128, NT], [1, 64]])
                    k_d = bass.AP(k2, 64, [[k2[:, :].ap[0][0], 128], [128, NT], [1, 64]])
                    v.tensor_tensor(k_s, k_s, kb, ALU.mult)
                    v.drain()
                    v.tensor_copy(k_d, k_s)
                    v.drain().then_inc(b.s_v.bump(1), 1)
                v_scale_done = b.s_v.n

            @b.blk.sync
            def _(sy):
                sy.wait_ge(b.s_v.h, v_scale_done)
                for ti in range(NT):
                    for hp in range(4):
                        sy.dma_start_transpose(
                            q_fm[:, hp * T + ti * 128:hp * T + (ti + 1) * 128],
                            q_tm[:, ti * 512 + hp * 128: ti * 512 + (hp + 1) * 128],
                        ).then_inc(b.s_tr.bump(16), 16)
                    sy.dma_start_transpose(
                        k_fm[:, ti * 128:(ti + 1) * 128],
                        k2[:, ti * 128:(ti + 1) * 128],
                    ).then_inc(b.s_tr.bump(16), 16)
                    if ti % 4 == 3:
                        sy.wait_ge(b.s_tr.h, b.s_tr.n)
                sy.wait_ge(b.s_tr.h, b.s_tr.n)
            tr_qk = b.s_tr.n

        et = A("et", [128, 32 * 512], BF16)
        gout = A("gout", [128, D], BF16)
        rsum = A("rsum", [1, 512], F32)
        bnC = b.STATS[:, 600:624]
        mvC = b.STATS[:, 624:626]
        rC3 = b.STATS[:, 626:627]
        nmC = b.STATS[:, 627:628]
        varC = bass.AP(b.STATS, 625, [[768, 128], [1, 1]])
        muC = bass.AP(b.STATS, 624, [[768, 128], [1, 1]])
        mmO_prev = [0]
        mmP_last = [0]
        aE_last = [0, 0]
        vN_last = [0, 0]
        vP_last = [0, 0]
        with (nc.psum_tensor(tag + "ps0", [128, 512], F32) as ps0,
              nc.psum_tensor(tag + "ps1", [128, 512], F32) as ps1,
              nc.psum_tensor(tag + "po0", [65, 512], F32) as po0,
              nc.psum_tensor(tag + "po1", [65, 512], F32) as po1,
              nc.psum_tensor(tag + "pbc", [64, 512], F32) as pbc,
              nc.psum_tensor(tag + "pp0", [128, 512], F32) as pp0,
              nc.psum_tensor(tag + "pp1", [128, 512], F32) as pp1):
            pss, pos = [ps0, ps1], [po0, po1]
            for qt in range(8):
                mmS_, aE, mmO, vN, mmB, aO, vO = [], [], [], [], [], [], []
                if qt == 0:
                    aO_last = [0, 0]
                for h in range(8):
                    hb = (h % 2) * 64
                    hp = h // 2

                    aE_base = b.s_a.n

                    @b.blk.tensor
                    def _(t, qt=qt, h=h, hb=hb, hp=hp, aE_base=aE_base):
                        if qt == 0 and h == 0:
                            t.wait_ge(b.s_tr.h, tr_qk)
                        for kc in range(32):
                            if kc >= 2:
                                t.wait_ge(b.s_a.h, aE_base + kc - 1)
                            elif h == 0 and qt == 0:
                                pass
                            else:
                                t.wait_ge(b.s_a.h, aE_base - 2 + kc + 1)
                            t.matmul(pss[kc % 2][:, :],
                                     k_fm[hb:hb + 64, kc * 128:(kc + 1) * 128],
                                     q_fm[hb:hb + 64,
                                          hp * T + qt * 512:hp * T + (qt + 1) * 512],
                                     start=True, stop=True,
                                     ).then_inc(b.s_mm.bump(1), 1)
                    mmS_.append(b.s_mm.n)

                    @b.blk.scalar
                    def _(s, qt=qt, h=h):
                        base = mmS_[h] - 32
                        if h >= 1 or qt > 0:
                            s.wait_ge(b.s_mm.h, mmO_prev[0])
                        for kc in range(32):
                            s.wait_ge(b.s_mm.h, base + kc + 1)
                            s.activation(et[:, kc * 512:(kc + 1) * 512],
                                         pss[kc % 2][:, :], AF.Exp,
                                         bias=b.ceps[:, 2:3],
                                         ).then_inc(b.s_a.bump(1), 1)
                        s.drain().then_inc(b.s_a.bump(1), 1)
                    aE.append(b.s_a.n)

                    @b.blk.tensor
                    def _(t, qt=qt, h=h):
                        t.wait_ge(b.s_a.h, aE[h])
                        if h >= 2:
                            t.wait_ge(b.s_v.h, vN[h - 2])
                            t.wait_ge(b.s_a.h, aO[h - 2])
                        elif qt > 0:
                            t.wait_ge(b.s_v.h, vN_last[h])
                            t.wait_ge(b.s_a.h, aO_last[h])
                        for kc in range(32):
                            mmi = t.matmul(pos[h % 2][:, :],
                                           vt1[:, kc * 65:(kc + 1) * 65],
                                           et[:, kc * 512:(kc + 1) * 512],
                                           start=(kc == 0), stop=(kc == 31))
                        mmi.then_inc(b.s_mm.bump(1), 1)
                        mmO_prev[0] = b.s_mm.n
                    mmO.append(b.s_mm.n)

                    @b.blk.vector
                    def _(v, qt=qt, h=h):
                        v.wait_ge(b.s_mm.h, mmO[h])
                        if h >= 1:
                            v.wait_ge(b.s_mm.h, mmB[h - 1])
                        with nc.allow_low_precision(reason="softmax recip"):
                            v.reciprocal(rsum[0:1, :], pos[h % 2][64:65, :])
                        v.drain().then_inc(b.s_v.bump(1), 1)
                    vN.append(b.s_v.n)

                    @b.blk.tensor
                    def _(t, qt=qt, h=h):
                        t.wait_ge(b.s_v.h, vN[h])
                        if h >= 1:
                            t.wait_ge(b.s_v.h, vO[h - 1])
                        t.matmul(pbc[:, :], b.onesf[0:1, 0:64], rsum[0:1, :],
                                 start=True, stop=True).then_inc(b.s_mm.bump(1), 1)
                    mmB.append(b.s_mm.n)

                    @b.blk.scalar
                    def _(s, qt=qt, h=h):
                        s.wait_ge(b.s_mm.h, mmO[h])
                        if qt > 0 and h == 0:
                            s.wait_ge(b.s_mm.h, mmP_last[0])
                        s.activation(osb[0:64, h * 512:(h + 1) * 512],
                                     pos[h % 2][0:64, :], AF.Copy)
                        s.drain().then_inc(b.s_a.bump(1), 1)
                    aO.append(b.s_a.n)

                    @b.blk.vector
                    def _(v, qt=qt, h=h):
                        v.wait_ge(b.s_mm.h, mmB[h])
                        v.wait_ge(b.s_a.h, aO[h])
                        v.tensor_tensor(osb[0:64, h * 512:(h + 1) * 512],
                                        osb[0:64, h * 512:(h + 1) * 512],
                                        pbc[0:64, :], ALU.mult)
                        v.drain().then_inc(b.s_v.bump(1), 1)
                    vO.append(b.s_v.n)
                aE_last = aE[-2:]
                vN_last = vN[-2:]
                aO_last = aO[-2:]

                mmP, vP = [], []
                for sub in range(4):
                    ti = qt * 4 + sub

                    @b.blk.tensor
                    def _(t, qt=qt, sub=sub):
                        if sub == 0:
                            t.wait_ge(b.s_v.h, vO[-1])
                        if sub >= 1:
                            t.wait_ge(b.s_v.h, vP[sub - 1])
                        elif qt > 0:
                            t.wait_ge(b.s_v.h, vP_last[-1])
                        for half, pps in ((0, pp0), (1, pp1)):
                            for h in range(8):
                                mmi = t.matmul(
                                    pps[:, :],
                                    osb[0:64, h * 512 + sub * 128:
                                        h * 512 + (sub + 1) * 128],
                                    woutp[0:64, h * D + half * 512:
                                          h * D + (half + 1) * 512],
                                    start=(h == 0), stop=(h == 7))
                            mmi.then_inc(b.s_mm.bump(1), 1)
                    mmP.append(b.s_mm.n)
                    mmP_last[0] = b.s_mm.n

                    @b.blk.vector
                    def _(v, qt=qt, sub=sub, ti=ti):
                        v.wait_ge(b.s_mm.h, mmP[sub])
                        v.bn_stats(bnC[:, 0:6], pp0[:, :])
                        v.bn_stats(bnC[:, 6:12], pp1[:, :])
                        v.drain()
                        v.bn_aggr(mvC[:, :], bnC[:, 0:12])
                        v.drain().then_inc(b.s_v.bump(1), 1)

                    @b.blk.scalar
                    def _(s, qt=qt, sub=sub):
                        s.wait_ge(b.s_v.h, b.s_v.n)
                        s.activation(rC3[:, :], varC, AF.Sqrt, bias=b.ceps[:, 0:1])
                        s.drain().then_inc(b.s_a.bump(1), 1)

                    @b.blk.vector
                    def _(v, qt=qt, sub=sub, ti=ti):
                        v.wait_ge(b.s_a.h, b.s_a.n)
                        v.reciprocal(rC3[:, :], rC3[:, :])
                        v.drain()
                        v.scalar_tensor_tensor(nmC[:, :], muC, -1.0, rC3[:, :],
                                               ALU.mult, ALU.mult)
                        v.drain()
                        for half, pps in ((0, pp0), (1, pp1)):
                            v.tensor_scalar(gout[:, half * 512:(half + 1) * 512],
                                            pps[:, :], rC3[:, 0:1], nmC[:, 0:1],
                                            ALU.mult, ALU.add)
                        v.drain()
                        v.tensor_mul(gout[:, :], gout[:, :], gain0o[:, :])
                        v.drain()
                        v.tensor_add(b.xres_tile(ti), b.xres_tile(ti), gout[:, :])
                        v.drain().then_inc(b.s_v.bump(1), 1)
                    vP.append(b.s_v.n)
                vP_last = vP[-2:]


    def emit_l1attn(self):
        b = self
        nc = self.nc
        with contextlib.ExitStack() as es:
            old = b.es
            b.es = es
            self._l1attn(es)
            b.es = old

    def _l1attn(self, es):
        b = self
        nc = self.nc
        tag = "l1a_"
        A = lambda n, s, dt: es.enter_context(nc.sbuf_tensor(tag + n, s, dt))
        r_, nmr = b.emit_ln_stats()

        wq = A("wq", [128, NCH * D], BF16)
        wk = A("wk", [128, NCH * D], BF16)
        wv = A("wv", [128, NCH * D], BF16)
        wo = A("wo", [128, NCH * D], BF16)
        wgm = A("wgm", [128, NCH * 16], BF16)
        wf1 = A("wf1", [128, NCH * 64], BF16)
        wf2 = A("wf2", [128, D], BF16)
        wog1 = A("wog1", [128, NCH * 64], BF16)
        wog2 = A("wog2", [128, D], BF16)
        ctu = A("ctu", [128, 128], F32)

        @b.blk.gpsimd
        def _(g):
            chl = lambda t_, w_: bass.AP(t_, 0, [[w_, 128], [128 * w_, NCH], [1, w_]])
            b.lg_dma(g, wq[:, :], chl(b.wq1, D))
            b.lg_dma(g, wk[:, :], chl(b.wk1, D))
            b.lg_dma(g, wv[:, :], chl(b.wv1, D))
            b.lg_dma(g, wo[:, :], chl(b.wo1, D))
            b.lg_dma(g, wgm[:, :], chl(b.wgam, 16))
            b.lg_dma(g, wf1[:, :], chl(b.wf1, 64))
            b.lg_dma(g, wog1[:, :], chl(b.wog1, 64))
            b.lg_dma(g, wf2[0:64, :], b.wf2.ap())
            b.lg_dma(g, wf2[64:128, :], b.wf2.ap())
            b.lg_dma(g, wog2[0:64, :], b.wog2.ap())
            b.lg_dma(g, wog2[64:128, :], b.wog2.ap())
            b.lg_dma(g, ctu[:, :], b.ctriu.ap())
        ld_w = b.s_lg.n

        S_sb = A("S", [128, 8 * 64], BF16)
        S0m = A("S0m", [128, 8 * 64], BF16)
        LrS = A("LrS", [128, 8 * 128], BF16)
        TjS = A("TjS", [128, 8 * 256], BF16)
        usbS = A("usbS", [128, 8 * 64], BF16)
        zeros = A("zeros", [128, 128], BF16)
        mskL = es.enter_context(nc.sbuf_tensor(tag + "mskL", [128, 128],
                                               mybir.dt.uint8))
        mskU = es.enter_context(nc.sbuf_tensor(tag + "mskU", [128, 128],
                                               mybir.dt.uint8))
        mskU1 = es.enter_context(nc.sbuf_tensor(tag + "mskU1", [128, 128],
                                                mybir.dt.uint8))

        @b.blk.gpsimd
        def _(g):
            g.memset(zeros[:, :], 0.0)
            g.memset(S_sb[:, :], 0.0).then_inc(b.s_g.bump(1), 1)
            b.lg_dma(g, mskL[:, :], b.cmskl.ap())
            b.lg_dma(g, mskU[:, :], b.cmsku.ap())
            b.lg_dma(g, mskU1[:, :], b.cmsku1.ap())

        RD = 2
        qf = [A(f"qf{i}", [128, 8 * 128], BF16) for i in range(RD)]
        kcf = [A(f"kcf{i}", [128, 8 * 128], BF16) for i in range(RD)]
        bhf = [A(f"bhf{i}", [128, 8 * 128], BF16) for i in range(RD)]
        kct = [A(f"kct{i}", [128, D], BF16) for i in range(RD)]
        vtm = [A(f"vtm{i}", [128, D], BF16) for i in range(RD)]
        ogt = [A(f"ogt{i}", [128, D], BF16) for i in range(RD)]
        otm = [A(f"otm{i}", [128, D], BF16) for i in range(RD)]
        pml = [A(f"pml{i}", [128, 24], F32) for i in range(RD)]
        xln2 = A("xln2", [128, D], BF16)
        fmr2 = A("fmr2", [128, D], BF16)
        emf = fmr2
        yfm = xln2
        qs_tm2 = [A(f"qs{i}", [128, D], BF16) for i in range(2)]
        kr_tm2 = [A(f"kr{i}", [128, D], BF16) for i in range(2)]
        sp_ = A("sp", [128, D], F32)
        cs_ = A("cs", [128, D], F32)
        mid_ = A("mid", [1, D], F32)
        gam2 = [A(f"gam{i}", [128, 16], F32) for i in range(2)]
        t12 = A("t12", [128, 128], BF16)
        t12f = A("t12f", [128, 128], BF16)
        o12 = A("o12", [128, 128], BF16)
        o12f = A("o12f", [128, 128], BF16)

        rk2 = A("rk2", [128, 16], F32)
        st4 = b.STATS[:, 640:680]

        with contextlib.ExitStack() as pes:
            P = lambda n, s, dt=F32: pes.enter_context(
                nc.psum_tensor(tag + n, s, dt))
            pPJ = P("pPJ", [128, 1024])
            pSQ = P("pSQ", [128, 1024])
            pTT = P("pTT", [128, 1024])
            pUU = P("pUU", [128, 512])
            pB = P("pB", [128, 128], BF16)
            pCS = pPJ

            def fmc(sub, c):
                return fmr2[:, c * 128:(c + 1) * 128]

            def pre(pair):
                for sub in range(2):
                    ti = 2 * pair + sub
                    rd = ti % RD

                    @b.blk.vector
                    def _(v, ti=ti, sub=sub):
                        v.wait_ge(b.s_mm.h, b.s_mm.n)
                        if sub == 1:
                            v.wait_ge(b.s_tr.h, b.s_tr.n)
                        v.tensor_scalar(xln2[:, :],
                                        b.xres_tile(ti), r_[:, ti:ti + 1],
                                        nmr[:, ti:ti + 1], ALU.mult, ALU.add)
                        v.drain().then_inc(b.s_v.bump(1), 1)
                    v_ln = b.s_v.n

                    @b.blk.sync
                    def _(sy, sub=sub, v_ln=v_ln):
                        sy.wait_ge(b.s_v.h, v_ln)
                        for c in range(NCH):
                            sy.dma_start_transpose(
                                fmc(sub, c),
                                xln2[:, c * 128:(c + 1) * 128],
                            ).then_inc(b.s_tr.bump(16), 16)
                        sy.wait_ge(b.s_tr.h, b.s_tr.n)
                    tr_f = b.s_tr.n

                    @b.blk.tensor
                    def _(t, ti=ti, sub=sub, tr_f=tr_f):
                        if ti == 0:
                            t.wait_ge(b.s_lg.h, ld_w)
                            t.wait_ge(b.s_v.h, b.s_v.n)
                        t.wait_ge(b.s_tr.h, tr_f)
                        t.wait_ge(b.s_a.h, b.s_a.n)
                        t.wait_ge(b.s_v.h, b.s_v.n)
                        for half in range(2):
                            for c in range(NCH):
                                mmi = t.matmul(pPJ[:, half * 512:(half + 1) * 512],
                                               fmc(sub, c),
                                               wq[:, c * D + half * 512:
                                                  c * D + (half + 1) * 512],
                                               start=(c == 0), stop=(c == NCH - 1))
                        mmi.then_inc(b.s_mm.bump(1), 1)
                    mm_q = b.s_mm.n

                    @b.blk.scalar
                    def _(s, sub=sub, mm_q=mm_q):
                        s.wait_ge(b.s_mm.h, mm_q)
                        s.activation(qs_tm2[sub][:, :], pPJ[:, :],
                                     AF.Silu).then_inc(b.s_a.bump(1), 1)
                    a_q = b.s_a.n

                    @b.blk.tensor
                    def _(t, sub=sub, a_q=a_q):
                        t.wait_ge(b.s_a.h, a_q)
                        for half in range(2):
                            for c in range(NCH):
                                t.matmul(pPJ[:, half * 512:(half + 1) * 512],
                                         fmc(sub, c),
                                         wk[:, c * D + half * 512:
                                            c * D + (half + 1) * 512],
                                         start=(c == 0), stop=(c == NCH - 1))
                        for c in range(NCH):
                            t.matmul(pUU[:, 0:64], fmc(sub, c),
                                     wf1[:, c * 64:(c + 1) * 64],
                                     start=(c == 0), stop=(c == NCH - 1))
                        for c in range(NCH):
                            t.matmul(pUU[:, 64:128], fmc(sub, c),
                                     wog1[:, c * 64:(c + 1) * 64],
                                     start=(c == 0), stop=(c == NCH - 1))
                        for c in range(NCH):
                            mmi = t.matmul(pUU[:, 128:144], fmc(sub, c),
                                           wgm[:, c * 16:(c + 1) * 16],
                                           start=(c == 0), stop=(c == NCH - 1))
                        mmi.then_inc(b.s_mm.bump(1), 1)
                    mm_p1 = b.s_mm.n

                    @b.blk.scalar
                    def _(s, sub=sub, mm_p1=mm_p1):
                        s.wait_ge(b.s_mm.h, mm_p1)
                        s.activation(kr_tm2[sub][:, :], pPJ[:, :], AF.Silu)
                        s.drain()
                        s.activation(gam2[sub][:, :], pUU[:, 128:144], AF.Sigmoid)
                        s.drain().then_inc(b.s_a.bump(1), 1)
                    a_p1 = b.s_a.n

                    @b.blk.vector
                    def _(v, ti=ti, sub=sub, rd=rd, a_p1=a_p1, mm_p1=mm_p1):
                        v.wait_ge(b.s_a.h, a_p1)
                        v.tensor_copy(t12[:, sub * 64:(sub + 1) * 64],
                                      pUU[:, 0:64])
                        v.drain()
                        v.tensor_copy(o12[:, sub * 64:(sub + 1) * 64],
                                      pUU[:, 64:128])
                        v.drain()
                        v.tensor_scalar_mul(gam2[sub][:, :], gam2[sub][:, :], -1.0)
                        v.drain()
                        ksq2 = otm[rd]
                        v.tensor_mul(ksq2[:, :], kr_tm2[sub][:, :], kr_tm2[sub][:, :])
                        v.drain()
                        kv3 = bass.AP(ksq2, 0, [[ksq2[:, :].ap[0][0], 128],
                                                [64, 16], [1, 64]])
                        v.tensor_reduce(rk2[:, :], kv3, axis=AXX, op=ALU.add)
                        v.drain().then_inc(b.s_v.bump(1), 1)

                    @b.blk.scalar
                    def _(s):
                        s.wait_ge(b.s_v.h, b.s_v.n)
                        s.activation(rk2[:, :], rk2[:, :], AF.Sqrt,
                                     bias=b.ceps[:, 1:2])
                        s.drain().then_inc(b.s_a.bump(1), 1)

                    @b.blk.vector
                    def _(v, rd=rd, sub=sub):
                        v.wait_ge(b.s_a.h, b.s_a.n)
                        v.reciprocal(rk2[:, :], rk2[:, :])
                        v.drain()
                        rb = bass.AP(rk2, 0, [[rk2[:, :].ap[0][0], 128],
                                              [1, 16], [0, 64]])
                        kv4 = bass.AP(kr_tm2[sub], 0,
                                      [[kr_tm2[sub][:, :].ap[0][0], 128],
                                       [64, 16], [1, 64]])
                        v.tensor_tensor(kv4, kv4, rb, ALU.mult)
                        v.drain()
                        v.tensor_copy(kct[rd][:, :], kr_tm2[sub][:, :])
                        v.drain().then_inc(b.s_v.bump(1), 1)
                    v_krdy = b.s_v.n

                    @b.blk.tensor
                    def _(t, sub=sub, v_krdy=v_krdy):
                        t.wait_ge(b.s_v.h, v_krdy)
                        for half in range(2):
                            for c in range(NCH):
                                mmi = t.matmul(pPJ[:, half * 512:(half + 1) * 512],
                                               fmc(sub, c),
                                               wv[:, c * D + half * 512:
                                                  c * D + (half + 1) * 512],
                                               start=(c == 0), stop=(c == NCH - 1))
                        mmi.then_inc(b.s_mm.bump(1), 1)
                    mm_v = b.s_mm.n

                    @b.blk.vector
                    def _(v, rd=rd, mm_v=mm_v):
                        v.wait_ge(b.s_mm.h, mm_v)
                        v.tensor_copy(vtm[rd][:, :], pPJ[:, :])
                        v.drain().then_inc(b.s_v.bump(1), 1)
                    v_v = b.s_v.n

                @b.blk.sync
                def _(sy):
                    sy.wait_ge(b.s_v.h, b.s_v.n)
                    sy.dma_start_transpose(t12f[:, :], t12[:, :]
                                           ).then_inc(b.s_tr.bump(16), 16)
                    sy.dma_start_transpose(o12f[:, :], o12[:, :]
                                           ).then_inc(b.s_tr.bump(16), 16)
                    sy.wait_ge(b.s_tr.h, b.s_tr.n)
                tr_t12 = b.s_tr.n

                for sub in range(2):
                    ti = 2 * pair + sub
                    rd = ti % RD
                    bs = sub * 64

                    @b.blk.tensor
                    def _(t, sub=sub, bs=bs, tr_t12=tr_t12):
                        t.wait_ge(b.s_tr.h, tr_t12)
                        t.wait_ge(b.s_v.h, b.s_v.n)
                        t.wait_ge(b.s_a.h, b.s_a.n)
                        for half in range(2):
                            mmi = t.matmul(pPJ[:, half * 512:(half + 1) * 512],
                                           t12f[bs:bs + 64, :],
                                           wf2[bs:bs + 64, half * 512:
                                               (half + 1) * 512],
                                           start=True, stop=True)
                        mmi.then_inc(b.s_mm.bump(1), 1)
                    mm_z = b.s_mm.n

                    @b.blk.scalar
                    def _(s, mm_z=mm_z):
                        s.wait_ge(b.s_mm.h, mm_z)
                        s.activation(sp_[:, :], pPJ[:, :], AF.Exp, scale=-1.0)
                        s.drain()
                        s.activation(sp_[:, :], sp_[:, :], AF.Ln, bias=1.0)
                        s.drain().then_inc(b.s_a.bump(1), 1)
                    a_sp = b.s_a.n

                    v_cs = None
                    for half in range(2):
                        @b.blk.tensor
                        def _(t, half=half, a_sp=a_sp):
                            t.wait_ge(b.s_a.h, a_sp)
                            if half == 1:
                                t.wait_ge(b.s_v.h, b.s_v.n)
                            t.matmul(pCS[:, 0:512], ctu[:, :],
                                     sp_[:, half * 512:(half + 1) * 512],
                                     start=True, stop=True
                                     ).then_inc(b.s_mm.bump(1), 1)
                        mm_csh = b.s_mm.n

                        @b.blk.vector
                        def _(v, half=half, mm_csh=mm_csh):
                            v.wait_ge(b.s_mm.h, mm_csh)
                            v.tensor_copy(cs_[:, half * 512:(half + 1) * 512],
                                          pCS[:, 0:512])
                            v.drain().then_inc(b.s_v.bump(1), 1)
                    v_cs = b.s_v.n

                    @b.blk.scalar
                    def _(s, rd=rd, v_cs=v_cs):
                        s.wait_ge(b.s_v.h, v_cs)
                        s.activation(ogt[rd][:, :], cs_[:, :], AF.Exp, scale=-1.0)
                        s.drain().then_inc(b.s_a.bump(1), 1)
                    a_emm = b.s_a.n

                    @b.blk.gpsimd
                    def _(g, v_cs=v_cs):
                        g.wait_ge(b.s_v.h, v_cs)
                        g.dma_start(out=mid_[0:1, :], in_=cs_[63:64, :]
                                    ).then_inc(b.s_ea.bump(16), 16)
                    g_mid = b.s_ea.n

                    mmh = []
                    for half in range(2):
                        @b.blk.tensor
                        def _(t, half=half, g_mid=g_mid):
                            t.wait_ge(b.s_ea.h, g_mid)
                            if half == 1:
                                t.wait_ge(b.s_v.h, b.s_v.n)
                            t.matmul(pCS[:, 0:512], b.onesf[0:1, :],
                                     mid_[0:1, half * 512:(half + 1) * 512],
                                     start=True, stop=True
                                     ).then_inc(b.s_mm.bump(1), 1)
                        mmh.append(b.s_mm.n)

                        @b.blk.vector
                        def _(v, half=half, a_emm=a_emm, mmh=tuple(mmh)):
                            v.wait_ge(b.s_mm.h, mmh[-1])
                            if half == 0:
                                v.wait_ge(b.s_a.h, a_emm)
                            v.tensor_sub(cs_[:, half * 512:(half + 1) * 512],
                                         cs_[:, half * 512:(half + 1) * 512],
                                         pCS[:, 0:512])
                            v.drain().then_inc(b.s_v.bump(1), 1)
                    mm_mid = b.s_mm.n

                    v_sc = b.s_v.n

                    @b.blk.scalar
                    def _(s, v_sc=v_sc):
                        s.wait_ge(b.s_v.h, v_sc)
                        s.activation(pPJ[:, :], cs_[:, :], AF.Exp,
                                     scale=-1.0).then_inc(b.s_a.bump(1), 1)
                    a_lam = b.s_a.n

                    @b.blk.vector
                    def _(v, rd=rd, sub=sub, a_lam=a_lam):
                        v.wait_ge(b.s_a.h, a_lam)
                        v.tensor_mul(qs_tm2[sub][:, :], qs_tm2[sub][:, :],
                                     pPJ[:, :])
                        v.drain().then_inc(b.s_v.bump(1), 1)
                    v_qt = b.s_v.n

                    @b.blk.scalar
                    def _(s, v_qt=v_qt):
                        s.wait_ge(b.s_v.h, v_qt)
                        s.activation(pPJ[:, :], cs_[:, :],
                                     AF.Exp).then_inc(b.s_a.bump(1), 1)
                    a_en = b.s_a.n

                    @b.blk.vector
                    def _(v, rd=rd, a_en=a_en):
                        v.wait_ge(b.s_a.h, a_en)
                        v.tensor_mul(kct[rd][:, :], kct[rd][:, :], pPJ[:, :])
                        v.drain()
                        v.tensor_add(sp_[:, :], sp_[:, :], cs_[:, :])
                        v.drain().then_inc(b.s_v.bump(1), 1)
                    v_kap = b.s_v.n

                    @b.blk.scalar
                    def _(s, rd=rd, v_kap=v_kap):
                        s.wait_ge(b.s_v.h, v_kap)
                        s.activation(ogt[rd][96:128, :], cs_[96:128, :],
                                     AF.Exp, scale=-1.0)
                        s.drain()
                        s.activation(pPJ[:, :], sp_[:, :], AF.Exp,
                                     scale=-1.0).then_inc(b.s_a.bump(1), 1)
                    a_e1 = b.s_a.n

                    @b.blk.vector
                    def _(v, rd=rd, sub=sub, a_e1=a_e1):
                        v.wait_ge(b.s_a.h, a_e1)
                        v.tensor_mul(kr_tm2[sub][:, :], kr_tm2[sub][:, :],
                                     pPJ[:, :])
                        v.drain()
                        gb = bass.AP(gam2[sub], 0, [[gam2[sub][:, :].ap[0][0], 128],
                                                    [1, 16], [0, 64]])
                        kv5 = bass.AP(kr_tm2[sub], 0,
                                      [[kr_tm2[sub][:, :].ap[0][0], 128],
                                       [64, 16], [1, 64]])
                        v.tensor_tensor(kv5, kv5, gb, ALU.mult)
                        v.drain().then_inc(b.s_v.bump(1), 1)
                    v_drv = b.s_v.n

                    @b.blk.sync
                    def _(sy, rd=rd, sub=sub, v_drv=v_drv):
                        sy.wait_ge(b.s_v.h, v_drv)
                        for c in range(NCH):
                            sy.dma_start_transpose(
                                qf[rd][:, c * 128:(c + 1) * 128],
                                qs_tm2[sub][:, c * 128:(c + 1) * 128],
                            ).then_inc(b.s_tr.bump(16), 16)
                        for c in range(NCH):
                            sy.dma_start_transpose(
                                kcf[rd][:, c * 128:(c + 1) * 128],
                                kct[rd][:, c * 128:(c + 1) * 128],
                            ).then_inc(b.s_tr.bump(16), 16)
                        for c in range(NCH):
                            sy.dma_start_transpose(
                                bhf[rd][:, c * 128:(c + 1) * 128],
                                kr_tm2[sub][:, c * 128:(c + 1) * 128],
                            ).then_inc(b.s_tr.bump(16), 16)
                        for c in range(NCH):
                            sy.dma_start_transpose(
                                emf[:, c * 128:(c + 1) * 128],
                                ogt[rd][:, c * 128:(c + 1) * 128],
                            ).then_inc(b.s_tr.bump(16), 16)
                        sy.wait_ge(b.s_tr.h, b.s_tr.n)
                    tr_drv = b.s_tr.n

                    @b.blk.vector
                    def _(v, rd=rd, tr_drv=tr_drv):
                        v.wait_ge(b.s_tr.h, tr_drv)
                        p63 = bass.AP(emf, 63, [[emf[:, :].ap[0][0], 128], [128, 8]])
                        p127 = bass.AP(emf, 127, [[emf[:, :].ap[0][0], 128], [128, 8]])
                        v.tensor_copy(pml[rd][:, 0:8], p63)
                        v.drain()
                        v.tensor_copy(pml[rd][:, 8:16], p127)
                        v.drain().then_inc(b.s_v.bump(1), 1)

                    @b.blk.tensor
                    def _(t, sub=sub, bs=bs):
                        t.wait_ge(b.s_v.h, b.s_v.n)
                        for half in range(2):
                            mmi = t.matmul(pPJ[:, half * 512:(half + 1) * 512],
                                           o12f[bs:bs + 64, :],
                                           wog2[bs:bs + 64, half * 512:
                                                (half + 1) * 512],
                                           start=True, stop=True)
                        mmi.then_inc(b.s_mm.bump(1), 1)
                    mm_og = b.s_mm.n

                    @b.blk.scalar
                    def _(s, rd=rd, mm_og=mm_og):
                        s.wait_ge(b.s_mm.h, mm_og)
                        s.activation(ogt[rd][:, :], pPJ[:, :], AF.Sigmoid)
                        s.drain().then_inc(b.s_a.bump(1), 1)

            def unit_group(ti, g):
                rd = ti % RD
                hs = [g * 8 + i for i in range(8)]

                def sl_bh(h):
                    return (h % 2) * 64

                def sl_S(h, t_):
                    bh = sl_bh(h)
                    return t_[bh:bh + 64, (h // 2) * 64:(h // 2) * 64 + 64]

                def sl_fm(h, t_):
                    bh = sl_bh(h)
                    return t_[bh:bh + 64, (h // 2) * 128:(h // 2) * 128 + 128]

                def cSQ(i):
                    return pSQ[:, i * 128:(i + 1) * 128]

                def cTT(i):
                    return pTT[:, i * 128:(i + 1) * 128]

                def cUU(i):
                    return pUU[:, i * 64:(i + 1) * 64]

                def cLr(i):
                    return LrS[:, i * 128:(i + 1) * 128]

                def cTj(i, j):
                    o = (i * 2 + (j & 1)) * 128
                    return TjS[:, o:o + 128]

                def cu(i):
                    return usbS[:, i * 64:(i + 1) * 64]

                @b.blk.vector
                def _(v):
                    v.wait_ge(b.s_v.h, b.s_v.n)
                    for i, h in enumerate(hs):
                        pm = pml[rd][sl_bh(h):sl_bh(h) + 64, h // 2:h // 2 + 1]
                        v.tensor_scalar(sl_S(h, S0m), sl_S(h, S_sb), pm, None,
                                        ALU.mult)
                    v.drain().then_inc(b.s_v.bump(1), 1)
                v_s0m = b.s_v.n
                a_prev = b.s_a.n
                if DBG_UG <= 0:
                    return

                pA_s = [pSQ[:, 0:128], pSQ[:, 512:640]]
                pU_s = [pUU[:, 0:64], pTT[:, 0:64]]
                mmB = {}
                vB = {-2: b.s_v.n, -1: b.s_v.n}
                aT0 = {-2: a_prev, -1: a_prev}
                mmT0 = {}
                for i, h in enumerate(hs):
                    s = i % 2

                    @b.blk.tensor
                    def _(t, i=i, h=h, s=s):
                        if i == 0:
                            t.wait_ge(b.s_v.h, v_s0m)
                            t.wait_ge(b.s_a.h, a_prev)
                        else:
                            t.wait_ge(b.s_v.h, vB[i - 2])
                        bf_ = sl_fm(h, bhf[rd])
                        kf_ = sl_fm(h, kcf[rd])
                        t.matmul(pU_s[s], bf_, sl_S(h, S0m),
                                 start=True, stop=True)
                        t.matmul(pA_s[s][:, 0:64], bf_, kf_[:, 0:64],
                                 start=True, stop=True)
                        mmi = t.matmul(pA_s[s][64:128, 64:128], bf_[:, 64:128],
                                       kf_[:, 64:128], start=True, stop=True,
                                       tile_position=(sl_bh(h), 64))
                        mmi.then_inc(b.s_mm.bump(1), 1)
                    mmB[i] = b.s_mm.n

                    @b.blk.vector
                    def _(v, i=i, h=h, s=s):
                        v.wait_ge(b.s_mm.h, mmB[i])
                        v.tensor_add(cu(i), vtm[rd][:, h * 64:(h + 1) * 64],
                                     pU_s[s])
                        v.drain()
                        v.select(cLr(i)[:, 0:64], mskL[:, 0:64],
                                 pA_s[s][:, 0:64], zeros[:, 0:64])
                        v.select(cLr(i)[64:128, 64:128], mskL[64:128, 64:128],
                                 pA_s[s][64:128, 64:128],
                                 zeros[64:128, 64:128])
                        v.memset(cLr(i)[0:64, 64:128], 0.0)
                        v.drain().then_inc(b.s_v.bump(1), 1)
                    vB[i] = b.s_v.n

                    @b.blk.tensor
                    def _(t, i=i):
                        t.wait_ge(b.s_v.h, vB[i])
                        t.wait_ge(b.s_a.h, aT0[i - 1])
                        t.transpose(pB[:, :], cLr(i)[:, :], b.ident[:, :]
                                    ).then_inc(b.s_mm.bump(1), 1)
                    mmT0[i] = b.s_mm.n

                    @b.blk.scalar
                    def _(s_, i=i):
                        s_.wait_ge(b.s_mm.h, mmT0[i])
                        s_.activation(cTj(i, 0), pB[:, :], AF.Copy)
                        s_.drain().then_inc(b.s_a.bump(1), 1)
                    aT0[i] = b.s_a.n
                v_ev0 = b.s_v.n
                a_ev0 = b.s_a.n
                if DBG_UG <= 2:
                    return

                aT = {0: a_ev0}
                vL = {0: v_ev0}
                mmA = {}
                vA = {}

                @b.blk.tensor
                def _(t):
                    t.wait_ge(b.s_v.h, v_ev0)
                    t.wait_ge(b.s_a.h, a_ev0)
                    for i in range(8):
                        mmi = t.matmul(cUU(i), cTj(i, 0), cu(i),
                                       start=True, stop=True)
                    mmi.then_inc(b.s_mm.bump(1), 1)
                mmA[0] = b.s_mm.n

                @b.blk.vector
                def _(v):
                    v.wait_ge(b.s_mm.h, mmA[0])
                    for i in range(8):
                        v.tensor_add(cu(i), cu(i), cUU(i))
                    v.drain().then_inc(b.s_v.bump(1), 1)
                vA[0] = b.s_v.n
                if DBG_UG <= 3:
                    return

                for j in range(1, 7):
                    @b.blk.tensor
                    def _(t, j=j):
                        t.wait_ge(b.s_a.h, aT[j - 1])
                        t.wait_ge(b.s_v.h, vL[j - 1])
                        for i in range(8):
                            if j == 6:
                                mmi = t.matmul(cTT(i)[0:64, 64:128],
                                               cLr(i)[:, 0:64],
                                               cTj(i, 5)[:, 64:128],
                                               start=True, stop=True)
                            else:
                                mmi = t.matmul(cTT(i), cLr(i), cTj(i, j - 1),
                                               start=True, stop=True)
                        mmi.then_inc(b.s_mm.bump(1), 1)
                    mmT = b.s_mm.n

                    if j < 6:
                        @b.blk.tensor
                        def _(t, j=j):
                            for i in range(8):
                                mmi = t.matmul(cSQ(i), cTj(i, j - 1), cLr(i),
                                               start=True, stop=True)
                            mmi.then_inc(b.s_mm.bump(1), 1)
                        mmS = b.s_mm.n

                    @b.blk.scalar
                    def _(s, j=j, mmT=mmT):
                        s.wait_ge(b.s_mm.h, mmT)
                        for i in range(8):
                            if j == 6:
                                s.activation(cTj(i, 6)[0:64, 64:128],
                                             cTT(i)[0:64, 64:128], AF.Copy)
                            else:
                                s.activation(cTj(i, j), cTT(i), AF.Copy)
                        s.drain().then_inc(b.s_a.bump(1), 1)
                    aT[j] = b.s_a.n

                    if j < 6:
                        @b.blk.vector
                        def _(v, j=j, mmS=mmS):
                            v.wait_ge(b.s_mm.h, mmS)
                            for i in range(8):
                                v.tensor_copy(cLr(i), cSQ(i))
                            v.drain().then_inc(b.s_v.bump(1), 1)
                        vL[j] = b.s_v.n
                    else:
                        vL[j] = vL[j - 1]

                    @b.blk.tensor
                    def _(t, j=j):
                        t.wait_ge(b.s_a.h, aT[j])
                        t.wait_ge(b.s_v.h, vA[j - 1])
                        for i in range(8):
                            if j == 6:
                                mmi = t.matmul(cUU(i)[64:128, :],
                                               cTj(i, 6)[0:64, 64:128],
                                               cu(i)[0:64, :],
                                               start=True, stop=True,
                                               tile_position=(0, 64))
                            else:
                                mmi = t.matmul(cUU(i), cTj(i, j), cu(i),
                                               start=True, stop=True)
                        mmi.then_inc(b.s_mm.bump(1), 1)
                    mmA[j] = b.s_mm.n

                    @b.blk.vector
                    def _(v, j=j):
                        v.wait_ge(b.s_mm.h, mmA[j])
                        for i in range(8):
                            if j == 6:
                                v.tensor_add(cu(i)[64:128, :], cu(i)[64:128, :],
                                             cUU(i)[64:128, :])
                            else:
                                v.tensor_add(cu(i), cu(i), cUU(i))
                        v.drain().then_inc(b.s_v.bump(1), 1)
                    vA[j] = b.s_v.n

                if DBG_UG <= 4:
                    return

                vPT = {}
                mmO1 = {}
                vO = {-2: vA[6], -1: vA[6]}
                aT6 = aT[6]
                for i, h in enumerate(hs):
                    s = i % 2

                    @b.blk.tensor
                    def _(t, i=i, h=h, s=s):
                        kf_ = sl_fm(h, kcf[rd])
                        qf_ = sl_fm(h, qf[rd])
                        if i == 0:
                            t.wait_ge(b.s_v.h, vA[6])
                            t.wait_ge(b.s_a.h, aT6)
                        else:
                            t.wait_ge(b.s_v.h, vO[i - 2])
                        t.matmul(pA_s[s][0:64, :], kf_[:, 0:64], qf_,
                                 start=True, stop=True)
                        t.matmul(pA_s[s][64:128, 64:128], kf_[:, 64:128],
                                 qf_[:, 64:128], start=True, stop=True,
                                 tile_position=(sl_bh(h), 64))
                        mmi = t.matmul(pU_s[s], qf_, sl_S(h, S0m),
                                       start=True, stop=True)
                        mmi.then_inc(b.s_mm.bump(1), 1)
                    mmPT = b.s_mm.n

                    @b.blk.vector
                    def _(v, i=i, h=h, s=s, mmPT=mmPT):
                        v.wait_ge(b.s_mm.h, mmPT)
                        v.select(cLr(i)[0:64, :], mskU[0:64, :],
                                 pA_s[s][0:64, :], zeros[0:64, :])
                        v.select(cLr(i)[64:128, 64:128], mskU[64:128, 64:128],
                                 pA_s[s][64:128, 64:128],
                                 zeros[64:128, 64:128])
                        v.memset(cLr(i)[64:128, 0:64], 0.0)
                        v.drain()
                        v.tensor_copy(otm[rd][:, h * 64:(h + 1) * 64], pU_s[s])
                        v.drain().then_inc(b.s_v.bump(1), 1)
                    vPT[i] = b.s_v.n

                    @b.blk.tensor
                    def _(t, i=i, h=h, s=s):
                        t.wait_ge(b.s_v.h, vPT[i])
                        t.matmul(cTT(i)[:, 64:128], cLr(i), cu(i),
                                 start=True, stop=True)
                        bh = sl_bh(h)
                        mmi = t.matmul(pU_s[s][bh:bh + 64, :] if bh else
                                       pU_s[s][0:64, :],
                                       kct[rd][:, h * 64:(h + 1) * 64], cu(i),
                                       start=True, stop=True,
                                       tile_position=(0, bh) if bh else None)
                        mmi.then_inc(b.s_mm.bump(1), 1)
                    mmO1[i] = b.s_mm.n

                    @b.blk.vector
                    def _(v, i=i, h=h, s=s):
                        v.wait_ge(b.s_mm.h, mmO1[i])
                        osl = otm[rd][:, h * 64:(h + 1) * 64]
                        v.tensor_add(osl, osl, cTT(i)[:, 64:128])
                        v.drain()
                        bh = sl_bh(h)
                        v.tensor_add(sl_S(h, S0m), sl_S(h, S0m),
                                     pU_s[s][bh:bh + 64, :])
                        v.drain()
                        lc = pml[rd][bh:bh + 64, 8 + h // 2:9 + h // 2]
                        v.tensor_scalar(sl_S(h, S_sb), sl_S(h, S0m), lc, None,
                                        ALU.mult)
                        v.drain().then_inc(b.s_v.bump(1), 1)
                    vO[i] = b.s_v.n

            def post(ti):
                rd = ti % RD
                bnB = b.STATS[:, 640:664]
                mvB = b.STATS[:, 664:666]
                rB = b.STATS[:, 666:667]
                nmB = b.STATS[:, 667:668]
                varB = bass.AP(b.STATS, 665, [[768, 128], [1, 1]])
                muB = bass.AP(b.STATS, 664, [[768, 128], [1, 1]])

                @b.blk.vector
                def _(v):
                    v.wait_ge(b.s_v.h, b.s_v.n)
                    v.wait_ge(b.s_a.h, b.s_a.n)
                    v.tensor_mul(otm[rd][:, :], otm[rd][:, :], ogt[rd][:, :])
                    v.drain()
                    for hh in range(2):
                        v.bn_stats(bnB[:, hh * 6:hh * 6 + 6],
                                   otm[rd][:, hh * 512:(hh + 1) * 512])
                    v.drain()
                    v.bn_aggr(mvB[:, :], bnB[:, 0:12])
                    v.drain().then_inc(b.s_v.bump(1), 1)

                @b.blk.scalar
                def _(s):
                    s.wait_ge(b.s_v.h, b.s_v.n)
                    s.activation(rB[:, :], varB, AF.Sqrt, bias=b.ceps[:, 0:1])
                    s.drain().then_inc(b.s_a.bump(1), 1)

                @b.blk.vector
                def _(v):
                    v.wait_ge(b.s_a.h, b.s_a.n)
                    v.reciprocal(rB[:, :], rB[:, :])
                    v.drain()
                    v.scalar_tensor_tensor(nmB[:, :], muB, -1.0, rB[:, :],
                                           ALU.mult, ALU.mult)
                    v.drain()
                    v.tensor_scalar(otm[rd][:, :], otm[rd][:, :], rB[:, 0:1],
                                    nmB[:, 0:1], ALU.mult, ALU.add)
                    v.drain()
                    v.tensor_mul(otm[rd][:, :], otm[rd][:, :], b.gain1[:, :])
                    v.drain().then_inc(b.s_v.bump(1), 1)
                v_y = b.s_v.n

                @b.blk.sync
                def _(sy, v_y=v_y):
                    sy.wait_ge(b.s_v.h, v_y)
                    for c in range(NCH):
                        sy.dma_start_transpose(
                            yfm[:, c * 128:(c + 1) * 128],
                            otm[rd][:, c * 128:(c + 1) * 128],
                        ).then_inc(b.s_tr.bump(16), 16)
                    sy.wait_ge(b.s_tr.h, b.s_tr.n)
                tr_y = b.s_tr.n

                @b.blk.tensor
                def _(t, tr_y=tr_y):
                    t.wait_ge(b.s_tr.h, tr_y)
                    t.wait_ge(b.s_v.h, b.s_v.n)
                    for half in range(2):
                        for c in range(NCH):
                            mmi = t.matmul(pPJ[:, half * 512:(half + 1) * 512],
                                           yfm[:, c * 128:(c + 1) * 128],
                                           wo[:, c * D + half * 512:
                                              c * D + (half + 1) * 512],
                                           start=(c == 0), stop=(c == NCH - 1))
                        mmi.then_inc(b.s_mm.bump(1), 1)
                mm_wo = b.s_mm.n

                @b.blk.vector
                def _(v, ti=ti, mm_wo=mm_wo):
                    v.wait_ge(b.s_mm.h, mm_wo)
                    for half in range(2):
                        xs = b.XRES[:, ti * D + half * 512: ti * D + (half + 1) * 512]
                        v.tensor_add(xs, xs, pPJ[:, half * 512:(half + 1) * 512])
                        v.drain()
                    v.drain().then_inc(b.s_v.bump(1), 1)

            for pair in range(NT // 2):
                pre(pair)
                for sub in range(2):
                    ti = 2 * pair + sub
                    for g in range(2):
                        unit_group(ti, g)
                    if DBG_L1A:
                        @b.blk.gpsimd
                        def _(g, ti=ti):
                            b.barrier_all(g)
                            g.dma_start(
                                out=bass.AP(b.e_tm, ti * 128 * NSLOT,
                                            [[NSLOT, 128], [1, D]]),
                                in_=otm[ti % RD][:, :]
                            ).then_inc(b.s_st.bump(16), 16)
                            g.dma_start(
                                out=bass.AP(b.e_sm, ti * 128 * NSLOT,
                                            [[NSLOT, 128], [1, D]]),
                                in_=ogt[ti % RD][:, :]
                            ).then_inc(b.s_st.bump(16), 16)
                            g.wait_ge(b.s_st.h, b.s_st.n)

                        @b.blk.vector
                        def _(v):
                            v.wait_ge(b.s_st.h, b.s_st.n)
                    post(ti)


SUBNAMES = ["l0ffn_a", "l0attn", "l0ffn_b", "l1ffn_a", "l1attn", "l1ffn_b"]


def build_nc(first=0, last=5):
    nc = bass.Bass("TRN2", target_bir_lowering=False,
                   detect_race_conditions=False)
    b = Builder(nc)
    b.declare_io()
    with contextlib.ExitStack() as es:
        b.alloc_persistent(es)
        with nc.Block() as blk:
            b.blk = blk
            b.load_initial()
            for k in range(first, last + 1):
                name = SUBNAMES[k]
                with nc.named_scope(f"s{k}_{name}"):
                    if name in ("l0ffn_a", "l0ffn_b"):
                        b.emit_l0ffn(tag=f"l0f{k}_")
                    elif name == "l0attn":
                        b.emit_l0attn()
                    elif name in ("l1ffn_a", "l1ffn_b"):
                        b.emit_l1ffn(tag=f"l1f{k}_", final=(k == 5))
                    elif name == "l1attn":
                        b.emit_l1attn()
            b.store_out()
    return nc


def make_consts():
    mask64 = np.zeros((128, 16 * 64), np.float32)
    for c in range(16):
        for p in range(128):
            mask64[p, c * 64 + (4 * c + p // 32)] = 1.0
    maskp = np.zeros((128, PSL), np.float32)
    for p in range(128):
        maskp[p, p % PSL] = 1.0
    tri = np.tril(np.ones((128, 128), np.float32))
    ident = np.eye(128, dtype=np.float32)
    triu = np.triu(np.ones((128, 128), np.float32))
    return mask64, maskp, tri, ident, triu


def stage_inputs(inputs, core):
    i = inputs
    mask64, maskp, tri, ident, triu = make_consts()
    C = np.ascontiguousarray
    return {
        "x0": C(i["L0_attn_x"][core]),
        "gw": C(i["L1_ffn_gate_w"]),
        "gb": C(i["L1_ffn_gate_b"].reshape(1, NEXP1)),
        "l1ew": C(i["L1_ffn_expert_w"]),
        "l1eb": C(i["L1_ffn_expert_b"]),
        "phiwT": C(i["L0_ffn_phi_w"].T),
        "phib": C(i["L0_ffn_phi_b"].reshape(1, NSLOT)),
        "l0ew": C(i["L0_ffn_expert_w"][8 * core:8 * core + 8]),
        "l0eb": C(i["L0_ffn_expert_b"][8 * core:8 * core + 8]),
        "l0ng": C(i["L0_attn_norm_g"].reshape(1, D)),
        "l0og": C(i["L0_attn_out_norm_g"].reshape(1, D)),
        "wq0": C(i["L0_attn_Wq"]),
        "wkv0": C(i["L0_attn_Wkv"]),
        "wout0": C(i["L0_attn_Wout"]),
        "wq1": C(i["L1_attn_Wq"]),
        "wk1": C(i["L1_attn_Wk"]),
        "wv1": C(i["L1_attn_Wv"]),
        "wo1": C(i["L1_attn_Wo"]),
        "wgam": C(i["L1_attn_Wgamma"]),
        "wf1": C(i["L1_attn_Wf1"]),
        "wf2": C(i["L1_attn_Wf2"]),
        "wog1": C(i["L1_attn_Wog1"]),
        "wog2": C(i["L1_attn_Wog2"]),
        "l1ng": C(i["L1_attn_norm_w"].reshape(1, D)),
        "cmask64": mask64,
        "cmaskp": maskp,
        "ctri": tri,
        "cident": ident,
        "ctriu": triu,
        "cmskl": np.tril(np.ones((128, 128), np.uint8), -1),
        "cmsku": np.triu(np.ones((128, 128), np.uint8)),
        "cmsku1": np.triu(np.ones((128, 128), np.uint8), 1),
    }


def kernel(**inputs):
    nc = build_nc(0, 5)
    in_maps = [stage_inputs(inputs, c) for c in range(NCORES)]
    res = run_bass_kernel_spmd(nc, in_maps, core_ids=list(range(NCORES)))
    return np.stack([res.results[c]["out"] for c in range(NCORES)], axis=0)


# version 07 (final)
# speedup vs baseline: 1.3663x; 1.0065x over previous
import contextlib
import numpy as np
import concourse.bass as bass
import concourse.mybir as mybir
from concourse.bass_utils import run_bass_kernel_spmd

F32 = mybir.dt.float32
BF16 = mybir.dt.bfloat16
AF = mybir.ActivationFunctionType
ALU = mybir.AluOpType
AXX = None

D = 1024
T = 4096
TT = 128
NT = T // TT
NCH = D // 128
NEXP1 = 8
NSLOT = 2048
PSL = 32
EPS = 1e-5
NCORES = 8
DBG_L1A = 0
DBG_UG = 99
DBG_PHASE = 99

AXX = mybir.AxisListType.X


class Sem:
    __slots__ = ("h", "n")

    def __init__(self, h):
        self.h = h
        self.n = 0

    def bump(self, k):
        self.n += k
        return self.h


class Builder:
    def __init__(self, nc):
        self.nc = nc
        self.blk = None
        self.es = None

    def declare_io(self):
        nc = self.nc
        din = lambda n, s: nc.dram_tensor(n, s, F32, kind="ExternalInput")
        self.x0 = din("x0", [T, D])
        self.out = nc.dram_tensor("out", [T, D], F32, kind="ExternalOutput")
        self.gw = din("gw", [D, NEXP1])
        self.gb = din("gb", [1, NEXP1])
        self.l1ew = din("l1ew", [NEXP1, D, D])
        self.l1eb = din("l1eb", [NEXP1, D])
        self.phiwT = din("phiwT", [D, NSLOT])
        self.phib = din("phib", [1, NSLOT])
        self.l0ew = din("l0ew", [8, D, D])
        self.l0eb = din("l0eb", [8, D])
        self.l0ng = din("l0ng", [1, D])
        self.l0og = din("l0og", [1, D])
        self.wq0 = din("wq0", [D, 512])
        self.wkv0 = din("wkv0", [D, 128])
        self.wout0 = din("wout0", [512, D])
        self.wq1 = din("wq1", [D, D])
        self.wk1 = din("wk1", [D, D])
        self.wv1 = din("wv1", [D, D])
        self.wo1 = din("wo1", [D, D])
        self.wgam = din("wgam", [D, 16])
        self.wf1 = din("wf1", [D, 64])
        self.wf2 = din("wf2", [64, D])
        self.wog1 = din("wog1", [D, 64])
        self.wog2 = din("wog2", [64, D])
        self.l1ng = din("l1ng", [1, D])
        self.cmask64 = din("cmask64", [128, 16 * 64])
        self.cmskl = nc.dram_tensor("cmskl", [128, 128], mybir.dt.uint8,
                                    kind="ExternalInput")
        self.cmsku = nc.dram_tensor("cmsku", [128, 128], mybir.dt.uint8,
                                    kind="ExternalInput")
        self.cmsku1 = nc.dram_tensor("cmsku1", [128, 128], mybir.dt.uint8,
                                     kind="ExternalInput")
        self.cmaskp = din("cmaskp", [128, PSL])
        self.ctri = din("ctri", [128, 128])
        self.cident = din("cident", [128, 128])
        self.ctriu = din("ctriu", [128, 128])
        ds = lambda n, s, dt=BF16: nc.dram_tensor(n, s, dt)
        self.e_tm = ds("e_tm", [NT, 128, NSLOT])
        self.e_sm = ds("e_sm", [NT, 128, NSLOT])
        self.xt_in = ds("xt_in", [NSLOT, D], F32)
        self.xt_rs = ds("xt_rs", [NSLOT // 8, D], F32)
        self.yt_in = ds("yt_in", [NSLOT // 8, D])
        self.yt_ag = ds("yt_ag", [NSLOT, D])

    def alloc_persistent(self, es):
        nc = self.nc
        A = lambda n, s, dt: es.enter_context(nc.sbuf_tensor(n, s, dt))
        self.XRES = A("XRES", [128, NT * D], BF16)
        self.STATS = A("STATS", [128, 768], F32)
        self.ones = A("ones", [1, 128], BF16)
        self.onesf = A("onesf", [1, 128], F32)
        self.gain1 = A("gain1", [128, D], BF16)
        self.ident = A("ident", [128, 128], BF16)
        self.ceps = A("ceps", [128, 4], F32)
        S = lambda n: Sem(es.enter_context(nc.semaphore(n)))
        self.s_li = S("s_li")
        self.s_lg = S("s_lg")
        self.s_wa = S("s_wa")
        self.s_wb = S("s_wb")
        self.s_tr = S("s_tr")
        self.s_sp = S("s_sp")
        self.s_mm = S("s_mm")
        self.s_v = S("s_v")
        self.s_a = S("s_a")
        self.s_cc = S("s_cc")
        self.s_st = S("s_st")
        self.s_g = S("s_g")
        self.s_ea = S("s_ea")
        self.s_eb = S("s_eb")
        self.s_sc = S("s_sc")

    def lg_dma(self, g, out, in_):
        g.wait_ge(self.s_lg.h, self.s_lg.n)
        g.dma_start(out=out, in_=in_).then_inc(self.s_lg.bump(16), 16)

    def barrier_all(self, eng, *, skip=()):
        for s in (self.s_li, self.s_lg, self.s_wa, self.s_wb, self.s_tr,
                  self.s_sp, self.s_mm, self.s_v, self.s_a, self.s_cc,
                  self.s_g, self.s_ea, self.s_eb, self.s_sc):
            if s not in skip:
                eng.wait_ge(s.h, s.n)

    def load_initial(self):
        b = self

        @b.blk.gpsimd
        def _(g):
            src = bass.AP(b.x0, 0, [[D, 128], [TT * D, NT], [1, D]])
            dst = bass.AP(b.XRES, 0, [[NT * D, 128], [D, NT], [1, D]])
            g.dma_start(out=dst, in_=src).then_inc(b.s_li.bump(16), 16)
            g.memset(b.ones[:, :], 1.0)
            g.memset(b.onesf[:, :], 1.0)
            bc = lambda t: bass.AP(t, 0, [[0, 128], [1, D]])
            g.dma_start(out=b.gain1[:, :], in_=bc(b.l1ng)).then_inc(b.s_li.bump(16), 16)
            g.dma_start(out=b.ident[:, :], in_=b.cident.ap()).then_inc(b.s_li.bump(16), 16)
            g.memset(b.ceps[:, 0:1], EPS)
            g.memset(b.ceps[:, 1:2], 1e-12)
            g.memset(b.ceps[:, 2:3], -16.0).then_inc(b.s_g.bump(1), 1)

    def store_out(self):
        b = self

        @b.blk.gpsimd
        def _(g):
            b.barrier_all(g)
            src = bass.AP(b.XRES, 0, [[NT * D, 128], [D, NT], [1, D]])
            dst = bass.AP(b.out, 0, [[D, 128], [TT * D, NT], [1, D]])
            g.dma_start(out=dst, in_=src).then_inc(b.s_st.bump(16), 16)
            g.wait_ge(b.s_st.h, b.s_st.n)

    def xres_tile(self, t):
        return self.XRES[:, t * D:(t + 1) * D]

    def emit_ln_stats(self):
        b = self
        r_ = b.STATS[:, 0:NT]
        nmr = b.STATS[:, NT:2 * NT]
        mv = b.STATS[:, 64:64 + 2 * NT]
        bn = b.STATS[:, 128:128 + 12 * NT]
        var_v = bass.AP(b.STATS, 64 + 1, [[768, 128], [2, NT]])
        mu_v = bass.AP(b.STATS, 64, [[768, 128], [2, NT]])

        @b.blk.gpsimd
        def _(g):
            b.barrier_all(g)

        @b.blk.sync
        def _(sy):
            b.barrier_all(sy)

        @b.blk.scalar
        def _(s):
            b.barrier_all(s, skip=(b.s_a,))

        @b.blk.tensor
        def _(t):
            b.barrier_all(t, skip=(b.s_mm,))

        @b.blk.vector
        def _(v):
            b.barrier_all(v, skip=(b.s_v,))
            for t in range(NT):
                xt = b.xres_tile(t)
                v.bn_stats(bn[:, 12 * t:12 * t + 6], xt[:, 0:512])
                v.bn_stats(bn[:, 12 * t + 6:12 * t + 12], xt[:, 512:1024])
            v.drain()
            for t in range(NT):
                v.bn_aggr(mv[:, 2 * t:2 * t + 2], bn[:, 12 * t:12 * t + 12])
            v.drain().then_inc(b.s_v.bump(1), 1)

        @b.blk.scalar
        def _(s):
            s.wait_ge(b.s_v.h, b.s_v.n)
            s.activation(r_[:, :], var_v, AF.Sqrt, bias=b.ceps[:, 0:1])
            s.drain().then_inc(b.s_a.bump(1), 1)

        @b.blk.vector
        def _(v):
            v.wait_ge(b.s_a.h, b.s_a.n)
            v.reciprocal(r_[:, :], r_[:, :])
            v.drain()
            v.scalar_tensor_tensor(nmr[:, :], mu_v, -1.0, r_[:, :],
                                   ALU.mult, ALU.mult)
            v.drain().then_inc(b.s_v.bump(1), 1)
        return r_, nmr

    def emit_xln_fm(self, dst, r_, nmr, gain=None, tiles=None, also_tm=None):
        b = self
        if tiles is None:
            tiles = list(range(NT))
        ntl = len(tiles)
        if also_tm is None:
            tmp = [b.es.enter_context(b.nc.sbuf_tensor(f"xln_tmp{i}_{b.s_v.n}", [128, D], BF16))
                   for i in range(2)]
            get_tm = lambda i: tmp[i % 2][:, :]
        else:
            get_tm = lambda i: also_tm[:, i * D:(i + 1) * D]
        v_marks = []
        tr_needed = []
        v0 = b.s_v.n
        tr0 = b.s_tr.n

        @b.blk.vector
        def _(v):
            for i, t in enumerate(tiles):
                if also_tm is None and i >= 2:
                    v.wait_ge(b.s_tr.h, tr0 + 16 * NCH * (i - 1))
                v.tensor_scalar(get_tm(i), b.xres_tile(t),
                                r_[:, t:t + 1], nmr[:, t:t + 1],
                                ALU.mult, ALU.add)
                if gain is not None:
                    v.drain()
                    v.tensor_mul(get_tm(i), get_tm(i), gain[:, :])
                v.drain().then_inc(b.s_v.bump(1), 1)
                v_marks.append(b.s_v.n)

        @b.blk.sync
        def _(sy):
            for i, t in enumerate(tiles):
                sy.wait_ge(b.s_v.h, v_marks[i])
                for c in range(NCH):
                    sy.dma_start_transpose(
                        dst[:, c * (ntl * 128) + i * 128:
                            c * (ntl * 128) + (i + 1) * 128],
                        get_tm(i)[:, c * 128:(c + 1) * 128],
                    ).then_inc(b.s_tr.bump(16), 16)
                sy.wait_ge(b.s_tr.h, b.s_tr.n)
        return tr0

    def emit_l1ffn(self, tag="l1f_", final=False):
        b = self
        nc = self.nc
        with contextlib.ExitStack() as es:
            old = b.es
            b.es = es
            r_, nmr = b.emit_ln_stats()
            if DBG_PHASE < 2:
                b.es = old
                return
            xfm = es.enter_context(nc.sbuf_tensor(tag + "xfm", [128, NCH * T], BF16))
            tr0 = b.emit_xln_fm(xfm, r_, nmr)
            if DBG_PHASE < 3:
                b.es = old
                return
            fmtile = lambda c, ti: xfm[:, c * T + ti * 128: c * T + (ti + 1) * 128]

            wsb = es.enter_context(nc.sbuf_tensor(tag + "w", [128, NT * NEXP1], F32))
            wtb = es.enter_context(nc.sbuf_tensor(tag + "wtb", [128, NT * NEXP1], BF16))
            wtT = es.enter_context(nc.sbuf_tensor(tag + "wtT", [NEXP1, T], BF16))
            gwsb = es.enter_context(nc.sbuf_tensor(tag + "gw", [128, NCH * NEXP1], BF16))
            gbsb = es.enter_context(nc.sbuf_tensor(tag + "gb", [1, NEXP1], BF16))
            ebsb = es.enter_context(nc.sbuf_tensor(tag + "eb", [NEXP1, D], BF16))
            gtmp = es.enter_context(nc.sbuf_tensor(tag + "gtmp", [128, NT * NEXP1], F32))
            gstat = es.enter_context(nc.sbuf_tensor(tag + "gs", [128, 2 * NT], F32))
            wbuf = [es.enter_context(nc.sbuf_tensor(f"{tag}wb{i}", [128, NCH * D], BF16))
                    for i in range(2)]
            with (nc.psum_tensor(tag + "pg", [128, 256], F32) as pg,
                  nc.psum_tensor(tag + "pt", [128, 128], BF16) as ptr,
                  nc.psum_tensor(tag + "p0", [128, 512], F32) as p0,
                  nc.psum_tensor(tag + "p1", [128, 512], F32) as p1,
                  nc.psum_tensor(tag + "p2", [128, 512], F32) as p2,
                  nc.psum_tensor(tag + "p3", [128, 512], F32) as p3):
                self._l1ffn_body(es, fmtile, tr0, wsb, wtb, wtT, gwsb, gbsb,
                                 ebsb, gtmp, gstat, wbuf, pg, ptr,
                                 [p0, p1, p2, p3], final)
            b.es = old

    def _l1ffn_body(self, es, fmtile, tr0, wsb, wtb, wtT, gwsb, gbsb, ebsb,
                    gtmp, gstat, wbuf, pg, ptr, psum, final):
        b = self
        nc = self.nc

        @b.blk.gpsimd
        def _(g):
            src = bass.AP(b.gw, 0, [[NEXP1, 128], [128 * NEXP1, NCH], [1, NEXP1]])
            b.lg_dma(g, gwsb[:, :], src)
            b.lg_dma(g, gbsb[:, :], b.gb.ap())
            b.lg_dma(g, ebsb[:, :], b.l1eb.ap())
        ld_gates = b.s_lg.n

        wsem = [b.s_wa, b.s_wb]

        def load_expert(g, e, buf):
            src = bass.AP(b.l1ew, e * D * D,
                          [[D, 128], [128 * D, NCH], [1, D]])
            g.dma_start(out=wbuf[buf][:, :], in_=src).then_inc(wsem[buf].bump(16), 16)
            return wsem[buf].n

        ld_marks = {}

        @b.blk.gpsimd
        def _(g):
            ld_marks[0] = load_expert(g, 0, 0)

        if DBG_PHASE == 5.3:
            @b.blk.gpsimd
            def _(g):
                g.memset(wtb[:, :], 0.125)
                g.memset(wsb[:, :], 0.125).then_inc(b.s_v.bump(1), 1)
        v_gates = b.s_v.n
        if DBG_PHASE == 5.3:
            mm_tr_marks = []

            @b.blk.tensor
            def _(t):
                t.wait_ge(b.s_v.h, v_gates)
                for ti in range(NT):
                    if ti >= 1:
                        t.wait_ge(b.s_v.h, v_gates + ti)
                    t.transpose(ptr[0:NEXP1, :],
                                wtb[:, ti * NEXP1:(ti + 1) * NEXP1],
                                b.ident[:, :]).then_inc(b.s_mm.bump(1), 1)
                    mm_tr_marks.append(b.s_mm.n)

            @b.blk.vector
            def _(v):
                for ti in range(NT):
                    v.wait_ge(b.s_mm.h, mm_tr_marks[ti])
                    v.tensor_copy(wtT[:, ti * 128:(ti + 1) * 128],
                                  ptr[0:NEXP1, :]).then_inc(b.s_v.bump(1), 1)
            return
        @b.blk.tensor
        def _(t):
            t.wait_ge(b.s_lg.h, ld_gates)
            for ti in range(NT):
                t.wait_ge(b.s_tr.h, tr0 + 16 * NCH * (ti + 1))
                for c in range(NCH):
                    t.matmul(pg[:, ti * NEXP1:(ti + 1) * NEXP1],
                             fmtile(c, ti), gwsb[:, c * NEXP1:(c + 1) * NEXP1],
                             start=(c == 0), stop=False)
                mmi = t.matmul(pg[:, ti * NEXP1:(ti + 1) * NEXP1],
                               b.ones[0:1, 0:128], gbsb[0:1, :],
                               start=False, stop=True)
                if ti == NT - 1:
                    mmi.then_inc(b.s_mm.bump(1), 1)
        mm_gates = b.s_mm.n

        if DBG_PHASE < 4:
            return
        @b.blk.scalar
        def _(s):
            s.wait_ge(b.s_mm.h, mm_gates)
            s.activation(gtmp[:, :], pg[:, :], AF.Exp)
            s.drain().then_inc(b.s_a.bump(1), 1)
        a_exp = b.s_a.n

        gsum = gstat[:, 0:NT]
        gwsum = gstat[:, NT:2 * NT]

        @b.blk.vector
        def _(v):
            v.wait_ge(b.s_a.h, a_exp)
            gv = bass.AP(gtmp, 0, [[gtmp[:, :].ap[0][0], 128], [NEXP1, NT], [1, NEXP1]])
            wv = bass.AP(wsb, 0, [[wsb[:, :].ap[0][0], 128], [NEXP1, NT], [1, NEXP1]])
            sumb = bass.AP(gstat, 0, [[gstat[:, :].ap[0][0], 128], [1, NT], [0, NEXP1]])
            wsumb = bass.AP(gstat, NT, [[gstat[:, :].ap[0][0], 128], [1, NT], [0, NEXP1]])
            v.reduce_sum(gsum[:, :], gv, axis=AXX)
            v.drain()
            v.tensor_scalar_mul(gsum[:, :], gsum[:, :], 0.1)
            v.drain()
            v.tensor_tensor(wv, gv, sumb, ALU.is_ge)
            v.drain()
            v.tensor_mul(wv, wv, gv)
            v.drain()
            v.reduce_sum(gwsum[:, :], wv, axis=AXX)
            v.drain()
            v.reciprocal(gwsum[:, :], gwsum[:, :])
            v.drain()
            v.tensor_scalar_mul(gwsum[:, :], gwsum[:, :], 0.5)
            v.drain()
            v.tensor_tensor(wv, wv, wsumb, ALU.mult)
            v.drain()
            v.tensor_copy(wtb[:, :], wsb[:, :])
            v.drain().then_inc(b.s_v.bump(1), 1)
        v_gates = b.s_v.n

        if DBG_PHASE < 5:
            return
        if DBG_PHASE == 7:
            @b.blk.sync
            def _(sy):
                sy.wait_ge(b.s_v.h, v_gates)
                sy.dma_start(out=bass.AP(b.out, 0, [[D, 128], [1, NT * NEXP1]]),
                             in_=wsb[:, :]).then_inc(b.s_tr.bump(16), 16)
            return
        mm_tr_marks = []
        ntr_loop = 1 if DBG_PHASE == 5.41 else NT

        @b.blk.tensor
        def _(t):
            t.wait_ge(b.s_v.h, v_gates)
            for ti in range(ntr_loop):
                if ti >= 1:
                    t.wait_ge(b.s_v.h, v_gates + ti)
                t.transpose(ptr[0:NEXP1, :], wtb[:, ti * NEXP1:(ti + 1) * NEXP1],
                            b.ident[:, :]).then_inc(b.s_mm.bump(1), 1)
                mm_tr_marks.append(b.s_mm.n)

        @b.blk.vector
        def _(v):
            for ti in range(ntr_loop):
                v.wait_ge(b.s_mm.h, mm_tr_marks[ti])
                v.tensor_copy(wtT[:, ti * 128:(ti + 1) * 128],
                              ptr[0:NEXP1, :]).then_inc(b.s_v.bump(1), 1)
        v_wtT = b.s_v.n

        if DBG_PHASE < 5.8:
            return
        mm_bias_marks = []

        @b.blk.tensor
        def _(t):
            for ti in range(NT):
                t.wait_ge(b.s_v.h, v_wtT - NT + ti + 1)
                if ti >= 2:
                    t.wait_ge(b.s_v.h, v_wtT + 2 * (ti - 1))
                for h in range(2):
                    t.matmul(psum[(2 * ti + h) % 4][:, :],
                             wtT[:, ti * 128:(ti + 1) * 128],
                             ebsb[:, h * 512:(h + 1) * 512],
                             start=True, stop=True).then_inc(b.s_mm.bump(1), 1)
                    mm_bias_marks.append(b.s_mm.n)

        @b.blk.vector
        def _(v):
            for ti in range(NT):
                for h in range(2):
                    v.wait_ge(b.s_mm.h, mm_bias_marks[2 * ti + h])
                    xs = b.XRES[:, ti * D + h * 512: ti * D + (h + 1) * 512]
                    v.tensor_add(xs, xs,
                                 psum[(2 * ti + h) % 4][:, :]).then_inc(b.s_v.bump(1), 1)

        if DBG_PHASE < 6:
            return
        mm_done_expert = {}
        v_done_expert = {}
        for e in range(NEXP1):
            mmb = b.s_mm.n
            vb = b.s_v.n

            @b.blk.tensor
            def _(t, e=e, mmb=mmb, vb=vb):
                t.wait_ge(wsem[e % 2].h, ld_marks[e])
                t.wait_ge(b.s_v.h, vb)
                k = 0
                for ti in range(NT):
                    for h in range(2):
                        if k >= 4:
                            t.wait_ge(b.s_v.h, vb + k - 3)
                        pb = psum[k % 4]
                        for c in range(NCH):
                            mmi = t.matmul(pb[:, :], fmtile(c, ti),
                                           wbuf[e % 2][:, c * D + h * 512:
                                                       c * D + (h + 1) * 512],
                                           start=(c == 0), stop=(c == NCH - 1))
                        mmi.then_inc(b.s_mm.bump(1), 1)
                        k += 1
            mm_done_expert[e] = b.s_mm.n

            @b.blk.gpsimd
            def _(g, e=e):
                if e + 1 < NEXP1:
                    if e >= 1:
                        g.wait_ge(b.s_mm.h, mm_done_expert[e - 1])
                    ld_marks[e + 1] = load_expert(g, e + 1, (e + 1) % 2)

            @b.blk.vector
            def _(v, e=e, mmb=mmb):
                k = 0
                for ti in range(NT):
                    for h in range(2):
                        v.wait_ge(b.s_mm.h, mmb + k + 1)
                        xs = b.XRES[:, ti * D + h * 512: ti * D + (h + 1) * 512]
                        v.scalar_tensor_tensor(
                            xs, psum[k % 4][:, :],
                            wsb[:, ti * NEXP1 + e: ti * NEXP1 + e + 1],
                            xs, ALU.mult, ALU.add).then_inc(b.s_v.bump(1), 1)
                        k += 1
            v_done_expert[e] = b.s_v.n

    def emit_l0ffn(self, tag):
        b = self
        nc = self.nc
        with contextlib.ExitStack() as es:
            old = b.es
            b.es = es
            self._l0ffn(es, tag)
            b.es = old

    def _l0ffn(self, es, tag):
        b = self
        nc = self.nc
        A = lambda n, s, dt: es.enter_context(nc.sbuf_tensor(tag + n, s, dt))
        r_, nmr = b.emit_ln_stats()
        rD = A("rD", [128, NT * 64], BF16)
        rCp = A("rCp", [32, T], BF16)
        with contextlib.ExitStack() as es_ab:
            xtm = es_ab.enter_context(
                nc.sbuf_tensor(tag + "xtm", [128, NT * D], BF16))
            self._l0ffn_passA(es_ab, tag, r_, nmr, xtm, rD, rCp)
            self._l0ffn_passB(es_ab, tag, xtm, rD)
        ysm = A("ysm", [128, 16 * D], BF16)
        with contextlib.ExitStack() as es_c:
            self._l0ffn_passC(es_c, tag, ysm)
        self._l0ffn_passD(es, tag, rCp, ysm)

    def _l0ffn_passA(self, es0, tag, r_, nmr, xtm, rD, rCp):
        b = self
        nc = self.nc
        with contextlib.ExitStack() as es:
            A = lambda n, s, dt: es.enter_context(nc.sbuf_tensor(tag + n, s, dt))
            phiw = A("phiw", [128, NCH * NSLOT], BF16)
            phib = A("phib", [1, NSLOT], BF16)
            mask64 = A("mask64", [128, 16 * 64], BF16)
            maskp = A("maskp", [128, PSL], BF16)
            fmr = A("fmr", [128, 2 * D], BF16)
            etm = [A(f"etm{i}", [128, NSLOT], BF16) for i in range(2)]
            esm = [A(f"esm{i}", [128, NSLOT], BF16) for i in range(2)]

            @b.blk.gpsimd
            def _(g):
                src = bass.AP(b.phiwT, 0,
                              [[NSLOT, 128], [128 * NSLOT, NCH], [1, NSLOT]])
                b.lg_dma(g, phiw[:, :], src)
                b.lg_dma(g, phib[:, :], b.phib.ap())
                b.lg_dma(g, mask64[:, :], b.cmask64.ap())
                b.lg_dma(g, maskp[:, :], b.cmaskp.ap())
            ld_phi = b.s_lg.n

            with (nc.psum_tensor(tag + "pl0", [128, 512], F32) as pl0,
                  nc.psum_tensor(tag + "pl1", [128, 512], F32) as pl1,
                  nc.psum_tensor(tag + "pl2", [128, 512], F32) as pl2,
                  nc.psum_tensor(tag + "pl3", [128, 512], F32) as pl3,
                  nc.psum_tensor(tag + "psd", [128, 64], F32) as psd,
                  nc.psum_tensor(tag + "psc", [32, 128], F32) as psc):
                pls = [pl0, pl1, pl2, pl3]
                vA, trF, mmA, aA, scE, mmS, vS = [], [], [], [], [], [], []
                for ti in range(NT):
                    @b.blk.vector
                    def _(v, ti=ti):
                        if ti == 0:
                            v.wait_ge(b.s_lg.h, ld_phi)
                        v.tensor_scalar(xtm[:, ti * D:(ti + 1) * D],
                                        b.xres_tile(ti), r_[:, ti:ti + 1],
                                        nmr[:, ti:ti + 1], ALU.mult, ALU.add)
                        v.drain().then_inc(b.s_v.bump(1), 1)
                    vA.append(b.s_v.n)

                    @b.blk.sync
                    def _(sy, ti=ti):
                        sy.wait_ge(b.s_v.h, vA[ti])
                        if ti >= 2:
                            sy.wait_ge(b.s_mm.h, mmA[ti - 2])
                        for c in range(NCH):
                            sy.dma_start_transpose(
                                fmr[:, (ti % 2) * D + c * 128:
                                    (ti % 2) * D + (c + 1) * 128],
                                xtm[:, ti * D + c * 128: ti * D + (c + 1) * 128],
                            ).then_inc(b.s_tr.bump(16), 16)
                        sy.wait_ge(b.s_tr.h, b.s_tr.n)
                    trF.append(b.s_tr.n)

                    @b.blk.tensor
                    def _(t, ti=ti):
                        if ti == 0:
                            t.wait_ge(b.s_lg.h, ld_phi)
                        t.wait_ge(b.s_tr.h, trF[ti])
                        if ti >= 1:
                            t.wait_ge(b.s_a.h, aA[ti - 1])
                        for st in range(4):
                            for c in range(NCH):
                                t.matmul(pls[st][:, :],
                                         fmr[:, (ti % 2) * D + c * 128:
                                             (ti % 2) * D + (c + 1) * 128],
                                         phiw[:, c * NSLOT + st * 512:
                                              c * NSLOT + (st + 1) * 512],
                                         start=(c == 0), stop=False)
                            mmi = t.matmul(pls[st][:, :], b.ones[0:1, 0:128],
                                           phib[0:1, st * 512:(st + 1) * 512],
                                           start=False, stop=True)
                        mmi.then_inc(b.s_mm.bump(1), 1)
                    mmA.append(b.s_mm.n)

                    @b.blk.scalar
                    def _(s, ti=ti):
                        s.wait_ge(b.s_mm.h, mmA[ti])
                        if ti >= 2:
                            s.wait_ge(b.s_mm.h, mmS[ti - 2])
                            s.wait_ge(b.s_sc.h, scE[ti - 2])
                        for st in range(4):
                            s.activation(etm[ti % 2][:, st * 512:(st + 1) * 512],
                                         pls[st][:, :], AF.Exp)
                        s.drain().then_inc(b.s_a.bump(1), 1)
                        s.dma_start(
                            out=bass.AP(b.e_tm, ti * 128 * NSLOT,
                                        [[NSLOT, 128], [1, NSLOT]]),
                            in_=etm[ti % 2][:, :]).then_inc(b.s_sc.bump(16), 16)
                        for j in range(16):
                            s.dma_start_transpose(
                                esm[ti % 2][:, j * 128:(j + 1) * 128],
                                etm[ti % 2][:, j * 128:(j + 1) * 128],
                            ).then_inc(b.s_sc.bump(16), 16)
                        s.wait_ge(b.s_sc.h, b.s_sc.n)
                        s.dma_start(
                            out=bass.AP(b.e_sm, ti * 128 * NSLOT,
                                        [[NSLOT, 128], [1, NSLOT]]),
                            in_=esm[ti % 2][:, :]).then_inc(b.s_sc.bump(16), 16)
                    aA.append(b.s_a.n)
                    scE.append(b.s_sc.n)

                    @b.blk.tensor
                    def _(t, ti=ti):
                        t.wait_ge(b.s_sc.h, scE[ti] - 16)
                        if ti >= 1:
                            t.wait_ge(b.s_v.h, vS[ti - 1])
                        for j in range(16):
                            ch = esm[ti % 2][:, j * 128:(j + 1) * 128]
                            t.matmul(psd[:, :], ch,
                                     mask64[:, j * 64:(j + 1) * 64],
                                     start=(j == 0), stop=(j == 15))
                        for j in range(16):
                            ch = esm[ti % 2][:, j * 128:(j + 1) * 128]
                            mmi = t.matmul(psc[:, :], maskp[:, :], ch,
                                           start=(j == 0), stop=(j == 15))
                        mmi.then_inc(b.s_mm.bump(1), 1)
                    mmS.append(b.s_mm.n)

                    @b.blk.vector
                    def _(v, ti=ti):
                        v.wait_ge(b.s_mm.h, mmS[ti])
                        with nc.allow_low_precision(reason="bf16 softmax divisors"):
                            v.reciprocal(rD[:, ti * 64:(ti + 1) * 64], psd[:, :])
                            v.reciprocal(rCp[:, ti * 128:(ti + 1) * 128],
                                         psc[0:32, :])
                        v.drain().then_inc(b.s_v.bump(1), 1)
                    vS.append(b.s_v.n)
                self._l0_scE_all = b.s_sc.n
                self._l0_vS_all = b.s_v.n

    def _l0ffn_passB(self, es0, tag, xtm, rD):
        b = self
        nc = self.nc
        with contextlib.ExitStack() as es:
            A = lambda n, s, dt: es.enter_context(nc.sbuf_tensor(tag + n, s, dt))
            er = [A(f"er{i}", [128, 512], BF16) for i in range(2)]
            dr = [A(f"dr{i}", [128, 512], BF16) for i in range(2)]
            xst = [A(f"xst{i}", [128, D], F32) for i in range(2)]
            with contextlib.ExitStack() as pes:
                pb = [pes.enter_context(
                    nc.psum_tensor(f"{tag}pb{i}", [128, 512], F32))
                    for i in range(8)]
                gE, vDv, mmB, vEv, spX = [], [], [], [], []
                k = 0
                for g in range(4):
                    for ti in range(NT):
                        slot = k % 2
                        sem = [b.s_ea, b.s_eb][slot]

                        @b.blk.gpsimd
                        def _(gp, g=g, ti=ti, slot=slot, sem=sem, k=k):
                            if k == 0:
                                gp.wait_ge(b.s_sc.h, self._l0_scE_all)
                                gp.wait_ge(b.s_v.h, self._l0_vS_all)
                            if k >= 2:
                                gp.wait_ge(b.s_v.h, vDv[k - 2])
                            src = bass.AP(b.e_tm, (ti * 128 * NSLOT) + g * 512,
                                          [[NSLOT, 128], [1, 512]])
                            gp.dma_start(out=er[slot][:, :], in_=src
                                         ).then_inc(sem.bump(16), 16)
                        gE.append(sem.n)

                        @b.blk.vector
                        def _(v, g=g, ti=ti, slot=slot, sem=sem, k=k):
                            v.wait_ge(sem.h, gE[k])
                            if k >= 2:
                                v.wait_ge(b.s_mm.h, mmB[k - 2])
                            dvs = bass.AP(rD, ti * 64 + 16 * g,
                                          [[rD[:, :].ap[0][0], 128], [1, 16], [0, PSL]])
                            v.tensor_tensor(dr[slot][:, :], er[slot][:, :],
                                            dvs, ALU.mult)
                            v.drain().then_inc(b.s_v.bump(1), 1)
                        vDv.append(b.s_v.n)

                        @b.blk.tensor
                        def _(t, g=g, ti=ti, slot=slot, k=k):
                            t.wait_ge(b.s_v.h, vDv[k])
                            if ti == 0 and g > 0:
                                t.wait_ge(b.s_v.h, vEv[g - 1])
                            for ch in range(4):
                                for h in range(2):
                                    mmi = t.matmul(
                                        pb[ch * 2 + h][:, :],
                                        dr[slot][:, ch * 128:(ch + 1) * 128],
                                        xtm[:, ti * D + h * 512:
                                            ti * D + (h + 1) * 512],
                                        start=(ti == 0), stop=(ti == NT - 1))
                            mmi.then_inc(b.s_mm.bump(1), 1)
                        mmB.append(b.s_mm.n)
                        k += 1

                    for ch in range(4):
                        @b.blk.vector
                        def _(v, g=g, ch=ch):
                            if ch == 0:
                                v.wait_ge(b.s_mm.h, mmB[-1])
                            if len(spX) >= 2:
                                v.wait_ge(b.s_sp.h, spX[-2])
                            v.tensor_copy(xst[ch % 2][:, 0:512], pb[ch * 2][:, :])
                            v.tensor_copy(xst[ch % 2][:, 512:1024],
                                          pb[ch * 2 + 1][:, :])
                            v.drain().then_inc(b.s_v.bump(1), 1)
                        vch = b.s_v.n

                        @b.blk.gpsimd
                        def _(gp, g=g, ch=ch, vch=vch):
                            gp.wait_ge(b.s_v.h, vch)
                            dst = bass.AP(b.xt_in, (g * 512 + ch * 128) * D,
                                          [[D, 128], [1, D]])
                            gp.dma_start(out=dst, in_=xst[ch % 2][:, :]
                                         ).then_inc(b.s_sp.bump(16), 16)
                        spX.append(b.s_sp.n)
                    vEv.append(b.s_v.n)

    def _l0ffn_passC(self, es, tag, ysm):
        b = self
        nc = self.nc
        A = lambda n, s, dt: es.enter_context(nc.sbuf_tensor(tag + n, s, dt))
        xtc = A("xtc", [128, 2 * D], F32)
        xtt = A("xtt", [128, 2 * D], BF16)
        xtf = A("xtf", [128, 2 * D], BF16)
        ytst = [A(f"ytst{i}", [32, D], BF16) for i in range(2)]
        l0eb = A("l0eb", [1, 8 * D], BF16)
        st2 = b.STATS[:, 512:544]

        @b.blk.gpsimd
        def _(g):
            g.wait_ge(b.s_sp.h, b.s_sp.n)
            g.collective_compute(
                "ReduceScatter", mybir.AluOpType.add,
                replica_groups=[list(range(NCORES))],
                ins=[b.xt_in.ap().opt()], outs=[b.xt_rs.ap().opt()],
            ).then_inc(b.s_cc.bump(1), 1)
            g.wait_ge(b.s_cc.h, b.s_cc.n)
            src = bass.AP(b.xt_rs, 0, [[D, 128], [128 * D, 2], [1, D]])
            b.lg_dma(g, xtc[:, :], src)
            b.lg_dma(g, l0eb[:, :], bass.AP(b.l0eb, 0, [[8 * D, 1], [1, 8 * D]]))
        ld_xtc = b.s_lg.n

        wbuf = [A(f"wb{i}", [128, NCH * D], BF16) for i in range(2)]
        wsem = [b.s_wa, b.s_wb]
        ldW = {}

        @b.blk.gpsimd
        def _(g):
            src = bass.AP(b.l0ew, 0, [[D, 128], [128 * D, NCH], [1, D]])
            g.dma_start(out=wbuf[0][:, :], in_=src).then_inc(b.s_wa.bump(16), 16)
            ldW[0] = b.s_wa.n

        @b.blk.vector
        def _(v):
            v.wait_ge(b.s_lg.h, ld_xtc)
            v.tensor_scalar(xtc[:, :], xtc[:, :], -33000.0, 65000.0,
                            ALU.max, ALU.min)
            v.drain()
            bnB = b.STATS[:, 544:592]
            for i in range(2):
                for hh in range(2):
                    v.bn_stats(bnB[:, i * 12 + hh * 6:i * 12 + hh * 6 + 6],
                               xtc[:, i * D + hh * 512:i * D + (hh + 1) * 512])
            v.drain()
            for i in range(2):
                v.bn_aggr(st2[:, 2 * i:2 * i + 2], bnB[:, i * 12:i * 12 + 12])
            v.drain().then_inc(b.s_v.bump(1), 1)

        @b.blk.scalar
        def _(s):
            s.wait_ge(b.s_v.h, b.s_v.n)
            varv = bass.AP(b.STATS, 512 + 1, [[768, 128], [2, 2]])
            s.activation(st2[:, 4:6], varv, AF.Sqrt, bias=b.ceps[:, 0:1])
            s.drain().then_inc(b.s_a.bump(1), 1)

        @b.blk.vector
        def _(v):
            v.wait_ge(b.s_a.h, b.s_a.n)
            v.reciprocal(st2[:, 4:6], st2[:, 4:6])
            v.drain()
            muv = bass.AP(b.STATS, 512, [[768, 128], [2, 2]])
            v.scalar_tensor_tensor(st2[:, 6:8], muv, -1.0, st2[:, 4:6],
                                   ALU.mult, ALU.mult)
            v.drain()
            for i in range(2):
                v.tensor_scalar(xtc[:, i * D:(i + 1) * D],
                                xtc[:, i * D:(i + 1) * D],
                                st2[:, 4 + i:5 + i], st2[:, 6 + i:7 + i],
                                ALU.mult, ALU.add)
            v.drain().then_inc(b.s_v.bump(1), 1)

        @b.blk.scalar
        def _(s):
            s.wait_ge(b.s_v.h, b.s_v.n)
            s.activation(xtt[:, :], xtc[:, :], AF.Tanh)
            s.drain().then_inc(b.s_a.bump(1), 1)
        a_tanh = b.s_a.n

        @b.blk.sync
        def _(sy):
            sy.wait_ge(b.s_a.h, a_tanh)
            for i in range(2):
                for j in range(NCH):
                    sy.dma_start_transpose(
                        xtf[:, j * 256 + i * 128: j * 256 + (i + 1) * 128],
                        xtt[:, i * D + j * 128: i * D + (j + 1) * 128],
                    ).then_inc(b.s_tr.bump(16), 16)
            sy.wait_ge(b.s_tr.h, b.s_tr.n)
        tr_xt = b.s_tr.n

        with (nc.psum_tensor(tag + "pe0", [32, D], F32) as pe0,
              nc.psum_tensor(tag + "pe1", [32, D], F32) as pe1):
            pes_ = [pe0, pe1]
            mmE, vY, spY = [], [], []
            for e in range(8):
                @b.blk.tensor
                def _(t, e=e):
                    if e == 0:
                        t.wait_ge(b.s_tr.h, tr_xt)
                        t.wait_ge(b.s_lg.h, ld_xtc)
                    t.wait_ge(wsem[e % 2].h, ldW[e])
                    if e >= 2:
                        t.wait_ge(b.s_v.h, vY[e - 2])
                    i, so = e // 4, (e % 4) * 32
                    for h in range(2):
                        for j in range(NCH):
                            t.matmul(pes_[e % 2][:, h * 512:(h + 1) * 512],
                                     xtf[:, j * 256 + i * 128 + so:
                                         j * 256 + i * 128 + so + 32],
                                     wbuf[e % 2][:, j * D + h * 512:
                                                 j * D + (h + 1) * 512],
                                     start=(j == 0), stop=False)
                        mmi = t.matmul(pes_[e % 2][:, h * 512:(h + 1) * 512],
                                       b.ones[0:1, 0:32],
                                       l0eb[0:1, e * D + h * 512:
                                            e * D + (h + 1) * 512],
                                       start=False, stop=True)
                    mmi.then_inc(b.s_mm.bump(1), 1)
                mmE.append(b.s_mm.n)

                @b.blk.gpsimd
                def _(g, e=e):
                    if e + 1 < 8:
                        if e >= 1:
                            g.wait_ge(b.s_mm.h, mmE[e - 1])
                        src = bass.AP(b.l0ew, (e + 1) * D * D,
                                      [[D, 128], [128 * D, NCH], [1, D]])
                        g.dma_start(out=wbuf[(e + 1) % 2][:, :], in_=src
                                    ).then_inc(wsem[(e + 1) % 2].bump(16), 16)
                        ldW[e + 1] = wsem[(e + 1) % 2].n

                @b.blk.vector
                def _(v, e=e):
                    v.wait_ge(b.s_mm.h, mmE[e])
                    if e >= 2:
                        v.wait_ge(b.s_sp.h, spY[e - 2])
                    v.tensor_copy(ytst[e % 2][:, :], pes_[e % 2][0:32, :])
                    v.drain().then_inc(b.s_v.bump(1), 1)
                vY.append(b.s_v.n)

                @b.blk.gpsimd
                def _(g, e=e):
                    g.wait_ge(b.s_v.h, vY[e])
                    dst = bass.AP(b.yt_in, e * 32 * D, [[D, 32], [1, D]])
                    g.dma_start(out=dst, in_=ytst[e % 2][:, :]
                                ).then_inc(b.s_sp.bump(16), 16)
                spY.append(b.s_sp.n)

        @b.blk.gpsimd
        def _(g):
            g.wait_ge(b.s_sp.h, b.s_sp.n)
            g.collective_compute(
                "AllGather", mybir.AluOpType.bypass,
                replica_groups=[list(range(NCORES))],
                ins=[b.yt_in.ap().opt()], outs=[b.yt_ag.ap().opt()],
            ).then_inc(b.s_cc.bump(1), 1)
            g.wait_ge(b.s_cc.h, b.s_cc.n)
            src = bass.AP(b.yt_ag, 0, [[D, 128], [128 * D, 16], [1, D]])
            b.lg_dma(g, ysm[:, :], src)

    def _l0ffn_passD(self, es, tag, rCp, ysm):
        b = self
        nc = self.nc
        A = lambda n, s, dt: es.enter_context(nc.sbuf_tensor(tag + n, s, dt))
        rCe = A("rCe", [128, T], BF16)
        emr = [A(f"emr{i}", [128, NSLOT], BF16) for i in range(2)]
        cr = [A(f"cr{i}", [128, NSLOT], BF16) for i in range(2)]
        ld_ysm = b.s_lg.n

        @b.blk.gpsimd
        def _(g):
            g.wait_ge(b.s_tr.h, b.s_tr.n)
            g.wait_ge(b.s_v.h, b.s_v.n)
            for i in range(4):
                b.lg_dma(g, rCe[32 * i:32 * (i + 1), :], rCp[0:32, :])
        ld_rce = b.s_lg.n

        with contextlib.ExitStack() as pes:
            pd = [pes.enter_context(
                nc.psum_tensor(f"{tag}pd{i}", [128, D], F32)) for i in range(2)]
            gE, vC, mmD, vR = [], [], [], []
            for ti in range(NT):
                slot = ti % 2
                sem = [b.s_ea, b.s_eb][slot]

                @b.blk.gpsimd
                def _(g, ti=ti, slot=slot, sem=sem):
                    if ti < 2:
                        g.wait_ge(b.s_lg.h, ld_rce)
                    else:
                        g.wait_ge(b.s_v.h, vC[ti - 2])
                    src = bass.AP(b.e_sm, ti * 128 * NSLOT, [[NSLOT, 128], [1, NSLOT]])
                    g.dma_start(out=emr[slot][:, :], in_=src
                                ).then_inc(sem.bump(16), 16)
                gE.append(sem.n)

                @b.blk.vector
                def _(v, ti=ti, slot=slot, sem=sem):
                    v.wait_ge(sem.h, gE[ti])
                    if ti >= 2:
                        v.wait_ge(b.s_mm.h, mmD[ti - 2])
                    dvs = bass.AP(rCe, ti * 128,
                                  [[rCe[:, :].ap[0][0], 128], [0, 16], [1, 128]])
                    v.tensor_tensor(cr[slot][:, :], emr[slot][:, :], dvs, ALU.mult)
                    v.drain().then_inc(b.s_v.bump(1), 1)
                vC.append(b.s_v.n)

                @b.blk.tensor
                def _(t, ti=ti, slot=slot):
                    t.wait_ge(b.s_v.h, vC[ti])
                    if ti == 0:
                        t.wait_ge(b.s_lg.h, ld_ysm)
                    if ti >= 2:
                        t.wait_ge(b.s_v.h, vR[ti - 2])
                    for h in range(2):
                        for j in range(16):
                            mmi = t.matmul(pd[slot][:, h * 512:(h + 1) * 512],
                                           cr[slot][:, j * 128:(j + 1) * 128],
                                           ysm[:, j * D + h * 512:
                                               j * D + (h + 1) * 512],
                                           start=(j == 0), stop=(j == 15))
                    mmi.then_inc(b.s_mm.bump(1), 1)
                mmD.append(b.s_mm.n)

                @b.blk.vector
                def _(v, ti=ti, slot=slot):
                    v.wait_ge(b.s_mm.h, mmD[ti])
                    for h in range(2):
                        xs = b.XRES[:, ti * D + h * 512: ti * D + (h + 1) * 512]
                        v.scalar_tensor_tensor(xs, pd[slot][:, h * 512:(h + 1) * 512],
                                               0.5, xs, ALU.mult, ALU.add)
                    v.drain().then_inc(b.s_v.bump(1), 1)
                vR.append(b.s_v.n)

    def emit_l0attn(self):
        b = self
        nc = self.nc
        with contextlib.ExitStack() as es:
            old = b.es
            b.es = es
            self._l0attn(es)
            b.es = old

    def _l0attn(self, es):
        b = self
        nc = self.nc
        tag = "l0a_"
        A = lambda n, s, dt: es.enter_context(nc.sbuf_tensor(tag + n, s, dt))
        r_, nmr = b.emit_ln_stats()
        gain0 = A("gain0", [128, D], BF16)
        gain0o = A("gain0o", [128, D], BF16)

        @b.blk.gpsimd
        def _(g):
            bc = lambda t_: bass.AP(t_, 0, [[0, 128], [1, D]])
            b.lg_dma(g, gain0[:, :], bc(b.l0ng))
            b.lg_dma(g, gain0o[:, :], bc(b.l0og))

        k2 = A("k2", [128, T], BF16)
        vt1 = A("vt1", [128, NT * 65], BF16)
        rq = A("rq", [128, NT * 8], F32)
        rk = A("rk", [128, NT], F32)
        sqq = A("sqq", [128, NT * 8], F32)
        ksq = A("ksq", [128, 64], F32)
        woutp = A("woutp", [64, 8 * D], BF16)
        q_fm = A("qfm", [128, 4 * T], BF16)
        k_fm = A("kfm", [128, T], BF16)
        osb = A("osb", [64, 8 * 512], BF16)
        with contextlib.ExitStack() as es1:
            A1 = lambda n, s, dt: es1.enter_context(nc.sbuf_tensor(tag + n, s, dt))
            tmp = A1("tmp", [128, 2 * D], BF16)
            fmr = A1("fmr", [128, 2 * D], BF16)
            q_tm = A1("qtm", [128, NT * 512], BF16)
            wq = A1("wq", [128, NCH * 512], BF16)
            wkv = A1("wkv", [128, NCH * 128], BF16)
            sqs = [A1(f"sqs{i}", [128, 512], BF16) for i in range(2)]

            @b.blk.gpsimd
            def _(g):
                b.lg_dma(g, wq[:, :], bass.AP(
                    b.wq0, 0, [[512, 128], [128 * 512, NCH], [1, 512]]))
                b.lg_dma(g, wkv[:, :], bass.AP(
                    b.wkv0, 0, [[128, 128], [128 * 128, NCH], [1, 128]]))
                b.lg_dma(g, woutp[:, :], bass.AP(
                    b.wout0, 0, [[D, 64], [64 * D, 8], [1, D]]))
                g.wait_ge(b.s_v.h, b.s_v.n)
                g.memset(bass.AP(vt1, 64, [[vt1[:, :].ap[0][0], 128],
                                           [65, NT], [1, 1]]),
                         1.0).then_inc(b.s_g.bump(1), 1)
            ld_w = b.s_lg.n

            with (nc.psum_tensor(tag + "pq0", [128, 512], F32) as pq0,
                  nc.psum_tensor(tag + "pq1", [128, 512], F32) as pq1,
                  nc.psum_tensor(tag + "pk0", [128, 128], F32) as pk0,
                  nc.psum_tensor(tag + "pk1", [128, 128], F32) as pk1):
                pqs, pks = [pq0, pq1], [pk0, pk1]
                vLn, trR, mmq, vq, aq, vq2 = [], [], [], [], [], []
                for ti in range(NT):
                    @b.blk.vector
                    def _(v, ti=ti):
                        if ti >= 2:
                            v.wait_ge(b.s_tr.h, trR[ti - 2])
                        v.tensor_scalar(tmp[:, (ti % 2) * D:(ti % 2 + 1) * D],
                                        b.xres_tile(ti), r_[:, ti:ti + 1],
                                        nmr[:, ti:ti + 1], ALU.mult, ALU.add)
                        v.drain()
                        v.tensor_mul(tmp[:, (ti % 2) * D:(ti % 2 + 1) * D],
                                     tmp[:, (ti % 2) * D:(ti % 2 + 1) * D],
                                     gain0[:, :])
                        v.drain().then_inc(b.s_v.bump(1), 1)
                    vLn.append(b.s_v.n)

                    @b.blk.sync
                    def _(sy, ti=ti):
                        sy.wait_ge(b.s_v.h, vLn[ti])
                        if ti >= 2:
                            sy.wait_ge(b.s_mm.h, mmq[ti - 2])
                        for c in range(NCH):
                            sy.dma_start_transpose(
                                fmr[:, (ti % 2) * D + c * 128:
                                    (ti % 2) * D + (c + 1) * 128],
                                tmp[:, (ti % 2) * D + c * 128:
                                    (ti % 2) * D + (c + 1) * 128],
                            ).then_inc(b.s_tr.bump(16), 16)
                        sy.wait_ge(b.s_tr.h, b.s_tr.n)
                    trR.append(b.s_tr.n)

                    @b.blk.tensor
                    def _(t, ti=ti):
                        if ti == 0:
                            t.wait_ge(b.s_lg.h, ld_w)
                            t.wait_ge(b.s_g.h, b.s_g.n)
                        t.wait_ge(b.s_tr.h, trR[ti])
                        if ti >= 2:
                            t.wait_ge(b.s_a.h, aq[ti - 2])
                            t.wait_ge(b.s_v.h, vq[ti - 2])
                        for c in range(NCH):
                            t.matmul(pqs[ti % 2][:, :],
                                     fmr[:, (ti % 2) * D + c * 128:
                                         (ti % 2) * D + (c + 1) * 128],
                                     wq[:, c * 512:(c + 1) * 512],
                                     start=(c == 0), stop=(c == NCH - 1))
                        for c in range(NCH):
                            mmi = t.matmul(pks[ti % 2][:, :],
                                           fmr[:, (ti % 2) * D + c * 128:
                                               (ti % 2) * D + (c + 1) * 128],
                                           wkv[:, c * 128:(c + 1) * 128],
                                           start=(c == 0), stop=(c == NCH - 1))
                        mmi.then_inc(b.s_mm.bump(1), 1)
                    mmq.append(b.s_mm.n)

                    @b.blk.scalar
                    def _(s, ti=ti):
                        s.wait_ge(b.s_mm.h, mmq[ti])
                        if ti >= 2:
                            s.wait_ge(b.s_v.h, vq2[ti - 2])
                        s.activation(sqs[ti % 2][:, :], pqs[ti % 2][:, :],
                                     AF.Square)
                        s.drain().then_inc(b.s_a.bump(1), 1)
                    aq.append(b.s_a.n)

                    @b.blk.vector
                    def _(v, ti=ti):
                        v.wait_ge(b.s_mm.h, mmq[ti])
                        v.tensor_copy(q_tm[:, ti * 512:(ti + 1) * 512],
                                      pqs[ti % 2][:, :])
                        v.drain()
                        v.tensor_copy(k2[:, ti * 128: ti * 128 + 64],
                                      pks[ti % 2][:, 0:64])
                        v.drain()
                        v.tensor_copy(vt1[:, ti * 65: ti * 65 + 64],
                                      pks[ti % 2][:, 64:128])
                        v.drain().then_inc(b.s_v.bump(1), 1)
                    vq.append(b.s_v.n)

                    @b.blk.vector
                    def _(v, ti=ti):
                        v.wait_ge(b.s_a.h, aq[ti])
                        qv = bass.AP(sqs[ti % 2], 0,
                                     [[sqs[ti % 2][:, :].ap[0][0], 128], [64, 8], [1, 64]])
                        v.tensor_reduce(sqq[:, ti * 8:(ti + 1) * 8], qv,
                                        axis=AXX, op=ALU.add)
                        v.drain().then_inc(b.s_v.bump(1), 1)
                    vq2.append(b.s_v.n)

                @b.blk.vector
                def _(v):
                    v.wait_ge(b.s_v.h, vq[-1])
                    for ti in range(NT):
                        kk = k2[:, ti * 128: ti * 128 + 64]
                        v.tensor_mul(ksq[:, :], kk, kk)
                        v.drain()
                        v.tensor_reduce(rk[:, ti:ti + 1], ksq[:, :],
                                        axis=AXX, op=ALU.add)
                        v.drain()
                    v.drain().then_inc(b.s_v.bump(1), 1)

                @b.blk.scalar
                def _(s):
                    s.wait_ge(b.s_v.h, b.s_v.n)
                    s.activation(sqq[:, :], sqq[:, :], AF.Sqrt, bias=b.ceps[:, 1:2])
                    s.drain()
                    s.activation(rk[:, :], rk[:, :], AF.Sqrt, bias=b.ceps[:, 1:2])
                    s.drain().then_inc(b.s_a.bump(1), 1)

                @b.blk.vector
                def _(v):
                    v.wait_ge(b.s_a.h, b.s_a.n)
                    v.reciprocal(sqq[:, :], sqq[:, :])
                    v.drain()
                    v.reciprocal(rk[:, :], rk[:, :])
                    v.drain()
                    v.tensor_scalar_mul(sqq[:, :], sqq[:, :], 4.0)
                    v.drain()
                    v.tensor_scalar_mul(rk[:, :], rk[:, :], 4.0)
                    v.drain()
                    qb = bass.AP(sqq, 0, [[sqq[:, :].ap[0][0], 128], [1, NT * 8], [0, 64]])
                    qv2 = bass.AP(q_tm, 0, [[q_tm[:, :].ap[0][0], 128], [64, NT * 8], [1, 64]])
                    v.tensor_tensor(qv2, qv2, qb, ALU.mult)
                    v.drain()
                    kb = bass.AP(rk, 0, [[rk[:, :].ap[0][0], 128], [1, NT], [0, 64]])
                    k_s = bass.AP(k2, 0, [[k2[:, :].ap[0][0], 128], [128, NT], [1, 64]])
                    k_d = bass.AP(k2, 64, [[k2[:, :].ap[0][0], 128], [128, NT], [1, 64]])
                    v.tensor_tensor(k_s, k_s, kb, ALU.mult)
                    v.drain()
                    v.tensor_copy(k_d, k_s)
                    v.drain().then_inc(b.s_v.bump(1), 1)
                v_scale_done = b.s_v.n

            @b.blk.sync
            def _(sy):
                sy.wait_ge(b.s_v.h, v_scale_done)
                for ti in range(NT):
                    for hp in range(4):
                        sy.dma_start_transpose(
                            q_fm[:, hp * T + ti * 128:hp * T + (ti + 1) * 128],
                            q_tm[:, ti * 512 + hp * 128: ti * 512 + (hp + 1) * 128],
                        ).then_inc(b.s_tr.bump(16), 16)
                    sy.dma_start_transpose(
                        k_fm[:, ti * 128:(ti + 1) * 128],
                        k2[:, ti * 128:(ti + 1) * 128],
                    ).then_inc(b.s_tr.bump(16), 16)
                    if ti % 4 == 3:
                        sy.wait_ge(b.s_tr.h, b.s_tr.n)
                sy.wait_ge(b.s_tr.h, b.s_tr.n)
            tr_qk = b.s_tr.n

        et = A("et", [128, 32 * 512], BF16)
        gout = A("gout", [128, D], BF16)
        rsum = A("rsum", [1, 512], F32)
        bnC = b.STATS[:, 600:624]
        mvC = b.STATS[:, 624:626]
        rC3 = b.STATS[:, 626:627]
        nmC = b.STATS[:, 627:628]
        varC = bass.AP(b.STATS, 625, [[768, 128], [1, 1]])
        muC = bass.AP(b.STATS, 624, [[768, 128], [1, 1]])
        mmO_prev = [0]
        mmP_last = [0]
        aE_last = [0, 0]
        vN_last = [0, 0]
        vP_last = [0, 0]
        with (nc.psum_tensor(tag + "ps0", [128, 512], F32) as ps0,
              nc.psum_tensor(tag + "ps1", [128, 512], F32) as ps1,
              nc.psum_tensor(tag + "po0", [65, 512], F32) as po0,
              nc.psum_tensor(tag + "po1", [65, 512], F32) as po1,
              nc.psum_tensor(tag + "pbc", [64, 512], F32) as pbc,
              nc.psum_tensor(tag + "pp0", [128, 512], F32) as pp0,
              nc.psum_tensor(tag + "pp1", [128, 512], F32) as pp1):
            pss, pos = [ps0, ps1], [po0, po1]
            for qt in range(8):
                mmS_, aE, mmO, vN, mmB, aO, vO = [], [], [], [], [], [], []
                if qt == 0:
                    aO_last = [0, 0]
                for h in range(8):
                    hb = (h % 2) * 64
                    hp = h // 2

                    aE_base = b.s_a.n

                    @b.blk.tensor
                    def _(t, qt=qt, h=h, hb=hb, hp=hp, aE_base=aE_base):
                        if qt == 0 and h == 0:
                            t.wait_ge(b.s_tr.h, tr_qk)
                        for kc in range(32):
                            if kc >= 2:
                                t.wait_ge(b.s_a.h, aE_base + kc - 1)
                            elif h == 0 and qt == 0:
                                pass
                            else:
                                t.wait_ge(b.s_a.h, aE_base - 2 + kc + 1)
                            t.matmul(pss[kc % 2][:, :],
                                     k_fm[hb:hb + 64, kc * 128:(kc + 1) * 128],
                                     q_fm[hb:hb + 64,
                                          hp * T + qt * 512:hp * T + (qt + 1) * 512],
                                     start=True, stop=True,
                                     ).then_inc(b.s_mm.bump(1), 1)
                    mmS_.append(b.s_mm.n)

                    @b.blk.scalar
                    def _(s, qt=qt, h=h):
                        base = mmS_[h] - 32
                        if h >= 1 or qt > 0:
                            s.wait_ge(b.s_mm.h, mmO_prev[0])
                        for kc in range(32):
                            s.wait_ge(b.s_mm.h, base + kc + 1)
                            s.activation(et[:, kc * 512:(kc + 1) * 512],
                                         pss[kc % 2][:, :], AF.Exp,
                                         bias=b.ceps[:, 2:3],
                                         ).then_inc(b.s_a.bump(1), 1)
                        s.drain().then_inc(b.s_a.bump(1), 1)
                    aE.append(b.s_a.n)

                    @b.blk.tensor
                    def _(t, qt=qt, h=h):
                        t.wait_ge(b.s_a.h, aE[h])
                        if h >= 2:
                            t.wait_ge(b.s_v.h, vN[h - 2])
                            t.wait_ge(b.s_a.h, aO[h - 2])
                        elif qt > 0:
                            t.wait_ge(b.s_v.h, vN_last[h])
                            t.wait_ge(b.s_a.h, aO_last[h])
                        for kc in range(32):
                            mmi = t.matmul(pos[h % 2][:, :],
                                           vt1[:, kc * 65:(kc + 1) * 65],
                                           et[:, kc * 512:(kc + 1) * 512],
                                           start=(kc == 0), stop=(kc == 31))
                        mmi.then_inc(b.s_mm.bump(1), 1)
                        mmO_prev[0] = b.s_mm.n
                    mmO.append(b.s_mm.n)

                    @b.blk.vector
                    def _(v, qt=qt, h=h):
                        v.wait_ge(b.s_mm.h, mmO[h])
                        if h >= 1:
                            v.wait_ge(b.s_mm.h, mmB[h - 1])
                        with nc.allow_low_precision(reason="softmax recip"):
                            v.reciprocal(rsum[0:1, :], pos[h % 2][64:65, :])
                        v.drain().then_inc(b.s_v.bump(1), 1)
                    vN.append(b.s_v.n)

                    @b.blk.tensor
                    def _(t, qt=qt, h=h):
                        t.wait_ge(b.s_v.h, vN[h])
                        if h >= 1:
                            t.wait_ge(b.s_v.h, vO[h - 1])
                        t.matmul(pbc[:, :], b.onesf[0:1, 0:64], rsum[0:1, :],
                                 start=True, stop=True).then_inc(b.s_mm.bump(1), 1)
                    mmB.append(b.s_mm.n)

                    @b.blk.scalar
                    def _(s, qt=qt, h=h):
                        s.wait_ge(b.s_mm.h, mmO[h])
                        if qt > 0 and h == 0:
                            s.wait_ge(b.s_mm.h, mmP_last[0])
                        s.activation(osb[0:64, h * 512:(h + 1) * 512],
                                     pos[h % 2][0:64, :], AF.Copy)
                        s.drain().then_inc(b.s_a.bump(1), 1)
                    aO.append(b.s_a.n)

                    @b.blk.vector
                    def _(v, qt=qt, h=h):
                        v.wait_ge(b.s_mm.h, mmB[h])
                        v.wait_ge(b.s_a.h, aO[h])
                        v.tensor_tensor(osb[0:64, h * 512:(h + 1) * 512],
                                        osb[0:64, h * 512:(h + 1) * 512],
                                        pbc[0:64, :], ALU.mult)
                        v.drain().then_inc(b.s_v.bump(1), 1)
                    vO.append(b.s_v.n)
                aE_last = aE[-2:]
                vN_last = vN[-2:]
                aO_last = aO[-2:]

                mmP, vP = [], []
                for sub in range(4):
                    ti = qt * 4 + sub

                    @b.blk.tensor
                    def _(t, qt=qt, sub=sub):
                        if sub == 0:
                            t.wait_ge(b.s_v.h, vO[-1])
                        if sub >= 1:
                            t.wait_ge(b.s_v.h, vP[sub - 1])
                        elif qt > 0:
                            t.wait_ge(b.s_v.h, vP_last[-1])
                        for half, pps in ((0, pp0), (1, pp1)):
                            for h in range(8):
                                mmi = t.matmul(
                                    pps[:, :],
                                    osb[0:64, h * 512 + sub * 128:
                                        h * 512 + (sub + 1) * 128],
                                    woutp[0:64, h * D + half * 512:
                                          h * D + (half + 1) * 512],
                                    start=(h == 0), stop=(h == 7))
                            mmi.then_inc(b.s_mm.bump(1), 1)
                    mmP.append(b.s_mm.n)
                    mmP_last[0] = b.s_mm.n

                    @b.blk.vector
                    def _(v, qt=qt, sub=sub, ti=ti):
                        v.wait_ge(b.s_mm.h, mmP[sub])
                        v.bn_stats(bnC[:, 0:6], pp0[:, :])
                        v.bn_stats(bnC[:, 6:12], pp1[:, :])
                        v.drain()
                        v.bn_aggr(mvC[:, :], bnC[:, 0:12])
                        v.drain().then_inc(b.s_v.bump(1), 1)

                    @b.blk.scalar
                    def _(s, qt=qt, sub=sub):
                        s.wait_ge(b.s_v.h, b.s_v.n)
                        s.activation(rC3[:, :], varC, AF.Sqrt, bias=b.ceps[:, 0:1])
                        s.drain().then_inc(b.s_a.bump(1), 1)

                    @b.blk.vector
                    def _(v, qt=qt, sub=sub, ti=ti):
                        v.wait_ge(b.s_a.h, b.s_a.n)
                        v.reciprocal(rC3[:, :], rC3[:, :])
                        v.drain()
                        v.scalar_tensor_tensor(nmC[:, :], muC, -1.0, rC3[:, :],
                                               ALU.mult, ALU.mult)
                        v.drain()
                        for half, pps in ((0, pp0), (1, pp1)):
                            v.tensor_scalar(gout[:, half * 512:(half + 1) * 512],
                                            pps[:, :], rC3[:, 0:1], nmC[:, 0:1],
                                            ALU.mult, ALU.add)
                        v.drain()
                        v.tensor_mul(gout[:, :], gout[:, :], gain0o[:, :])
                        v.drain()
                        v.tensor_add(b.xres_tile(ti), b.xres_tile(ti), gout[:, :])
                        v.drain().then_inc(b.s_v.bump(1), 1)
                    vP.append(b.s_v.n)
                vP_last = vP[-2:]


    def emit_l1attn(self):
        b = self
        nc = self.nc
        with contextlib.ExitStack() as es:
            old = b.es
            b.es = es
            self._l1attn(es)
            b.es = old

    def _l1attn(self, es):
        b = self
        nc = self.nc
        tag = "l1a_"
        A = lambda n, s, dt: es.enter_context(nc.sbuf_tensor(tag + n, s, dt))
        r_, nmr = b.emit_ln_stats()

        wq = A("wq", [128, NCH * D], BF16)
        wk = A("wk", [128, NCH * D], BF16)
        wv = A("wv", [128, NCH * D], BF16)
        wo = A("wo", [128, NCH * D], BF16)
        wgm = A("wgm", [128, NCH * 16], BF16)
        wf1 = A("wf1", [128, NCH * 64], BF16)
        wf2 = A("wf2", [128, D], BF16)
        wog1 = A("wog1", [128, NCH * 64], BF16)
        wog2 = A("wog2", [128, D], BF16)
        ctu = A("ctu", [128, 128], F32)

        @b.blk.gpsimd
        def _(g):
            chl = lambda t_, w_: bass.AP(t_, 0, [[w_, 128], [128 * w_, NCH], [1, w_]])
            b.lg_dma(g, wq[:, :], chl(b.wq1, D))
            b.lg_dma(g, wk[:, :], chl(b.wk1, D))
            b.lg_dma(g, wv[:, :], chl(b.wv1, D))
            b.lg_dma(g, wo[:, :], chl(b.wo1, D))
            b.lg_dma(g, wgm[:, :], chl(b.wgam, 16))
            b.lg_dma(g, wf1[:, :], chl(b.wf1, 64))
            b.lg_dma(g, wog1[:, :], chl(b.wog1, 64))
            b.lg_dma(g, wf2[0:64, :], b.wf2.ap())
            b.lg_dma(g, wf2[64:128, :], b.wf2.ap())
            b.lg_dma(g, wog2[0:64, :], b.wog2.ap())
            b.lg_dma(g, wog2[64:128, :], b.wog2.ap())
            b.lg_dma(g, ctu[:, :], b.ctriu.ap())
        ld_w = b.s_lg.n

        S_sb = A("S", [128, 8 * 64], BF16)
        S0m = A("S0m", [128, 8 * 64], BF16)
        LrS = A("LrS", [128, 8 * 128], BF16)
        TjS = A("TjS", [128, 8 * 256], BF16)
        usbS = A("usbS", [128, 8 * 64], BF16)
        zeros = A("zeros", [128, 128], BF16)
        mskL = es.enter_context(nc.sbuf_tensor(tag + "mskL", [128, 128],
                                               mybir.dt.uint8))
        mskU = es.enter_context(nc.sbuf_tensor(tag + "mskU", [128, 128],
                                               mybir.dt.uint8))
        mskU1 = es.enter_context(nc.sbuf_tensor(tag + "mskU1", [128, 128],
                                                mybir.dt.uint8))

        @b.blk.gpsimd
        def _(g):
            g.memset(zeros[:, :], 0.0)
            g.memset(S_sb[:, :], 0.0).then_inc(b.s_g.bump(1), 1)
            b.lg_dma(g, mskL[:, :], b.cmskl.ap())
            b.lg_dma(g, mskU[:, :], b.cmsku.ap())
            b.lg_dma(g, mskU1[:, :], b.cmsku1.ap())

        RD = 2
        qf = [A(f"qf{i}", [128, 8 * 128], BF16) for i in range(RD)]
        kcf = [A(f"kcf{i}", [128, 8 * 128], BF16) for i in range(RD)]
        bhf = [A(f"bhf{i}", [128, 8 * 128], BF16) for i in range(RD)]
        kct = [A(f"kct{i}", [128, D], BF16) for i in range(RD)]
        vtm = [A(f"vtm{i}", [128, D], BF16) for i in range(RD)]
        ogt = [A(f"ogt{i}", [128, D], BF16) for i in range(RD)]
        otm = [A(f"otm{i}", [128, D], BF16) for i in range(RD)]
        pml = [A(f"pml{i}", [128, 24], F32) for i in range(RD)]
        xln2 = A("xln2", [128, D], BF16)
        fmr2 = A("fmr2", [128, D], BF16)
        emf = fmr2
        yfm = xln2
        qs_tm2 = [A(f"qs{i}", [128, D], BF16) for i in range(2)]
        kr_tm2 = [A(f"kr{i}", [128, D], BF16) for i in range(2)]
        sp_ = A("sp", [128, D], F32)
        cs_ = A("cs", [128, D], F32)
        mid_ = A("mid", [1, D], F32)
        gam2 = [A(f"gam{i}", [128, 16], F32) for i in range(2)]
        t12 = A("t12", [128, 128], BF16)
        t12f = A("t12f", [128, 128], BF16)
        o12 = A("o12", [128, 128], BF16)
        o12f = A("o12f", [128, 128], BF16)

        rk2 = A("rk2", [128, 16], F32)
        st4 = b.STATS[:, 640:680]

        with contextlib.ExitStack() as pes:
            P = lambda n, s, dt=F32: pes.enter_context(
                nc.psum_tensor(tag + n, s, dt))
            pPJ = P("pPJ", [128, 1024])
            pSQ = P("pSQ", [128, 1024])
            pTT = P("pTT", [128, 1024])
            pUU = P("pUU", [128, 512])
            pB = P("pB", [128, 128], BF16)
            pCS = pPJ

            def fmc(sub, c):
                return fmr2[:, c * 128:(c + 1) * 128]

            def pre(pair):
                for sub in range(2):
                    ti = 2 * pair + sub
                    rd = ti % RD

                    @b.blk.vector
                    def _(v, ti=ti, sub=sub):
                        v.wait_ge(b.s_mm.h, b.s_mm.n)
                        if sub == 1:
                            v.wait_ge(b.s_tr.h, b.s_tr.n)
                        v.tensor_scalar(xln2[:, :],
                                        b.xres_tile(ti), r_[:, ti:ti + 1],
                                        nmr[:, ti:ti + 1], ALU.mult, ALU.add)
                        v.drain().then_inc(b.s_v.bump(1), 1)
                    v_ln = b.s_v.n

                    @b.blk.sync
                    def _(sy, sub=sub, v_ln=v_ln):
                        sy.wait_ge(b.s_v.h, v_ln)
                        for c in range(NCH):
                            sy.dma_start_transpose(
                                fmc(sub, c),
                                xln2[:, c * 128:(c + 1) * 128],
                            ).then_inc(b.s_tr.bump(16), 16)
                        sy.wait_ge(b.s_tr.h, b.s_tr.n)
                    tr_f = b.s_tr.n

                    @b.blk.tensor
                    def _(t, ti=ti, sub=sub, tr_f=tr_f):
                        if ti == 0:
                            t.wait_ge(b.s_lg.h, ld_w)
                            t.wait_ge(b.s_v.h, b.s_v.n)
                        t.wait_ge(b.s_tr.h, tr_f)
                        t.wait_ge(b.s_a.h, b.s_a.n)
                        t.wait_ge(b.s_v.h, b.s_v.n)
                        for half in range(2):
                            for c in range(NCH):
                                mmi = t.matmul(pPJ[:, half * 512:(half + 1) * 512],
                                               fmc(sub, c),
                                               wq[:, c * D + half * 512:
                                                  c * D + (half + 1) * 512],
                                               start=(c == 0), stop=(c == NCH - 1))
                        mmi.then_inc(b.s_mm.bump(1), 1)
                    mm_q = b.s_mm.n

                    @b.blk.scalar
                    def _(s, sub=sub, mm_q=mm_q):
                        s.wait_ge(b.s_mm.h, mm_q)
                        s.activation(qs_tm2[sub][:, :], pPJ[:, :],
                                     AF.Silu).then_inc(b.s_a.bump(1), 1)
                    a_q = b.s_a.n

                    @b.blk.tensor
                    def _(t, sub=sub, a_q=a_q):
                        t.wait_ge(b.s_a.h, a_q)
                        for half in range(2):
                            for c in range(NCH):
                                t.matmul(pPJ[:, half * 512:(half + 1) * 512],
                                         fmc(sub, c),
                                         wk[:, c * D + half * 512:
                                            c * D + (half + 1) * 512],
                                         start=(c == 0), stop=(c == NCH - 1))
                        for c in range(NCH):
                            t.matmul(pUU[:, 0:64], fmc(sub, c),
                                     wf1[:, c * 64:(c + 1) * 64],
                                     start=(c == 0), stop=(c == NCH - 1))
                        for c in range(NCH):
                            t.matmul(pUU[:, 64:128], fmc(sub, c),
                                     wog1[:, c * 64:(c + 1) * 64],
                                     start=(c == 0), stop=(c == NCH - 1))
                        for c in range(NCH):
                            mmi = t.matmul(pUU[:, 128:144], fmc(sub, c),
                                           wgm[:, c * 16:(c + 1) * 16],
                                           start=(c == 0), stop=(c == NCH - 1))
                        mmi.then_inc(b.s_mm.bump(1), 1)
                    mm_p1 = b.s_mm.n

                    @b.blk.scalar
                    def _(s, sub=sub, mm_p1=mm_p1):
                        s.wait_ge(b.s_mm.h, mm_p1)
                        s.activation(kr_tm2[sub][:, :], pPJ[:, :], AF.Silu)
                        s.drain()
                        s.activation(gam2[sub][:, :], pUU[:, 128:144], AF.Sigmoid)
                        s.drain().then_inc(b.s_a.bump(1), 1)
                    a_p1 = b.s_a.n

                    @b.blk.vector
                    def _(v, ti=ti, sub=sub, rd=rd, a_p1=a_p1, mm_p1=mm_p1):
                        v.wait_ge(b.s_a.h, a_p1)
                        v.tensor_copy(t12[:, sub * 64:(sub + 1) * 64],
                                      pUU[:, 0:64])
                        v.drain()
                        v.tensor_copy(o12[:, sub * 64:(sub + 1) * 64],
                                      pUU[:, 64:128])
                        v.drain()
                        v.tensor_scalar_mul(gam2[sub][:, :], gam2[sub][:, :], -1.0)
                        v.drain()
                        ksq2 = otm[rd]
                        v.tensor_mul(ksq2[:, :], kr_tm2[sub][:, :], kr_tm2[sub][:, :])
                        v.drain()
                        kv3 = bass.AP(ksq2, 0, [[ksq2[:, :].ap[0][0], 128],
                                                [64, 16], [1, 64]])
                        v.tensor_reduce(rk2[:, :], kv3, axis=AXX, op=ALU.add)
                        v.drain().then_inc(b.s_v.bump(1), 1)

                    @b.blk.scalar
                    def _(s):
                        s.wait_ge(b.s_v.h, b.s_v.n)
                        s.activation(rk2[:, :], rk2[:, :], AF.Sqrt,
                                     bias=b.ceps[:, 1:2])
                        s.drain().then_inc(b.s_a.bump(1), 1)

                    @b.blk.vector
                    def _(v, rd=rd, sub=sub):
                        v.wait_ge(b.s_a.h, b.s_a.n)
                        v.reciprocal(rk2[:, :], rk2[:, :])
                        v.drain()
                        rb = bass.AP(rk2, 0, [[rk2[:, :].ap[0][0], 128],
                                              [1, 16], [0, 64]])
                        kv4 = bass.AP(kr_tm2[sub], 0,
                                      [[kr_tm2[sub][:, :].ap[0][0], 128],
                                       [64, 16], [1, 64]])
                        v.tensor_tensor(kv4, kv4, rb, ALU.mult)
                        v.drain()
                        v.tensor_copy(kct[rd][:, :], kr_tm2[sub][:, :])
                        v.drain().then_inc(b.s_v.bump(1), 1)
                    v_krdy = b.s_v.n

                    @b.blk.tensor
                    def _(t, sub=sub, v_krdy=v_krdy):
                        t.wait_ge(b.s_v.h, v_krdy)
                        for half in range(2):
                            for c in range(NCH):
                                mmi = t.matmul(pPJ[:, half * 512:(half + 1) * 512],
                                               fmc(sub, c),
                                               wv[:, c * D + half * 512:
                                                  c * D + (half + 1) * 512],
                                               start=(c == 0), stop=(c == NCH - 1))
                        mmi.then_inc(b.s_mm.bump(1), 1)
                    mm_v = b.s_mm.n

                    @b.blk.vector
                    def _(v, rd=rd, mm_v=mm_v):
                        v.wait_ge(b.s_mm.h, mm_v)
                        v.tensor_copy(vtm[rd][:, :], pPJ[:, :])
                        v.drain().then_inc(b.s_v.bump(1), 1)
                    v_v = b.s_v.n

                @b.blk.sync
                def _(sy):
                    sy.wait_ge(b.s_v.h, b.s_v.n)
                    sy.dma_start_transpose(t12f[:, :], t12[:, :]
                                           ).then_inc(b.s_tr.bump(16), 16)
                    sy.dma_start_transpose(o12f[:, :], o12[:, :]
                                           ).then_inc(b.s_tr.bump(16), 16)
                    sy.wait_ge(b.s_tr.h, b.s_tr.n)
                tr_t12 = b.s_tr.n

                for sub in range(2):
                    ti = 2 * pair + sub
                    rd = ti % RD
                    bs = sub * 64

                    @b.blk.tensor
                    def _(t, sub=sub, bs=bs, tr_t12=tr_t12):
                        t.wait_ge(b.s_tr.h, tr_t12)
                        t.wait_ge(b.s_v.h, b.s_v.n)
                        t.wait_ge(b.s_a.h, b.s_a.n)
                        for half in range(2):
                            mmi = t.matmul(pPJ[:, half * 512:(half + 1) * 512],
                                           t12f[bs:bs + 64, :],
                                           wf2[bs:bs + 64, half * 512:
                                               (half + 1) * 512],
                                           start=True, stop=True)
                        mmi.then_inc(b.s_mm.bump(1), 1)
                    mm_z = b.s_mm.n

                    @b.blk.scalar
                    def _(s, mm_z=mm_z):
                        s.wait_ge(b.s_mm.h, mm_z)
                        s.activation(sp_[:, :], pPJ[:, :], AF.Exp, scale=-1.0)
                        s.drain()
                        s.activation(sp_[:, :], sp_[:, :], AF.Ln, bias=1.0)
                        s.drain().then_inc(b.s_a.bump(1), 1)
                    a_sp = b.s_a.n

                    v_cs = None
                    for half in range(2):
                        @b.blk.tensor
                        def _(t, half=half, a_sp=a_sp):
                            t.wait_ge(b.s_a.h, a_sp)
                            if half == 1:
                                t.wait_ge(b.s_v.h, b.s_v.n)
                            t.matmul(pCS[:, 0:512], ctu[:, :],
                                     sp_[:, half * 512:(half + 1) * 512],
                                     start=True, stop=True
                                     ).then_inc(b.s_mm.bump(1), 1)
                        mm_csh = b.s_mm.n

                        @b.blk.vector
                        def _(v, half=half, mm_csh=mm_csh):
                            v.wait_ge(b.s_mm.h, mm_csh)
                            v.tensor_copy(cs_[:, half * 512:(half + 1) * 512],
                                          pCS[:, 0:512])
                            v.drain().then_inc(b.s_v.bump(1), 1)
                    v_cs = b.s_v.n

                    @b.blk.scalar
                    def _(s, rd=rd, v_cs=v_cs):
                        s.wait_ge(b.s_v.h, v_cs)
                        s.activation(ogt[rd][:, :], cs_[:, :], AF.Exp, scale=-1.0)
                        s.drain().then_inc(b.s_a.bump(1), 1)
                    a_emm = b.s_a.n

                    @b.blk.gpsimd
                    def _(g, v_cs=v_cs):
                        g.wait_ge(b.s_v.h, v_cs)
                        g.dma_start(out=mid_[0:1, :], in_=cs_[63:64, :]
                                    ).then_inc(b.s_ea.bump(16), 16)
                    g_mid = b.s_ea.n

                    mmh = []
                    for half in range(2):
                        @b.blk.tensor
                        def _(t, half=half, g_mid=g_mid):
                            t.wait_ge(b.s_ea.h, g_mid)
                            if half == 1:
                                t.wait_ge(b.s_v.h, b.s_v.n)
                            t.matmul(pCS[:, 0:512], b.onesf[0:1, :],
                                     mid_[0:1, half * 512:(half + 1) * 512],
                                     start=True, stop=True
                                     ).then_inc(b.s_mm.bump(1), 1)
                        mmh.append(b.s_mm.n)

                        @b.blk.vector
                        def _(v, half=half, a_emm=a_emm, mmh=tuple(mmh)):
                            v.wait_ge(b.s_mm.h, mmh[-1])
                            if half == 0:
                                v.wait_ge(b.s_a.h, a_emm)
                            v.tensor_sub(cs_[:, half * 512:(half + 1) * 512],
                                         cs_[:, half * 512:(half + 1) * 512],
                                         pCS[:, 0:512])
                            v.drain().then_inc(b.s_v.bump(1), 1)
                    mm_mid = b.s_mm.n

                    v_sc = b.s_v.n

                    @b.blk.scalar
                    def _(s, v_sc=v_sc):
                        s.wait_ge(b.s_v.h, v_sc)
                        s.activation(pPJ[:, :], cs_[:, :], AF.Exp,
                                     scale=-1.0).then_inc(b.s_a.bump(1), 1)
                    a_lam = b.s_a.n

                    @b.blk.vector
                    def _(v, rd=rd, sub=sub, a_lam=a_lam):
                        v.wait_ge(b.s_a.h, a_lam)
                        v.tensor_mul(qs_tm2[sub][:, :], qs_tm2[sub][:, :],
                                     pPJ[:, :])
                        v.drain().then_inc(b.s_v.bump(1), 1)
                    v_qt = b.s_v.n

                    @b.blk.scalar
                    def _(s, v_qt=v_qt):
                        s.wait_ge(b.s_v.h, v_qt)
                        s.activation(pPJ[:, :], cs_[:, :],
                                     AF.Exp).then_inc(b.s_a.bump(1), 1)
                    a_en = b.s_a.n

                    @b.blk.vector
                    def _(v, rd=rd, a_en=a_en):
                        v.wait_ge(b.s_a.h, a_en)
                        v.tensor_mul(kct[rd][:, :], kct[rd][:, :], pPJ[:, :])
                        v.drain()
                        v.tensor_add(sp_[:, :], sp_[:, :], cs_[:, :])
                        v.drain().then_inc(b.s_v.bump(1), 1)
                    v_kap = b.s_v.n

                    @b.blk.scalar
                    def _(s, rd=rd, v_kap=v_kap):
                        s.wait_ge(b.s_v.h, v_kap)
                        s.activation(ogt[rd][96:128, :], cs_[96:128, :],
                                     AF.Exp, scale=-1.0)
                        s.drain()
                        s.activation(pPJ[:, :], sp_[:, :], AF.Exp,
                                     scale=-1.0).then_inc(b.s_a.bump(1), 1)
                    a_e1 = b.s_a.n

                    @b.blk.vector
                    def _(v, rd=rd, sub=sub, a_e1=a_e1):
                        v.wait_ge(b.s_a.h, a_e1)
                        v.tensor_mul(kr_tm2[sub][:, :], kr_tm2[sub][:, :],
                                     pPJ[:, :])
                        v.drain()
                        gb = bass.AP(gam2[sub], 0, [[gam2[sub][:, :].ap[0][0], 128],
                                                    [1, 16], [0, 64]])
                        kv5 = bass.AP(kr_tm2[sub], 0,
                                      [[kr_tm2[sub][:, :].ap[0][0], 128],
                                       [64, 16], [1, 64]])
                        v.tensor_tensor(kv5, kv5, gb, ALU.mult)
                        v.drain().then_inc(b.s_v.bump(1), 1)
                    v_drv = b.s_v.n

                    @b.blk.sync
                    def _(sy, rd=rd, sub=sub, v_drv=v_drv):
                        sy.wait_ge(b.s_v.h, v_drv)
                        for c in range(NCH):
                            sy.dma_start_transpose(
                                qf[rd][:, c * 128:(c + 1) * 128],
                                qs_tm2[sub][:, c * 128:(c + 1) * 128],
                            ).then_inc(b.s_tr.bump(16), 16)
                        for c in range(NCH):
                            sy.dma_start_transpose(
                                kcf[rd][:, c * 128:(c + 1) * 128],
                                kct[rd][:, c * 128:(c + 1) * 128],
                            ).then_inc(b.s_tr.bump(16), 16)
                        for c in range(NCH):
                            sy.dma_start_transpose(
                                bhf[rd][:, c * 128:(c + 1) * 128],
                                kr_tm2[sub][:, c * 128:(c + 1) * 128],
                            ).then_inc(b.s_tr.bump(16), 16)
                        for c in range(NCH):
                            sy.dma_start_transpose(
                                emf[:, c * 128:(c + 1) * 128],
                                ogt[rd][:, c * 128:(c + 1) * 128],
                            ).then_inc(b.s_tr.bump(16), 16)
                        sy.wait_ge(b.s_tr.h, b.s_tr.n)
                    tr_drv = b.s_tr.n

                    @b.blk.vector
                    def _(v, rd=rd, tr_drv=tr_drv):
                        v.wait_ge(b.s_tr.h, tr_drv)
                        p63 = bass.AP(emf, 63, [[emf[:, :].ap[0][0], 128], [128, 8]])
                        p127 = bass.AP(emf, 127, [[emf[:, :].ap[0][0], 128], [128, 8]])
                        v.tensor_copy(pml[rd][:, 0:8], p63)
                        v.drain()
                        v.tensor_copy(pml[rd][:, 8:16], p127)
                        v.drain().then_inc(b.s_v.bump(1), 1)

                    @b.blk.tensor
                    def _(t, sub=sub, bs=bs):
                        t.wait_ge(b.s_v.h, b.s_v.n)
                        for half in range(2):
                            mmi = t.matmul(pPJ[:, half * 512:(half + 1) * 512],
                                           o12f[bs:bs + 64, :],
                                           wog2[bs:bs + 64, half * 512:
                                                (half + 1) * 512],
                                           start=True, stop=True)
                        mmi.then_inc(b.s_mm.bump(1), 1)
                    mm_og = b.s_mm.n

                    @b.blk.scalar
                    def _(s, rd=rd, mm_og=mm_og):
                        s.wait_ge(b.s_mm.h, mm_og)
                        s.activation(ogt[rd][:, :], pPJ[:, :], AF.Sigmoid)
                        s.drain().then_inc(b.s_a.bump(1), 1)

            def unit_group(ti, g):
                rd = ti % RD
                hs = [g * 8 + i for i in range(8)]

                def sl_bh(h):
                    return (h % 2) * 64

                def sl_S(h, t_):
                    bh = sl_bh(h)
                    return t_[bh:bh + 64, (h // 2) * 64:(h // 2) * 64 + 64]

                def sl_fm(h, t_):
                    bh = sl_bh(h)
                    return t_[bh:bh + 64, (h // 2) * 128:(h // 2) * 128 + 128]

                def cSQ(i):
                    return pSQ[:, i * 128:(i + 1) * 128]

                def cTT(i):
                    return pTT[:, i * 128:(i + 1) * 128]

                def cUU(i):
                    return pUU[:, i * 64:(i + 1) * 64]

                def cLr(i):
                    return LrS[:, i * 128:(i + 1) * 128]

                def cTj(i, j):
                    o = (i * 2 + (j & 1)) * 128
                    return TjS[:, o:o + 128]

                def cu(i):
                    return usbS[:, i * 64:(i + 1) * 64]

                @b.blk.vector
                def _(v):
                    v.wait_ge(b.s_v.h, b.s_v.n)
                    for i, h in enumerate(hs):
                        pm = pml[rd][sl_bh(h):sl_bh(h) + 64, h // 2:h // 2 + 1]
                        v.tensor_scalar(sl_S(h, S0m), sl_S(h, S_sb), pm, None,
                                        ALU.mult)
                    v.drain().then_inc(b.s_v.bump(1), 1)
                v_s0m = b.s_v.n
                a_prev = b.s_a.n
                if DBG_UG <= 0:
                    return

                pA_s = [pSQ[:, 0:128], pSQ[:, 512:640]]
                pU_s = [pUU[:, 0:64], pTT[:, 0:64]]
                mmB = {}
                vB = {-2: b.s_v.n, -1: b.s_v.n}
                aT0 = {-2: a_prev, -1: a_prev}
                mmT0 = {}
                for i, h in enumerate(hs):
                    s = i % 2

                    @b.blk.tensor
                    def _(t, i=i, h=h, s=s):
                        if i == 0:
                            t.wait_ge(b.s_v.h, v_s0m)
                            t.wait_ge(b.s_a.h, a_prev)
                        else:
                            t.wait_ge(b.s_v.h, vB[i - 2])
                        bf_ = sl_fm(h, bhf[rd])
                        kf_ = sl_fm(h, kcf[rd])
                        t.matmul(pU_s[s], bf_, sl_S(h, S0m),
                                 start=True, stop=True)
                        t.matmul(pA_s[s][:, 0:64], bf_, kf_[:, 0:64],
                                 start=True, stop=True)
                        mmi = t.matmul(pA_s[s][64:128, 64:128], bf_[:, 64:128],
                                       kf_[:, 64:128], start=True, stop=True,
                                       tile_position=(sl_bh(h), 64))
                        mmi.then_inc(b.s_mm.bump(1), 1)
                    mmB[i] = b.s_mm.n

                    @b.blk.vector
                    def _(v, i=i, h=h, s=s):
                        v.wait_ge(b.s_mm.h, mmB[i])
                        v.tensor_add(cu(i), vtm[rd][:, h * 64:(h + 1) * 64],
                                     pU_s[s])
                        v.drain()
                        v.select(cLr(i)[:, 0:64], mskL[:, 0:64],
                                 pA_s[s][:, 0:64], zeros[:, 0:64])
                        v.select(cLr(i)[64:128, 64:128], mskL[64:128, 64:128],
                                 pA_s[s][64:128, 64:128],
                                 zeros[64:128, 64:128])
                        v.memset(cLr(i)[0:64, 64:128], 0.0)
                        v.drain().then_inc(b.s_v.bump(1), 1)
                    vB[i] = b.s_v.n

                    @b.blk.tensor
                    def _(t, i=i):
                        t.wait_ge(b.s_v.h, vB[i])
                        t.wait_ge(b.s_a.h, aT0[i - 1])
                        t.transpose(pB[:, :], cLr(i)[:, :], b.ident[:, :]
                                    ).then_inc(b.s_mm.bump(1), 1)
                    mmT0[i] = b.s_mm.n

                    @b.blk.scalar
                    def _(s_, i=i):
                        s_.wait_ge(b.s_mm.h, mmT0[i])
                        s_.activation(cTj(i, 0), pB[:, :], AF.Copy)
                        s_.drain().then_inc(b.s_a.bump(1), 1)
                    aT0[i] = b.s_a.n
                v_ev0 = b.s_v.n
                a_ev0 = b.s_a.n
                if DBG_UG <= 2:
                    return

                aT = {0: a_ev0}
                vL = {0: v_ev0}
                mmA = {}
                vA = {}

                @b.blk.tensor
                def _(t):
                    t.wait_ge(b.s_v.h, v_ev0)
                    t.wait_ge(b.s_a.h, a_ev0)
                    for i in range(8):
                        mmi = t.matmul(cUU(i), cTj(i, 0), cu(i),
                                       start=True, stop=True)
                    mmi.then_inc(b.s_mm.bump(1), 1)
                mmA[0] = b.s_mm.n

                @b.blk.vector
                def _(v):
                    v.wait_ge(b.s_mm.h, mmA[0])
                    for i in range(8):
                        v.tensor_add(cu(i), cu(i), cUU(i))
                    v.drain().then_inc(b.s_v.bump(1), 1)
                vA[0] = b.s_v.n
                if DBG_UG <= 3:
                    return

                for j in range(1, 7):
                    @b.blk.tensor
                    def _(t, j=j):
                        t.wait_ge(b.s_a.h, aT[j - 1])
                        t.wait_ge(b.s_v.h, vL[j - 1])
                        for i in range(8):
                            if j == 6:
                                mmi = t.matmul(cTT(i)[0:64, 64:128],
                                               cLr(i)[:, 0:64],
                                               cTj(i, 5)[:, 64:128],
                                               start=True, stop=True)
                            else:
                                mmi = t.matmul(cTT(i), cLr(i), cTj(i, j - 1),
                                               start=True, stop=True)
                            if i == 3:
                                mmi.then_inc(b.s_mm.bump(1), 1)
                        mmi.then_inc(b.s_mm.bump(1), 1)
                    mmT_a = b.s_mm.n - 1
                    mmT = b.s_mm.n

                    if j < 6:
                        @b.blk.tensor
                        def _(t, j=j):
                            for i in range(8):
                                mmi = t.matmul(cSQ(i), cTj(i, j - 1), cLr(i),
                                               start=True, stop=True)
                            mmi.then_inc(b.s_mm.bump(1), 1)
                        mmS = b.s_mm.n

                    @b.blk.scalar
                    def _(s, j=j, mmT_a=mmT_a, mmT=mmT):
                        s.wait_ge(b.s_mm.h, mmT_a)
                        for i in range(4):
                            if j == 6:
                                s.activation(cTj(i, 6)[0:64, 64:128],
                                             cTT(i)[0:64, 64:128], AF.Copy)
                            else:
                                s.activation(cTj(i, j), cTT(i), AF.Copy)
                        s.drain().then_inc(b.s_a.bump(1), 1)
                        s.wait_ge(b.s_mm.h, mmT)
                        for i in range(4, 8):
                            if j == 6:
                                s.activation(cTj(i, 6)[0:64, 64:128],
                                             cTT(i)[0:64, 64:128], AF.Copy)
                            else:
                                s.activation(cTj(i, j), cTT(i), AF.Copy)
                        s.drain().then_inc(b.s_a.bump(1), 1)
                    aT_a = b.s_a.n - 1
                    aT[j] = b.s_a.n

                    if j < 6:
                        @b.blk.vector
                        def _(v, j=j, mmS=mmS):
                            v.wait_ge(b.s_mm.h, mmS)
                            for i in range(8):
                                v.tensor_copy(cLr(i), cSQ(i))
                            v.drain().then_inc(b.s_v.bump(1), 1)
                        vL[j] = b.s_v.n
                    else:
                        vL[j] = vL[j - 1]

                    @b.blk.tensor
                    def _(t, j=j, aT_a=aT_a):
                        t.wait_ge(b.s_a.h, aT_a)
                        t.wait_ge(b.s_v.h, vA[j - 1])
                        for i in range(8):
                            if i == 4:
                                t.wait_ge(b.s_a.h, aT[j])
                            if j == 6:
                                mmi = t.matmul(cUU(i)[64:128, :],
                                               cTj(i, 6)[0:64, 64:128],
                                               cu(i)[0:64, :],
                                               start=True, stop=True,
                                               tile_position=(0, 64))
                            else:
                                mmi = t.matmul(cUU(i), cTj(i, j), cu(i),
                                               start=True, stop=True)
                        mmi.then_inc(b.s_mm.bump(1), 1)
                    mmA[j] = b.s_mm.n

                    @b.blk.vector
                    def _(v, j=j):
                        v.wait_ge(b.s_mm.h, mmA[j])
                        for i in range(8):
                            if j == 6:
                                v.tensor_add(cu(i)[64:128, :], cu(i)[64:128, :],
                                             cUU(i)[64:128, :])
                            else:
                                v.tensor_add(cu(i), cu(i), cUU(i))
                        v.drain().then_inc(b.s_v.bump(1), 1)
                    vA[j] = b.s_v.n

                if DBG_UG <= 4:
                    return

                vPT = {}
                mmO1 = {}
                vO = {-2: vA[6], -1: vA[6]}
                aT6 = aT[6]
                for i, h in enumerate(hs):
                    s = i % 2

                    @b.blk.tensor
                    def _(t, i=i, h=h, s=s):
                        kf_ = sl_fm(h, kcf[rd])
                        qf_ = sl_fm(h, qf[rd])
                        if i == 0:
                            t.wait_ge(b.s_v.h, vA[6])
                            t.wait_ge(b.s_a.h, aT6)
                        else:
                            t.wait_ge(b.s_v.h, vO[i - 2])
                        t.matmul(pA_s[s][0:64, :], kf_[:, 0:64], qf_,
                                 start=True, stop=True)
                        t.matmul(pA_s[s][64:128, 64:128], kf_[:, 64:128],
                                 qf_[:, 64:128], start=True, stop=True,
                                 tile_position=(sl_bh(h), 64))
                        mmi = t.matmul(pU_s[s], qf_, sl_S(h, S0m),
                                       start=True, stop=True)
                        mmi.then_inc(b.s_mm.bump(1), 1)
                    mmPT = b.s_mm.n

                    @b.blk.vector
                    def _(v, i=i, h=h, s=s, mmPT=mmPT):
                        v.wait_ge(b.s_mm.h, mmPT)
                        v.select(cLr(i)[0:64, :], mskU[0:64, :],
                                 pA_s[s][0:64, :], zeros[0:64, :])
                        v.select(cLr(i)[64:128, 64:128], mskU[64:128, 64:128],
                                 pA_s[s][64:128, 64:128],
                                 zeros[64:128, 64:128])
                        v.memset(cLr(i)[64:128, 0:64], 0.0)
                        v.drain()
                        v.tensor_copy(otm[rd][:, h * 64:(h + 1) * 64], pU_s[s])
                        v.drain().then_inc(b.s_v.bump(1), 1)
                    vPT[i] = b.s_v.n

                    @b.blk.tensor
                    def _(t, i=i, h=h, s=s):
                        t.wait_ge(b.s_v.h, vPT[i])
                        t.matmul(cTT(i)[:, 64:128], cLr(i), cu(i),
                                 start=True, stop=True)
                        bh = sl_bh(h)
                        mmi = t.matmul(pU_s[s][bh:bh + 64, :] if bh else
                                       pU_s[s][0:64, :],
                                       kct[rd][:, h * 64:(h + 1) * 64], cu(i),
                                       start=True, stop=True,
                                       tile_position=(0, bh) if bh else None)
                        mmi.then_inc(b.s_mm.bump(1), 1)
                    mmO1[i] = b.s_mm.n

                    @b.blk.vector
                    def _(v, i=i, h=h, s=s):
                        v.wait_ge(b.s_mm.h, mmO1[i])
                        osl = otm[rd][:, h * 64:(h + 1) * 64]
                        v.tensor_add(osl, osl, cTT(i)[:, 64:128])
                        v.drain()
                        bh = sl_bh(h)
                        v.tensor_add(sl_S(h, S0m), sl_S(h, S0m),
                                     pU_s[s][bh:bh + 64, :])
                        v.drain()
                        lc = pml[rd][bh:bh + 64, 8 + h // 2:9 + h // 2]
                        v.tensor_scalar(sl_S(h, S_sb), sl_S(h, S0m), lc, None,
                                        ALU.mult)
                        v.drain().then_inc(b.s_v.bump(1), 1)
                    vO[i] = b.s_v.n

            def post(ti):
                rd = ti % RD
                bnB = b.STATS[:, 640:664]
                mvB = b.STATS[:, 664:666]
                rB = b.STATS[:, 666:667]
                nmB = b.STATS[:, 667:668]
                varB = bass.AP(b.STATS, 665, [[768, 128], [1, 1]])
                muB = bass.AP(b.STATS, 664, [[768, 128], [1, 1]])

                @b.blk.vector
                def _(v):
                    v.wait_ge(b.s_v.h, b.s_v.n)
                    v.wait_ge(b.s_a.h, b.s_a.n)
                    v.tensor_mul(otm[rd][:, :], otm[rd][:, :], ogt[rd][:, :])
                    v.drain()
                    for hh in range(2):
                        v.bn_stats(bnB[:, hh * 6:hh * 6 + 6],
                                   otm[rd][:, hh * 512:(hh + 1) * 512])
                    v.drain()
                    v.bn_aggr(mvB[:, :], bnB[:, 0:12])
                    v.drain().then_inc(b.s_v.bump(1), 1)

                @b.blk.scalar
                def _(s):
                    s.wait_ge(b.s_v.h, b.s_v.n)
                    s.activation(rB[:, :], varB, AF.Sqrt, bias=b.ceps[:, 0:1])
                    s.drain().then_inc(b.s_a.bump(1), 1)

                @b.blk.vector
                def _(v):
                    v.wait_ge(b.s_a.h, b.s_a.n)
                    v.reciprocal(rB[:, :], rB[:, :])
                    v.drain()
                    v.scalar_tensor_tensor(nmB[:, :], muB, -1.0, rB[:, :],
                                           ALU.mult, ALU.mult)
                    v.drain()
                    v.tensor_scalar(otm[rd][:, :], otm[rd][:, :], rB[:, 0:1],
                                    nmB[:, 0:1], ALU.mult, ALU.add)
                    v.drain()
                    v.tensor_mul(otm[rd][:, :], otm[rd][:, :], b.gain1[:, :])
                    v.drain().then_inc(b.s_v.bump(1), 1)
                v_y = b.s_v.n

                @b.blk.sync
                def _(sy, v_y=v_y):
                    sy.wait_ge(b.s_v.h, v_y)
                    for c in range(NCH):
                        sy.dma_start_transpose(
                            yfm[:, c * 128:(c + 1) * 128],
                            otm[rd][:, c * 128:(c + 1) * 128],
                        ).then_inc(b.s_tr.bump(16), 16)
                    sy.wait_ge(b.s_tr.h, b.s_tr.n)
                tr_y = b.s_tr.n

                @b.blk.tensor
                def _(t, tr_y=tr_y):
                    t.wait_ge(b.s_tr.h, tr_y)
                    t.wait_ge(b.s_v.h, b.s_v.n)
                    for half in range(2):
                        for c in range(NCH):
                            mmi = t.matmul(pPJ[:, half * 512:(half + 1) * 512],
                                           yfm[:, c * 128:(c + 1) * 128],
                                           wo[:, c * D + half * 512:
                                              c * D + (half + 1) * 512],
                                           start=(c == 0), stop=(c == NCH - 1))
                        mmi.then_inc(b.s_mm.bump(1), 1)
                mm_wo = b.s_mm.n

                @b.blk.vector
                def _(v, ti=ti, mm_wo=mm_wo):
                    v.wait_ge(b.s_mm.h, mm_wo)
                    for half in range(2):
                        xs = b.XRES[:, ti * D + half * 512: ti * D + (half + 1) * 512]
                        v.tensor_add(xs, xs, pPJ[:, half * 512:(half + 1) * 512])
                        v.drain()
                    v.drain().then_inc(b.s_v.bump(1), 1)

            for pair in range(NT // 2):
                pre(pair)
                for sub in range(2):
                    ti = 2 * pair + sub
                    for g in range(2):
                        unit_group(ti, g)
                    if DBG_L1A:
                        @b.blk.gpsimd
                        def _(g, ti=ti):
                            b.barrier_all(g)
                            g.dma_start(
                                out=bass.AP(b.e_tm, ti * 128 * NSLOT,
                                            [[NSLOT, 128], [1, D]]),
                                in_=otm[ti % RD][:, :]
                            ).then_inc(b.s_st.bump(16), 16)
                            g.dma_start(
                                out=bass.AP(b.e_sm, ti * 128 * NSLOT,
                                            [[NSLOT, 128], [1, D]]),
                                in_=ogt[ti % RD][:, :]
                            ).then_inc(b.s_st.bump(16), 16)
                            g.wait_ge(b.s_st.h, b.s_st.n)

                        @b.blk.vector
                        def _(v):
                            v.wait_ge(b.s_st.h, b.s_st.n)
                    post(ti)


SUBNAMES = ["l0ffn_a", "l0attn", "l0ffn_b", "l1ffn_a", "l1attn", "l1ffn_b"]


def build_nc(first=0, last=5):
    nc = bass.Bass("TRN2", target_bir_lowering=False,
                   detect_race_conditions=False)
    b = Builder(nc)
    b.declare_io()
    with contextlib.ExitStack() as es:
        b.alloc_persistent(es)
        with nc.Block() as blk:
            b.blk = blk
            b.load_initial()
            for k in range(first, last + 1):
                name = SUBNAMES[k]
                with nc.named_scope(f"s{k}_{name}"):
                    if name in ("l0ffn_a", "l0ffn_b"):
                        b.emit_l0ffn(tag=f"l0f{k}_")
                    elif name == "l0attn":
                        b.emit_l0attn()
                    elif name in ("l1ffn_a", "l1ffn_b"):
                        b.emit_l1ffn(tag=f"l1f{k}_", final=(k == 5))
                    elif name == "l1attn":
                        b.emit_l1attn()
            b.store_out()
    return nc


def make_consts():
    mask64 = np.zeros((128, 16 * 64), np.float32)
    for c in range(16):
        for p in range(128):
            mask64[p, c * 64 + (4 * c + p // 32)] = 1.0
    maskp = np.zeros((128, PSL), np.float32)
    for p in range(128):
        maskp[p, p % PSL] = 1.0
    tri = np.tril(np.ones((128, 128), np.float32))
    ident = np.eye(128, dtype=np.float32)
    triu = np.triu(np.ones((128, 128), np.float32))
    return mask64, maskp, tri, ident, triu


def stage_inputs(inputs, core):
    i = inputs
    mask64, maskp, tri, ident, triu = make_consts()
    C = np.ascontiguousarray
    return {
        "x0": C(i["L0_attn_x"][core]),
        "gw": C(i["L1_ffn_gate_w"]),
        "gb": C(i["L1_ffn_gate_b"].reshape(1, NEXP1)),
        "l1ew": C(i["L1_ffn_expert_w"]),
        "l1eb": C(i["L1_ffn_expert_b"]),
        "phiwT": C(i["L0_ffn_phi_w"].T),
        "phib": C(i["L0_ffn_phi_b"].reshape(1, NSLOT)),
        "l0ew": C(i["L0_ffn_expert_w"][8 * core:8 * core + 8]),
        "l0eb": C(i["L0_ffn_expert_b"][8 * core:8 * core + 8]),
        "l0ng": C(i["L0_attn_norm_g"].reshape(1, D)),
        "l0og": C(i["L0_attn_out_norm_g"].reshape(1, D)),
        "wq0": C(i["L0_attn_Wq"]),
        "wkv0": C(i["L0_attn_Wkv"]),
        "wout0": C(i["L0_attn_Wout"]),
        "wq1": C(i["L1_attn_Wq"]),
        "wk1": C(i["L1_attn_Wk"]),
        "wv1": C(i["L1_attn_Wv"]),
        "wo1": C(i["L1_attn_Wo"]),
        "wgam": C(i["L1_attn_Wgamma"]),
        "wf1": C(i["L1_attn_Wf1"]),
        "wf2": C(i["L1_attn_Wf2"]),
        "wog1": C(i["L1_attn_Wog1"]),
        "wog2": C(i["L1_attn_Wog2"]),
        "l1ng": C(i["L1_attn_norm_w"].reshape(1, D)),
        "cmask64": mask64,
        "cmaskp": maskp,
        "ctri": tri,
        "cident": ident,
        "ctriu": triu,
        "cmskl": np.tril(np.ones((128, 128), np.uint8), -1),
        "cmsku": np.triu(np.ones((128, 128), np.uint8)),
        "cmsku1": np.triu(np.ones((128, 128), np.uint8), 1),
    }


def kernel(**inputs):
    nc = build_nc(0, 5)
    in_maps = [stage_inputs(inputs, c) for c in range(NCORES)]
    res = run_bass_kernel_spmd(nc, in_maps, core_ids=list(range(NCORES)))
    return np.stack([res.results[c]["out"] for c in range(NCORES)], axis=0)
